# version 11
# baseline (speedup 1.0000x reference)
import numpy as np
import concourse.bass as bass
import concourse.mybir as mybir
from concourse.bass_utils import run_bass_kernel_spmd
from contextlib import ExitStack

F32 = mybir.dt.float32
BF16 = mybir.dt.bfloat16
AF = mybir.ActivationFunctionType
ALU = mybir.AluOpType
AX = mybir.AxisListType.X

MAGIC = float(np.float32(3 * 2.0 ** 22))
D = 1024
NB = 8192
HID = 4096
TOK = 4096
T = 512
NT = TOK // T
SQRT_N = float(np.float32(np.sqrt(np.float32(NB))))
EPS = 1e-5
P = 128

ENGS = ("sync", "vector", "scalar", "tensor", "gpsimd")
ESEM = {"sync": "SYD", "vector": "VS", "scalar": "SS", "tensor": "TS",
        "gpsimd": "GSD"}


class Prog:

    def __init__(self):
        self.q = {e: [] for e in ENGS}
        self.cnt = {}
        self.wc = {e: {} for e in ENGS}
        self.W = {}
        self.R = {}

    def op(self, eng, fn, reads=(), writes=(), extra=(), sem=None, n=1,
           track=True):
        sem = sem if sem is not None else ESEM[eng]
        waits = []
        for b in reads:
            waits.append(self.W.get(b))
        for b in writes:
            waits.append(self.W.get(b))
            waits.extend(self.R.get(b, ()))
        waits.extend(extra)
        wl = []
        for w in waits:
            if w is None:
                continue
            s, v = w
            if v <= 0:
                continue
            assert v <= self.cnt.get(s, 0), f"wait on future {s}={v}"
            if self.wc[eng].get(s, 0) >= v:
                continue
            self.wc[eng][s] = v
            wl.append((s, v))
        self.cnt[sem] = self.cnt.get(sem, 0) + n
        ev = (sem, self.cnt[sem])
        self.q[eng].append((fn, tuple(wl), sem, n))
        if track:
            for b in reads:
                self.R.setdefault(b, []).append(ev)
            for b in writes:
                self.W[b] = ev
                self.R[b] = []
        return ev

    def wait(self, eng, evs):
        wl = []
        for w in evs:
            if w is None:
                continue
            s, v = w
            assert v <= self.cnt.get(s, 0)
            if self.wc[eng].get(s, 0) >= v:
                continue
            self.wc[eng][s] = v
            wl.append((s, v))
        if wl:
            self.q[eng].append((None, tuple(wl), None, 0))

    def note_read(self, bufs, ev):
        for b in bufs:
            self.R.setdefault(b, []).append(ev)

    def note_write(self, bufs, ev):
        for b in bufs:
            self.W[b] = ev
            self.R[b] = []

    def barrier(self):
        tgt = self.cnt.get("BAR", 0) + len(ENGS)
        for e in ENGS:
            self.op(e, lambda h: h.nop(), sem="BAR", track=False)
        for e in ENGS:
            self.wait(e, [("BAR", tgt)])

    def emit(self, block, semh):
        decos = {"sync": block.sync, "vector": block.vector,
                 "scalar": block.scalar, "tensor": block.tensor,
                 "gpsimd": block.gpsimd}
        for e in ENGS:
            q = self.q[e]

            def body(h, q=q):
                for fn, waits, sem, n in q:
                    for (s, v) in waits:
                        h.wait_ge(semh[s], v)
                    if fn is not None:
                        ins = fn(h)
                        if sem is not None:
                            ins.then_inc(semh[sem], n)
            decos[e](body)


def build_nc(n_tiles=NT):
    nc = bass.Bass("TRN2", target_bir_lowering=False)

    xs = nc.dram_tensor("xs", [TOK + 2, D], F32, kind="ExternalInput").ap()
    kk = nc.dram_tensor("kk", [NB, D], F32, kind="ExternalInput").ap()
    vv = nc.dram_tensor("vv", [NB, D], F32, kind="ExternalInput").ap()
    w1d = nc.dram_tensor("w1", [HID, D], F32, kind="ExternalInput").ap()
    w2d = nc.dram_tensor("w2", [D, HID], F32, kind="ExternalInput").ap()
    cwd = nc.dram_tensor("cw", [HID, 3], F32, kind="ExternalInput").ap()
    cbd = nc.dram_tensor("cb", [HID], F32, kind="ExternalInput").ap()
    id_in = nc.dram_tensor("ident", [P, P], F32, kind="ExternalInput").ap()
    outd = nc.dram_tensor("out", [TOK, D], F32, kind="ExternalOutput").ap()

    kt_s = nc.dram_tensor("kt_s", [32, P, 2048], BF16, kind="Internal").ap()
    v_s = nc.dram_tensor("v_s", [NB, D], BF16, kind="Internal").ap()
    w1t_s = nc.dram_tensor("w1t_s", [16, P, 2048], BF16, kind="Internal").ap()
    w2t_s = nc.dram_tensor("w2t_s", [16, P, 2048], BF16, kind="Internal").ap()

    p = Prog()

    with ExitStack() as S:
        def sb(name, shape, dtype=F32):
            return S.enter_context(nc.sbuf_tensor(name, shape, dtype))

        semh = {}
        sem_names = (["SYD", "VS", "SS", "TS", "GSD", "BAR", "SSD"]
                     + [f"SY{i}" for i in range(6)]
                     + [f"GP{i}" for i in range(5)]
                     + ["XL0", "XL1", "XH", "XA0", "XA1", "VO0", "VO1",
                        "KC0", "KC1", "WC0", "WC1", "WD0", "WD1",
                        "CS0", "CS1", "CS2", "KTS", "W1S", "W2S", "VCS",
                        "GS"])
        for s_name in sem_names:
            semh[s_name] = S.enter_context(nc.semaphore(s_name))

        ident = sb("identc", [P, P])
        identb = sb("identbc", [P, P], BF16)
        ones_colf = sb("ones_colf", [P, 1])
        ones_sq = sb("ones_sq", [P, P], BF16)
        ones_row = sb("ones_row", [1, P])
        taps = sb("tapsc", [P, 96])
        bias_c = sb("bias_c", [P, 32])
        c1col = sb("c1col", [P, 1])
        c2col = sb("c2col", [P, 1])
        sw1col = sb("sw1col", [P, 1])
        sw2col = sb("sw2col", [P, 1])

        p.op("sync", lambda h: h.dma_start(out=ident[:], in_=id_in),
             writes=["ident"], n=16, sem="CS0")
        p.op("sync", lambda h: h.dma_start(
            out=taps[:], in_=bass.AP(cwd.tensor, 0, [[3, P], [384, 32], [1, 3]])),
            writes=["taps"], n=16, sem="CS1")
        p.op("sync", lambda h: h.dma_start(
            out=bias_c[:], in_=bass.AP(cbd.tensor, 0, [[1, P], [P, 32]]),
            allow_slow_non_contiguous=True),
            writes=["bias"], n=16, sem="CS2")
        p.op("vector", lambda h: h.tensor_copy(identb[:], ident[:]),
             reads=["ident"], writes=["identb"])
        p.op("vector", lambda h: h.memset(ones_colf[:], 1.0), writes=["onesf"])
        p.op("vector", lambda h: h.memset(ones_sq[:], 1.0), writes=["onessq"])
        p.op("vector", lambda h: h.memset(ones_row[:], 1.0), writes=["onesr"])

        with ExitStack() as SU:
            def sbu(name, shape, dtype=F32):
                return SU.enter_context(nc.sbuf_tensor(name, shape, dtype))

            def psu(name, shape, dtype=F32):
                return SU.enter_context(nc.psum_tensor(name, shape, dtype))

            kch = [sbu("kch0", [P, D]), sbu("kch1", [P, D])]
            kbf = [sbu("kbf0", [P, D], BF16), sbu("kbf1", [P, D], BF16)]
            kstage = [sbu("kstage0", [P, 2048], BF16),
                      sbu("kstage1", [P, 2048], BF16)]
            ptk = psu("ptk", [P, 1024], BF16)
            w1ch = [sbu("w1ch0", [P, D]), sbu("w1ch1", [P, D])]
            w1q = sbu("w1q", [P, D])
            w1b = [sbu("w1b0", [P, D], BF16), sbu("w1b1", [P, D], BF16)]
            w1stage = [sbu("w1stage0", [P, 2048], BF16),
                       sbu("w1stage1", [P, 2048], BF16)]
            ptw1 = psu("ptw1", [P, 1024], BF16)
            w2ch = [sbu("w2ch0", [P, HID]), sbu("w2ch1", [P, HID])]
            w2q = sbu("w2q", [P, HID])
            w2full = sbu("w2full", [P, 8 * HID], BF16)
            w2stage = [sbu("w2stage0", [P, 2048], BF16),
                       sbu("w2stage1", [P, 2048], BF16)]
            ptw2 = psu("ptw2", [P, 1024], BF16)
            acc1 = sbu("acc1s", [P, 32])
            accw = sbu("accws", [P, 1])
            rowt = sbu("rowts", [1, 2])
            pmean = psu("pmean", [1, 2])
            pbcs = psu("pbcs", [P, 2])

            for i in range(8):
                p.op("gpsimd", lambda h, i=i: h.dma_start(
                    out=v_s[i * 1024:(i + 1) * 1024, :],
                    in_=vv[i * 1024:(i + 1) * 1024, :]),
                    writes=[f"v_s{i}"], n=16, sem="VCS")

            def mean_pass(wd, nchunks, width, chname, chbufs, queue, sw_col,
                          c_col, swn, cn):
                for i in range(nchunks):
                    p.op(queue, lambda h, i=i: h.dma_start(
                        out=chbufs[i % 2][:, 0:width],
                        in_=wd[i * P:(i + 1) * P, :]),
                        writes=[f"{chname}{i % 2}"], n=16,
                        sem=(f"WC{i % 2}" if queue == "scalar"
                             else f"WD{i % 2}"))
                    p.op("vector", lambda h, i=i: h.reduce_sum(
                        acc1[:, i:i + 1], chbufs[i % 2][:, 0:width], AX,
                        apply_absolute_value=True),
                        reads=[f"{chname}{i % 2}"], writes=[f"{chname}ac{i}"])
                p.op("vector", lambda h, n=nchunks: h.reduce_sum(
                    accw[:], acc1[:, 0:n], AX),
                    reads=[f"{chname}ac{i}" for i in range(nchunks)],
                    writes=["accw"])
                p.op("tensor", lambda h: h.matmul(
                    pmean[0:1, 0:1], ones_colf[:], accw[:],
                    start=True, stop=True),
                    reads=["accw", "onesf"], writes=["pmean"])
                p.op("vector", lambda h: h.tensor_scalar(
                    rowt[0:1, 0:1], pmean[0:1, 0:1], 1.0 / (HID * D), 1e-5,
                    op0=ALU.mult, op1=ALU.max),
                    reads=["pmean"], writes=["rowt0"])
                p.op("vector", lambda h: h.reciprocal(
                    rowt[0:1, 1:2], rowt[0:1, 0:1]),
                    reads=["rowt0"], writes=["rowt1"])
                p.op("tensor", lambda h: h.matmul(
                    pbcs[:, 0:1], ones_row[:], rowt[0:1, 1:2],
                    start=True, stop=True),
                    reads=["rowt1", "onesr"], writes=["pbcs0"])
                p.op("tensor", lambda h: h.matmul(
                    pbcs[:, 1:2], ones_row[:], rowt[0:1, 0:1],
                    start=True, stop=True),
                    reads=["rowt0", "onesr"], writes=["pbcs1"])
                p.op("vector", lambda h: h.tensor_copy(sw_col[:], pbcs[:, 0:1]),
                     reads=["pbcs0"], writes=[swn])
                p.op("vector", lambda h: h.tensor_scalar(
                    c_col[:], pbcs[:, 1:2], 1.0 / 127.0, None, op0=ALU.mult),
                    reads=["pbcs1"], writes=[cn])

            mean_pass(w1d, 32, D, "w1ch", w1ch, "scalar", sw1col, c1col,
                      "sw1", "c1")
            mean_pass(w2d, 8, HID, "w2ch", w2ch, "gpsimd", sw2col, c2col,
                      "sw2", "c2")

            for sl in range(32):
                par = sl % 2
                for j in range(2):
                    kcn = sl * 2 + j
                    bp = kcn % 2
                    p.op("sync", lambda h, kcn=kcn, bp=bp: h.dma_start(
                        out=kch[bp][:], in_=kk[kcn * P:(kcn + 1) * P, :]),
                        writes=[f"kch{bp}"], n=16, sem=f"KC{bp}")
                    p.op("vector", lambda h, bp=bp: h.tensor_copy(
                        kbf[bp][:], kch[bp][:]),
                        reads=[f"kch{bp}"], writes=[f"kbf{bp}"])
                    ev = None
                    for dc in range(8):
                        fn = (lambda h, bp=bp, dc=dc:
                              h.transpose(ptk[:, dc * P:(dc + 1) * P],
                                          kbf[bp][:, dc * P:(dc + 1) * P],
                                          identb[:]))
                        if dc == 0:
                            ev = p.op("tensor", fn,
                                      reads=[f"kbf{bp}", "identb"],
                                      writes=["ptk"], track=False)
                        else:
                            ev = p.op("tensor", fn, track=False)
                    p.note_read([f"kbf{bp}", "identb"], ev)
                    p.note_write(["ptk"], ev)
                    for dc in range(8):
                        p.op("vector",
                             lambda h, par=par, dc=dc, j=j: h.tensor_copy(
                                 kstage[par][:, dc * 256 + j * P:
                                             dc * 256 + (j + 1) * P],
                                 ptk[:, dc * P:(dc + 1) * P]),
                             reads=["ptk"], writes=[f"kst{par}_{j}_{dc}"])
                p.op("sync", lambda h, sl=sl, par=par: h.dma_start(
                    out=kt_s[sl], in_=kstage[par][:]),
                    reads=[f"kst{par}_{j}_{dc}" for j in range(2)
                           for dc in range(8)],
                    writes=[f"kt_s{sl}"], n=16, sem="KTS")

            for sl in range(16):
                par = sl % 2
                for j in range(2):
                    jc = sl * 2 + j
                    bp = jc % 2
                    p.op("scalar", lambda h, jc=jc, bp=bp: h.dma_start(
                        out=w1ch[bp][:], in_=w1d[jc * P:(jc + 1) * P, :]),
                        writes=[f"w1ch{bp}"], n=16, sem=f"WC{bp}")
                    p.op("vector", lambda h, bp=bp: h.tensor_scalar(
                        w1q[:], w1ch[bp][:], sw1col[:], None, op0=ALU.mult),
                        reads=[f"w1ch{bp}", "sw1"], writes=["w1q"])
                    p.op("vector", lambda h: h.tensor_scalar(
                        w1q[:], w1q[:], 1.0, -1.0, op0=ALU.min, op1=ALU.max),
                        reads=["w1q"], writes=["w1q"])
                    p.op("vector", lambda h, bp=bp: h.tensor_scalar(
                        w1b[bp][:], w1q[:], MAGIC, MAGIC,
                        op0=ALU.add, op1=ALU.subtract),
                        reads=["w1q"], writes=[f"w1b{bp}"])
                    ev = None
                    for dc in range(8):
                        fn = (lambda h, bp=bp, dc=dc:
                              h.transpose(ptw1[:, dc * P:(dc + 1) * P],
                                          w1b[bp][:, dc * P:(dc + 1) * P],
                                          identb[:]))
                        if dc == 0:
                            ev = p.op("tensor", fn,
                                      reads=[f"w1b{bp}", "identb"],
                                      writes=["ptw1"], track=False)
                        else:
                            ev = p.op("tensor", fn, track=False)
                    p.note_read([f"w1b{bp}", "identb"], ev)
                    p.note_write(["ptw1"], ev)
                    for dc in range(8):
                        p.op("vector",
                             lambda h, par=par, j=j, dc=dc: h.tensor_copy(
                                 w1stage[par][:, j * 1024 + dc * P:
                                              j * 1024 + (dc + 1) * P],
                                 ptw1[:, dc * P:(dc + 1) * P]),
                             reads=["ptw1"], writes=[f"w1st{par}_{j}_{dc}"])
                p.op("scalar", lambda h, sl=sl, par=par: h.dma_start(
                    out=w1t_s[sl], in_=w1stage[par][:]),
                    reads=[f"w1st{par}_{j}_{dc}" for j in range(2)
                           for dc in range(8)],
                    writes=[f"w1t_s{sl}"], n=16, sem="W1S")

            for i in range(8):
                bp = i % 2
                p.op("gpsimd", lambda h, i=i, bp=bp: h.dma_start(
                    out=w2ch[bp][:], in_=w2d[i * P:(i + 1) * P, :]),
                    writes=[f"w2ch{bp}"], n=16, sem=f"WD{bp}")
                p.op("vector", lambda h, bp=bp: h.tensor_scalar(
                    w2q[:], w2ch[bp][:], sw2col[:], None, op0=ALU.mult),
                    reads=[f"w2ch{bp}", "sw2"], writes=["w2q"])
                p.op("vector", lambda h: h.tensor_scalar(
                    w2q[:], w2q[:], 1.0, -1.0, op0=ALU.min, op1=ALU.max),
                    reads=["w2q"], writes=["w2q"])
                p.op("vector", lambda h, i=i: h.tensor_scalar(
                    w2full[:, i * HID:(i + 1) * HID], w2q[:], MAGIC, MAGIC,
                    op0=ALU.add, op1=ALU.subtract),
                    reads=["w2q"], writes=[f"w2f{i}"])
            for sl in range(16):
                par = sl % 2
                for j in range(2):
                    jc = sl * 2 + j
                    ev = None
                    for ocx in range(8):
                        fn = (lambda h, ocx=ocx, jc=jc:
                              h.transpose(ptw2[:, ocx * P:(ocx + 1) * P],
                                          w2full[:, ocx * HID + jc * P:
                                                 ocx * HID + (jc + 1) * P],
                                          identb[:]))
                        if ocx == 0:
                            ev = p.op("tensor", fn,
                                      reads=[f"w2f{x}" for x in range(8)]
                                      + ["identb"],
                                      writes=["ptw2"], track=False)
                        else:
                            ev = p.op("tensor", fn, track=False)
                    p.note_read([f"w2f{x}" for x in range(8)] + ["identb"], ev)
                    p.note_write(["ptw2"], ev)
                    for ocx in range(8):
                        p.op("vector",
                             lambda h, par=par, j=j, ocx=ocx: h.tensor_copy(
                                 w2stage[par][:, j * 1024 + ocx * P:
                                              j * 1024 + (ocx + 1) * P],
                                 ptw2[:, ocx * P:(ocx + 1) * P]),
                             reads=["ptw2"], writes=[f"w2st{par}_{j}_{ocx}"])
                p.op("gpsimd", lambda h, sl=sl, par=par: h.dma_start(
                    out=w2t_s[sl], in_=w2stage[par][:]),
                    reads=[f"w2st{par}_{j}_{ocx}" for j in range(2)
                           for ocx in range(8)],
                    writes=[f"w2t_s{sl}"], n=16, sem="W2S")

            p.barrier()
            for sl in range(32):
                p.W[f"kt_s{sl}"] = ("KTS", 16 * 32)
            for sl in range(16):
                p.W[f"w1t_s{sl}"] = ("W1S", 16 * 16)
                p.W[f"w2t_s{sl}"] = ("W2S", 16 * 16)
            for i in range(8):
                p.W[f"v_s{i}"] = ("VCS", 16 * 8)

        def ps(name, shape, dtype=F32):
            return S.enter_context(nc.psum_tensor(name, shape, dtype))

        pb = [ps(f"pb{i}", [P, 512]) for i in range(8)]
        PBN = [f"pb{i}" for i in range(8)]

        scT = sb("scT", [P, 64 * 512], BF16)
        y_full = sb("y_full", [P, 32 * 512], BF16)
        wsy = [sb(f"wsy{i}", [P, 2048], BF16) for i in range(6)]
        wgp = [sb(f"wgp{i}", [P, 2048], BF16) for i in range(5)]
        xnT = sb("xnT", [P, 8 * 512], BF16)
        xqT = sb("xqT", [P, 8 * 514], BF16)
        xln = [sb("xln0", [P, D]), sb("xln1", [P, D])]
        xadd = [sb("xadd0", [P, D]), sb("xadd1", [P, D])]
        qtc = [sb("qtc0", [P, D]), sb("qtc1", [P, D])]
        qtmp = sb("qtmp", [P, 512])
        sqz = [sb("sqz0", [P, 512], BF16), sb("sqz1", [P, 512], BF16),
               sb("sqz2", [P, 512], BF16)]
        y_scr = [sb("y_scr0", [P, 256]), sb("y_scr1", [P, 256])]
        mxq = sb("mxq", [P, 512])
        csbc = sb("csbc", [P, 512])
        csbc2 = sb("csbc2", [P, 512])
        invbc = sb("invbc", [P, 512])
        rows = sb("rows", [1, 1024])
        stat = sb("stat", [P, 32])
        x_halo = sb("x_halo", [2, D])
        xq_halo = sb("xq_halo", [2, D], BF16)
        stat_h = sb("stat_h", [2, 8])

        def bslot(tc, which):
            c0 = tc * 2048 + which * 1024
            names = [f"yf{tc * 4 + 2 * which + q}_{hf}"
                     for q in (0, 1) for hf in (0, 1)]
            return y_full[:, c0:c0 + 1024], names

        def mm_group(fns, bank, reads):
            ev = None
            for i, fn in enumerate(fns):
                if i == 0:
                    ev = p.op("tensor", fn, reads=reads, writes=[bank],
                              track=False)
                else:
                    ev = p.op("tensor", fn, track=False)
            p.note_read(reads, ev)
            p.note_write([bank], ev)
            return ev

        def sy_slot(t, i):
            return (t * 64 + i) % 6

        def gp_slot(t, i):
            return (t * 32 + i) % 5

        def emit_B_ln(t):
            g0 = t * T
            p.op("gpsimd", lambda h, g0=g0: h.dma_start(
                out=x_halo[:],
                in_=bass.AP(xs.tensor, g0 * D, [[513 * D, 2], [1, D]])),
                writes=["x_halo"], n=16, sem="XH")
            for tc in range(4):
                sl = tc % 2
                xnb, xnn = bslot(tc, 0)
                xqb, xqn = bslot(tc, 1)
                p.op("gpsimd", lambda h, tc=tc, sl=sl, g0=g0: h.dma_start(
                    out=xln[sl][:],
                    in_=xs[1 + g0 + tc * P: 1 + g0 + (tc + 1) * P, :]),
                    writes=[f"xln{sl}"], n=16, sem=f"XL{sl}")
                st = stat[:, tc * 8:tc * 8 + 8]
                sn = [f"st{tc}_{i}" for i in range(8)]
                p.op("vector", lambda h, sl=sl, st=st: h.reduce_sum(
                    st[:, 0:1], xln[sl][:], AX),
                    reads=[f"xln{sl}"], writes=[sn[0]])
                p.op("vector", lambda h, st=st: h.tensor_scalar(
                    st[:, 0:1], st[:, 0:1], 1.0 / D, None, op0=ALU.mult),
                    reads=[sn[0]], writes=[sn[0]])
                p.op("vector", lambda h, sl=sl, st=st: h.tensor_scalar(
                    xln[sl][:], xln[sl][:], st[:, 0:1], None,
                    op0=ALU.subtract),
                    reads=[sn[0]], writes=[f"xln{sl}"])
                p.op("scalar", lambda h, sl=sl, xqb=xqb, st=st: h.activation(
                    xqb[:], xln[sl][:], AF.Square, accum_out=st[:, 1:2]),
                    reads=[f"xln{sl}"], writes=xqn + [sn[1]])
                p.op("vector", lambda h, st=st: h.tensor_scalar(
                    st[:, 1:2], st[:, 1:2], 1.0 / D, EPS,
                    op0=ALU.mult, op1=ALU.add),
                    reads=[sn[1]], writes=[sn[1]])
                p.op("scalar", lambda h, st=st: h.activation(
                    st[:, 1:2], st[:, 1:2], AF.Sqrt),
                    reads=[sn[1]], writes=[sn[1]])
                p.op("vector", lambda h, st=st: h.reciprocal(
                    st[:, 2:3], st[:, 1:2]), reads=[sn[1]], writes=[sn[2]])
                p.op("vector", lambda h, sl=sl, st=st: h.reduce_max(
                    st[:, 3:4], xln[sl][:], AX, apply_absolute_value=True),
                    reads=[f"xln{sl}"], writes=[sn[3]])
                p.op("vector", lambda h, st=st: h.tensor_tensor(
                    st[:, 3:4], st[:, 3:4], st[:, 2:3], op=ALU.mult),
                    reads=[sn[3], sn[2]], writes=[sn[3]])
                p.op("vector", lambda h, st=st: h.tensor_scalar(
                    st[:, 3:4], st[:, 3:4], 1e-5, None, op0=ALU.max),
                    reads=[sn[3]], writes=[sn[3]])
                p.op("vector", lambda h, st=st: h.reciprocal(
                    st[:, 4:5], st[:, 3:4]), reads=[sn[3]], writes=[sn[4]])
                p.op("vector", lambda h, st=st: h.tensor_scalar(
                    st[:, 4:5], st[:, 4:5], 127.0, None, op0=ALU.mult),
                    reads=[sn[4]], writes=[sn[4]])
                p.op("vector", lambda h, st=st: h.tensor_tensor(
                    st[:, 5:6], st[:, 2:3], st[:, 4:5], op=ALU.mult),
                    reads=[sn[2], sn[4]], writes=[sn[5]])
                p.op("vector", lambda h, st=st: h.tensor_tensor(
                    st[:, 6:7], st[:, 3:4], c1col[:], op=ALU.mult),
                    reads=[sn[3], "c1"], writes=[sn[6]])
                p.op("vector", lambda h, sl=sl, xnb=xnb, st=st: h.tensor_scalar(
                    xnb[:], xln[sl][:], st[:, 2:3], None, op0=ALU.mult),
                    reads=[f"xln{sl}", sn[2]], writes=xnn)
                for hh in (0, 1):
                    p.op("vector", lambda h, sl=sl, st=st, hh=hh: h.tensor_scalar(
                        qtmp[:], xln[sl][:, hh * 512:(hh + 1) * 512],
                        st[:, 5:6], MAGIC, op0=ALU.mult, op1=ALU.add),
                        reads=[f"xln{sl}", sn[5]], writes=["qtmp"])
                    p.op("vector", lambda h: h.tensor_scalar(
                        qtmp[:], qtmp[:], MAGIC, 127.0,
                        op0=ALU.subtract, op1=ALU.min),
                        reads=["qtmp"], writes=["qtmp"])
                    p.op("vector", lambda h, xqb=xqb, st=st, hh=hh: h.tensor_scalar(
                        xqb[:, hh * 512:(hh + 1) * 512], qtmp[:], -128.0,
                        st[:, 6:7], op0=ALU.max, op1=ALU.mult),
                        reads=["qtmp", sn[6]], writes=xqn)

            p.op("vector", lambda h: h.reduce_sum(
                stat_h[:, 0:1], x_halo[:], AX), reads=["x_halo"], writes=["sh0"])
            p.op("vector", lambda h: h.tensor_scalar(
                stat_h[:, 0:1], stat_h[:, 0:1], 1.0 / D, None, op0=ALU.mult),
                reads=["sh0"], writes=["sh0"])
            p.op("vector", lambda h: h.tensor_scalar(
                x_halo[:], x_halo[:], stat_h[:, 0:1], None, op0=ALU.subtract),
                reads=["sh0"], writes=["x_halo"])
            p.op("scalar", lambda h: h.activation(
                xq_halo[:], x_halo[:], AF.Square, accum_out=stat_h[:, 1:2]),
                reads=["x_halo"], writes=["xq_halo", "sh1"])
            p.op("vector", lambda h: h.tensor_scalar(
                stat_h[:, 1:2], stat_h[:, 1:2], 1.0 / D, EPS,
                op0=ALU.mult, op1=ALU.add), reads=["sh1"], writes=["sh1"])
            p.op("scalar", lambda h: h.activation(
                stat_h[:, 1:2], stat_h[:, 1:2], AF.Sqrt),
                reads=["sh1"], writes=["sh1"])
            p.op("vector", lambda h: h.reciprocal(
                stat_h[:, 2:3], stat_h[:, 1:2]), reads=["sh1"], writes=["sh2"])
            p.op("vector", lambda h: h.reduce_max(
                stat_h[:, 3:4], x_halo[:], AX, apply_absolute_value=True),
                reads=["x_halo"], writes=["sh3"])
            p.op("vector", lambda h: h.tensor_tensor(
                stat_h[:, 3:4], stat_h[:, 3:4], stat_h[:, 2:3], op=ALU.mult),
                reads=["sh3", "sh2"], writes=["sh3"])
            p.op("vector", lambda h: h.tensor_scalar(
                stat_h[:, 3:4], stat_h[:, 3:4], 1e-5, None, op0=ALU.max),
                reads=["sh3"], writes=["sh3"])
            p.op("vector", lambda h: h.reciprocal(
                stat_h[:, 4:5], stat_h[:, 3:4]), reads=["sh3"], writes=["sh4"])
            p.op("vector", lambda h: h.tensor_scalar(
                stat_h[:, 4:5], stat_h[:, 4:5], 127.0, None, op0=ALU.mult),
                reads=["sh4"], writes=["sh4"])
            p.op("vector", lambda h: h.tensor_tensor(
                stat_h[:, 5:6], stat_h[:, 2:3], stat_h[:, 4:5], op=ALU.mult),
                reads=["sh2", "sh4"], writes=["sh5"])
            p.op("vector", lambda h: h.tensor_tensor(
                stat_h[:, 6:7], stat_h[:, 3:4], c1col[0:2, :], op=ALU.mult),
                reads=["sh3", "c1"], writes=["sh6"])
            p.op("vector", lambda h: h.tensor_scalar(
                x_halo[:], x_halo[:], stat_h[:, 5:6], MAGIC,
                op0=ALU.mult, op1=ALU.add),
                reads=["sh5"], writes=["x_halo"])
            p.op("vector", lambda h: h.tensor_scalar(
                x_halo[:], x_halo[:], MAGIC, 127.0,
                op0=ALU.subtract, op1=ALU.min),
                reads=["x_halo"], writes=["x_halo"])
            p.op("vector", lambda h: h.tensor_scalar(
                xq_halo[:], x_halo[:], -128.0, stat_h[:, 6:7],
                op0=ALU.max, op1=ALU.mult),
                reads=["x_halo", "sh6"], writes=["xq_halo"])

        def emit_B_tr(t, bc0):
            bcnt = bc0
            for tc in range(4):
                for which in (0, 1):
                    src, srcn = bslot(tc, which)
                    dst = xnT if which == 0 else xqT
                    dname = "xnT" if which == 0 else "xqT"
                    off = 0 if which == 0 else 1
                    w = 512 if which == 0 else 514
                    for hg in (0, 1):
                        bk = pb[2 + (bcnt % 2)]
                        bkn = PBN[2 + (bcnt % 2)]
                        bcnt += 1
                        ev = None
                        for dq in range(4):
                            dc = hg * 4 + dq
                            fn = (lambda h, src=src, dc=dc, dq=dq, bk=bk:
                                  h.matmul(bk[:, dq * P:(dq + 1) * P],
                                           src[:, dc * P:(dc + 1) * P],
                                           identb[:], start=True, stop=True))
                            if dq == 0:
                                ev = p.op("tensor", fn,
                                          reads=srcn + ["identb"],
                                          writes=[bkn], track=False)
                            else:
                                ev = p.op("tensor", fn, track=False)
                        p.note_read(srcn + ["identb"], ev)
                        p.note_write([bkn], ev)
                        c0 = hg * 4 * w + off + tc * P
                        rl = 8 * w
                        p.op("vector",
                             lambda h, dst=dst, c0=c0, w=w, bk=bk, rl=rl:
                             h.tensor_copy(
                                 bass.AP(dst, c0, [[rl, P], [w, 4], [1, P]]),
                                 bass.AP(bk, 0, [[512, P], [P, 4], [1, P]])),
                             reads=[bkn], writes=[dname])
            ev = None
            for dc in range(8):
                fn = (lambda h, dc=dc: h.matmul(
                    pb[2][:, dc * 2:dc * 2 + 2],
                    xq_halo[:, dc * P:(dc + 1) * P], identb[0:2, 0:2],
                    start=True, stop=True))
                if dc == 0:
                    ev = p.op("tensor", fn, reads=["xq_halo", "identb"],
                              writes=[PBN[2]], track=False)
                else:
                    ev = p.op("tensor", fn, track=False)
            p.note_read(["xq_halo", "identb"], ev)
            p.note_write([PBN[2]], ev)
            for dc in range(8):
                p.op("vector", lambda h, dc=dc: h.tensor_copy(
                    xqT[:, dc * 514:dc * 514 + 1], pb[2][:, dc * 2:dc * 2 + 1]),
                    reads=[PBN[2]], writes=["xqT"])
                p.op("vector", lambda h, dc=dc: h.tensor_copy(
                    xqT[:, dc * 514 + 513:dc * 514 + 514],
                    pb[2][:, dc * 2 + 1:dc * 2 + 2]),
                    reads=[PBN[2]], writes=["xqT"])

        def emit_main(t):
            g0 = t * T

            for ncx in range(64):
                sl = ncx // 2
                if ncx % 2 == 0:
                    ss = sy_slot(t, sl)
                    p.op("sync", lambda h, sl=sl, ss=ss: h.dma_start(
                        out=wsy[ss][:], in_=kt_s[sl]),
                        reads=[f"kt_s{sl}"], writes=[f"wsy{ss}"], n=16,
                        sem=f"SY{ss}")
                ss = sy_slot(t, ncx // 2)
                bank = PBN[ncx % 4]
                fns = []
                for dc in range(8):
                    fns.append(lambda h, ss=ss, ncx=ncx, dc=dc:
                               h.matmul(pb[ncx % 4][:],
                                        wsy[ss][
                                            :, dc * 256 + (ncx % 2) * P:
                                            dc * 256 + (ncx % 2 + 1) * P],
                                        xnT[:, dc * 512:(dc + 1) * 512],
                                        start=(dc == 0), stop=(dc == 7)))
                mm_group(fns, bank, [f"wsy{ss}", "xnT"])
                p.op("vector", lambda h, ncx=ncx: h.tensor_copy(
                    scT[:, ncx * 512:(ncx + 1) * 512], pb[ncx % 4][:]),
                    reads=[bank], writes=[f"sc{ncx}"])
                p.op("scalar", lambda h, ncx=ncx: h.activation(
                    sqz[ncx % 3][:], scT[:, ncx * 512:(ncx + 1) * 512],
                    AF.Square),
                    reads=[f"sc{ncx}"], writes=[f"sqz{ncx % 3}"])
                if ncx >= 2:
                    nl = ncx - 2
                    p.op("tensor", lambda h, nl=nl: h.matmul(
                        pb[4][:], ones_sq[:], sqz[nl % 3][:],
                        start=(nl == 0), stop=False),
                        reads=[f"sqz{nl % 3}", "onessq"], writes=[PBN[4]])
            for nl in (62, 63):
                p.op("tensor", lambda h, nl=nl: h.matmul(
                    pb[4][:], ones_sq[:], sqz[nl % 3][:],
                    start=False, stop=(nl == 63)),
                    reads=[f"sqz{nl % 3}", "onessq"], writes=[PBN[4]])

            p.op("vector", lambda h: h.tensor_copy(
                rows[0:1, 0:512], pb[4][0:1, :]),
                reads=[PBN[4]], writes=["rows"])
            p.op("scalar", lambda h: h.activation(
                rows[0:1, 0:512], rows[0:1, 0:512], AF.Sqrt),
                reads=["rows"], writes=["rows"])
            p.op("vector", lambda h: h.reciprocal(
                rows[0:1, 0:512], rows[0:1, 0:512]),
                reads=["rows"], writes=["rows"])
            p.op("vector", lambda h: h.tensor_scalar(
                rows[0:1, 0:512], rows[0:1, 0:512], SQRT_N, None,
                op0=ALU.mult), reads=["rows"], writes=["rows"])

            def minmax(jq):
                if jq == 0:
                    p.op("vector", lambda h: h.tensor_copy(
                        mxq[:], y_full[:, 0:512]),
                        reads=["yf0_0", "yf0_1"], writes=["mxq0", "mxq1"])
                    p.op("vector", lambda h: h.tensor_copy(
                        pb[7][:], y_full[:, 0:512]),
                        reads=["yf0_0", "yf0_1"], writes=[PBN[7]])
                else:
                    p.op("vector", lambda h, jq=jq: h.tensor_tensor(
                        mxq[:], mxq[:],
                        y_full[:, jq * 512:(jq + 1) * 512], op=ALU.max),
                        reads=[f"yf{jq}_0", f"yf{jq}_1", "mxq0", "mxq1"],
                        writes=["mxq0", "mxq1"])
                    p.op("vector", lambda h, jq=jq: h.tensor_tensor(
                        pb[7][:], pb[7][:],
                        y_full[:, jq * 512:(jq + 1) * 512], op=ALU.min),
                        reads=[f"yf{jq}_0", f"yf{jq}_1", PBN[7]],
                        writes=[PBN[7]])

            for jc in range(32):
                sl = jc // 2
                if jc == 1:
                    p.op("tensor", lambda h: h.matmul(
                        pb[5][:], ones_row[:], rows[0:1, 0:512],
                        start=True, stop=True),
                        reads=["rows", "onesr"], writes=[PBN[5]])
                    p.op("vector", lambda h: h.tensor_copy(csbc[:], pb[5][:]),
                         reads=[PBN[5]], writes=["csbc"])
                if jc % 2 == 0:
                    ws = sy_slot(t, 32 + sl)
                    p.op("sync", lambda h, sl=sl, ws=ws: h.dma_start(
                        out=wsy[ws][:], in_=w1t_s[sl]),
                        reads=[f"w1t_s{sl}"], writes=[f"wsy{ws}"], n=16,
                        sem=f"SY{ws}")
                ws = sy_slot(t, 32 + jc // 2)
                for hf in range(2):
                    ph = (2 * jc + hf) % 6
                    bank = PBN[ph]
                    fns = []
                    for dc in range(8):
                        fns.append(lambda h, ws=ws, jc=jc, dc=dc, hf=hf, ph=ph:
                                   h.matmul(pb[ph][:, 0:258],
                                            wsy[ws][
                                                :, (jc % 2) * 1024 + dc * P:
                                                (jc % 2) * 1024 + (dc + 1) * P],
                                            xqT[:, dc * 514 + hf * 256:
                                                dc * 514 + hf * 256 + 258],
                                            start=(dc == 0), stop=(dc == 7)))
                    mm_group(fns, bank, [f"wsy{ws}", "xqT"])
                    p.op("vector", lambda h, ph=ph, jc=jc: h.tensor_scalar(
                        y_scr[ph % 2][:], pb[ph][:, 1:257],
                        taps[:, jc * 3 + 1:jc * 3 + 2], None, op0=ALU.mult),
                        reads=[bank, "taps"], writes=[f"ysc{ph % 2}"])
                    p.op("vector", lambda h, ph=ph, jc=jc: h.scalar_tensor_tensor(
                        y_scr[ph % 2][:], pb[ph][:, 0:256],
                        taps[:, jc * 3:jc * 3 + 1], y_scr[ph % 2][:],
                        op0=ALU.mult, op1=ALU.add),
                        reads=[bank, "taps"], writes=[f"ysc{ph % 2}"])
                    p.op("vector", lambda h, ph=ph, jc=jc: h.scalar_tensor_tensor(
                        y_scr[ph % 2][:], pb[ph][:, 2:258],
                        taps[:, jc * 3 + 2:jc * 3 + 3], y_scr[ph % 2][:],
                        op0=ALU.mult, op1=ALU.add),
                        reads=[bank, "taps"], writes=[f"ysc{ph % 2}"])
                    p.op("scalar", lambda h, ph=ph, jc=jc, hf=hf: h.activation(
                        y_full[:, jc * 512 + hf * 256:jc * 512 + hf * 256 + 256],
                        y_scr[ph % 2][:], AF.Silu, bias=bias_c[:, jc:jc + 1]),
                        reads=[f"ysc{ph % 2}", "bias"], writes=[f"yf{jc}_{hf}"])
                if jc >= 1:
                    minmax(jc - 1)
                if 27 <= jc <= 31:
                    sv = jc - 27
                    gs = gp_slot(t, sv)
                    p.op("gpsimd", lambda h, sv=sv, gs=gs: h.dma_start(
                        out=bass.AP(wgp[gs], 0,
                                    [[2048, P], [1024, 2], [1, 1024]]),
                        in_=bass.AP(v_s.tensor, sv * 256 * D,
                                    [[D, P], [P * D, 2], [1, D]])),
                        reads=[f"v_s{sv // 4}"], writes=[f"wgp{gs}"], n=16,
                        sem=f"GP{gs}")
            minmax(31)
            p.op("vector", lambda h: h.tensor_scalar(
                pb[7][:], pb[7][:], -1.0, None, op0=ALU.mult),
                reads=[PBN[7]], writes=[PBN[7]])
            p.op("vector", lambda h: h.tensor_tensor(
                mxq[:], mxq[:], pb[7][:], op=ALU.max),
                reads=[PBN[7], "mxq0", "mxq1"],
                writes=["mxq0", "mxq1"])

            def hq_mxqT():
                ev = None
                for tc in range(4):
                    fn = (lambda h, tc=tc: h.transpose(
                        pb[6][:, tc * P:(tc + 1) * P],
                        mxq[:, tc * P:(tc + 1) * P], ident[:]))
                    if tc == 0:
                        ev = p.op("tensor", fn,
                                  reads=["mxq0", "mxq1", "ident"],
                                  writes=[PBN[6]], track=False)
                    else:
                        ev = p.op("tensor", fn, track=False)
                p.note_read(["mxq0", "mxq1", "ident"], ev)
                p.note_write([PBN[6]], ev)
                for tc in range(4):
                    st = stat[:, tc * 8:tc * 8 + 8]
                    p.op("vector", lambda h, tc=tc, st=st: h.reduce_max(
                        st[:, 7:8], pb[6][:, tc * P:(tc + 1) * P], AX),
                        reads=[PBN[6]], writes=[f"st{tc}_7"])

            def hq_statT():
                ev = None
                for tc in range(4):
                    ev = p.op("tensor", lambda h, tc=tc: h.transpose(
                        pb[7][0:1, tc * P:(tc + 1) * P],
                        stat[:, tc * 8 + 7:tc * 8 + 8], ident[:]),
                        reads=[f"st{tc}_7", "ident"],
                        writes=[PBN[7]] if tc == 0 else (), track=(tc == 0))
                p.note_write([PBN[7]], ev)
                p.note_read([f"st{tc}_7" for tc in range(4)], ev)
                p.op("vector", lambda h: h.tensor_scalar(
                    rows[0:1, 512:1024], pb[7][0:1, 0:512], 1e-5, None,
                    op0=ALU.max),
                    reads=[PBN[7]], writes=["rowsi"])
                p.op("vector", lambda h: h.reciprocal(
                    rows[0:1, 0:512], rows[0:1, 512:1024]),
                    reads=["rowsi"], writes=["rows"])
                p.op("vector", lambda h: h.tensor_scalar(
                    rows[0:1, 0:512], rows[0:1, 0:512], 127.0, None,
                    op0=ALU.mult),
                    reads=["rows"], writes=["rows"])
                p.op("vector", lambda h: h.tensor_scalar(
                    rows[0:1, 512:1024], rows[0:1, 512:1024], c2col[0:1, 0:1],
                    None, op0=ALU.mult),
                    reads=["rowsi", "c2"], writes=["rowsi"])

            def hq_bc1():
                p.op("tensor", lambda h: h.matmul(
                    pb[6][:], ones_row[:], rows[0:1, 0:512],
                    start=True, stop=True), reads=["rows", "onesr"],
                    writes=[PBN[6]])
                p.op("vector", lambda h: h.tensor_copy(csbc2[:], pb[6][:]),
                     reads=[PBN[6]], writes=["csbc2"])

            def hq_bc2():
                p.op("tensor", lambda h: h.matmul(
                    pb[7][:], ones_row[:], rows[0:1, 512:1024],
                    start=True, stop=True), reads=["rowsi", "onesr"],
                    writes=[PBN[7]])
                p.op("vector", lambda h: h.tensor_copy(
                    invbc[:, 0:512], pb[7][:]), reads=[PBN[7]],
                    writes=["invbc"])

            def emit_F(f):
                p.op("vector", lambda h, f=f: h.tensor_tensor(
                    sqz[f % 3][:], scT[:, f * 512:(f + 1) * 512],
                    csbc[:], op=ALU.mult),
                    reads=[f"sc{f}", "csbc"], writes=[f"sqz{f % 3}"])
                p.op("scalar", lambda h, f=f: h.activation(
                    scT[:, f * 512:(f + 1) * 512], sqz[f % 3][:],
                    AF.Gelu),
                    reads=[f"sqz{f % 3}"], writes=[f"sc{f}"])

            emit_F(0)
            emit_F(1)
            hq_steps = {1: hq_mxqT, 3: hq_statT, 5: hq_bc1, 7: hq_bc2}
            for ncx in range(8):
                for f in (2 * ncx + 2, 2 * ncx + 3):
                    if f < 64:
                        emit_F(f)
                sl = ncx // 2
                if ncx % 2 == 0 and sl >= 5:
                    gs = gp_slot(t, sl)
                    p.op("gpsimd", lambda h, sl=sl, gs=gs: h.dma_start(
                        out=bass.AP(wgp[gs], 0,
                                    [[2048, P], [1024, 2], [1, 1024]]),
                        in_=bass.AP(v_s.tensor, sl * 256 * D,
                                    [[D, P], [P * D, 2], [1, D]])),
                        reads=[f"v_s{sl // 4}"], writes=[f"wgp{gs}"], n=16,
                        sem=f"GP{gs}")
                gs = gp_slot(t, ncx // 2)
                ev = None
                for ol in range(6):
                    fn = (lambda h, gs=gs, ncx=ncx, ol=ol:
                          h.matmul(pb[ol][:],
                                   wgp[gs][
                                       :, (ncx % 2) * 1024 + ol * P:
                                       (ncx % 2) * 1024 + (ol + 1) * P],
                                   scT[:, ncx * 512:(ncx + 1) * 512],
                                   start=(ncx == 0), stop=False))
                    rd = [f"wgp{gs}", f"sc{ncx}"] if ol == 0 else ()
                    wr = [PBN[ol]] if ncx == 0 else ()
                    ev = p.op("tensor", fn, reads=rd, writes=wr,
                              track=False)
                p.note_read([f"wgp{gs}", f"sc{ncx}"], ev)
                if ncx in hq_steps:
                    hq_steps[ncx]()

            for ncx in range(8):
                gs = gp_slot(t, ncx // 2)
                ev = None
                for ol in (6, 7):
                    fn = (lambda h, gs=gs, ncx=ncx, ol=ol:
                          h.matmul(pb[ol][:],
                                   wgp[gs][
                                       :, (ncx % 2) * 1024 + ol * P:
                                       (ncx % 2) * 1024 + (ol + 1) * P],
                                   scT[:, ncx * 512:(ncx + 1) * 512],
                                   start=(ncx == 0), stop=False))
                    rd = [f"wgp{gs}", f"sc{ncx}"] if ol == 6 else ()
                    wr = [PBN[ol]] if ncx == 0 else ()
                    ev = p.op("tensor", fn, reads=rd, writes=wr,
                              track=False)
                p.note_read([f"wgp{gs}", f"sc{ncx}"], ev)

            for jq in range(32):
                p.op("vector", lambda h, jq=jq: h.tensor_tensor(
                    mxq[:], y_full[:, jq * 512:(jq + 1) * 512], csbc2[:],
                    op=ALU.mult),
                    reads=[f"yf{jq}_0", f"yf{jq}_1", "csbc2"],
                    writes=["mxq0", "mxq1"])
                p.op("vector", lambda h: h.tensor_scalar(
                    mxq[:], mxq[:], 127.0, -128.0, op0=ALU.min,
                    op1=ALU.max),
                    reads=["mxq0", "mxq1"], writes=["mxq0", "mxq1"])
                p.op("vector", lambda h: h.tensor_scalar(
                    mxq[:], mxq[:], MAGIC, MAGIC, op0=ALU.add,
                    op1=ALU.subtract),
                    reads=["mxq0", "mxq1"], writes=["mxq0", "mxq1"])
                p.op("vector", lambda h, jq=jq: h.tensor_tensor(
                    y_full[:, jq * 512:(jq + 1) * 512], mxq[:],
                    invbc[:, 0:512], op=ALU.mult),
                    reads=["mxq0", "mxq1", "invbc"],
                    writes=[f"yf{jq}_0", f"yf{jq}_1"])

            last_ev = [None] * 8
            for ncx in range(8, 64):
                for f in (2 * ncx + 2, 2 * ncx + 3):
                    if f < 64:
                        emit_F(f)
                sl = ncx // 2
                if ncx % 2 == 0 and sl >= 5:
                    gs = gp_slot(t, sl)
                    p.op("gpsimd", lambda h, sl=sl, gs=gs: h.dma_start(
                        out=bass.AP(wgp[gs], 0,
                                    [[2048, P], [1024, 2], [1, 1024]]),
                        in_=bass.AP(v_s.tensor, sl * 256 * D,
                                    [[D, P], [P * D, 2], [1, D]])),
                        reads=[f"v_s{sl // 4}"], writes=[f"wgp{gs}"], n=16,
                        sem=f"GP{gs}")
                gs = gp_slot(t, ncx // 2)
                ev = None
                for ol in range(8):
                    fn = (lambda h, gs=gs, ncx=ncx, ol=ol:
                          h.matmul(pb[ol][:],
                                   wgp[gs][
                                       :, (ncx % 2) * 1024 + ol * P:
                                       (ncx % 2) * 1024 + (ol + 1) * P],
                                   scT[:, ncx * 512:(ncx + 1) * 512],
                                   start=False, stop=False))
                    rd = [f"wgp{gs}", f"sc{ncx}"] if ol == 0 else ()
                    ev = p.op("tensor", fn, reads=rd, track=False)
                p.note_read([f"wgp{gs}", f"sc{ncx}"], ev)
            for jc in range(32):
                sl2 = jc // 2
                if jc % 2 == 0:
                    ss = sy_slot(t, 48 + sl2)
                    p.op("sync", lambda h, sl2=sl2, ss=ss: h.dma_start(
                        out=wsy[ss][:], in_=w2t_s[sl2]),
                        reads=[f"w2t_s{sl2}"], writes=[f"wsy{ss}"], n=16,
                        sem=f"SY{ss}")
                ss = sy_slot(t, 48 + jc // 2)
                ev = None
                for ol in range(8):
                    fn = (lambda h, ss=ss, jc=jc, ol=ol:
                          h.matmul(pb[ol][:],
                                   wsy[ss][
                                       :, (jc % 2) * 1024 + ol * P:
                                       (jc % 2) * 1024 + (ol + 1) * P],
                                   y_full[:, jc * 512:(jc + 1) * 512],
                                   start=False, stop=(jc == 31)))
                    ev = p.op("tensor", fn,
                              reads=[f"wsy{ss}", f"yf{jc}_0", f"yf{jc}_1"]
                              if ol == 0 else (), track=False)
                    if jc == 31:
                        last_ev[ol] = ev
                p.note_read([f"wsy{ss}", f"yf{jc}_0", f"yf{jc}_1"], ev)
            for ol in range(8):
                p.note_write([PBN[ol]], last_ev[ol])

        def emit_evac(t):
            g0 = t * T
            xa = [xadd[0], xadd[1], xadd[0], xadd[1]]
            xan = ["xadd0", "xadd1", "xadd0", "xadd1"]
            for tc in range(2):
                p.op("gpsimd", lambda h, tc=tc, g0=g0: h.dma_start(
                    out=xa[tc][:],
                    in_=xs[1 + g0 + tc * P: 1 + g0 + (tc + 1) * P, :]),
                    writes=[xan[tc]], n=16, sem=f"XA{tc % 2}")
            for ol in range(8):
                stg = mxq if ol % 2 == 0 else csbc
                stgn = ["mxq0", "mxq1"] if ol % 2 == 0 else ["csbc"]
                p.op("vector", lambda h, ol=ol, stg=stg: h.tensor_copy(
                    stg[:], pb[ol][:]),
                    reads=[PBN[ol]], writes=stgn)
                ev = None
                for tc in range(4):
                    fn = (lambda h, ol=ol, tc=tc, stg=stg: h.transpose(
                        pb[ol][:, tc * P:(tc + 1) * P],
                        stg[:, tc * P:(tc + 1) * P], ident[:]))
                    if tc == 0:
                        ev = p.op("tensor", fn, reads=stgn + ["ident"],
                                  writes=[PBN[ol]], track=False)
                    else:
                        ev = p.op("tensor", fn, track=False)
                p.note_read(stgn + ["ident"], ev)
                p.note_write([PBN[ol]], ev)
            for tc in range(4):
                for ol in range(8):
                    p.op("vector",
                         lambda h, tc=tc, ol=ol: h.scalar_tensor_tensor(
                             qtc[tc % 2][:, ol * P:(ol + 1) * P],
                             pb[ol][:, tc * P:(tc + 1) * P], 1.0,
                             xa[tc][:, ol * P:(ol + 1) * P],
                             op0=ALU.mult, op1=ALU.add),
                         reads=[PBN[ol], xan[tc]],
                         writes=[f"qtc{tc % 2}"])
                p.op("scalar", lambda h, tc=tc, g0=g0: h.dma_start(
                    out=outd[g0 + tc * P:g0 + (tc + 1) * P, :],
                    in_=qtc[tc % 2][:]),
                    reads=[f"qtc{tc % 2}"], writes=[f"o{t}_{tc}"],
                    n=16, sem=f"VO{tc % 2}")
                if tc < 2:
                    p.op("gpsimd", lambda h, tc=tc, g0=g0: h.dma_start(
                        out=xa[tc + 2][:],
                        in_=xs[1 + g0 + (tc + 2) * P: 1 + g0 + (tc + 3) * P, :]),
                        writes=[xan[tc + 2]], n=16, sem=f"XA{tc % 2}")

        emit_B_ln(0)
        emit_B_tr(0, 0)
        for t in range(n_tiles):
            emit_main(t)
            if t + 1 < n_tiles:
                emit_B_ln(t + 1)
            emit_evac(t)
            if t + 1 < n_tiles:
                emit_B_tr(t + 1, (t + 1) % 2)

        sync_sems = [f"SY{i}" for i in range(6)] + ["KC0", "KC1",
                     "CS0", "CS1", "CS2", "KTS"]
        gp_sems = ([f"GP{i}" for i in range(5)]
                   + ["XL0", "XL1", "XH", "XA0", "XA1",
                      "WD0", "WD1", "W2S", "VCS"])
        sc_sems = ["WC0", "WC1", "W1S", "VO0", "VO1"]
        p.wait("sync", [(s, p.cnt.get(s, 0)) for s in sync_sems])
        p.wait("gpsimd", [(s, p.cnt.get(s, 0)) for s in gp_sems])
        p.wait("scalar", [(s, p.cnt.get(s, 0)) for s in sc_sems])

        block = S.enter_context(nc.Block())
        p.emit(block, semh)

    return nc


def kernel(L0_attn_q, L0_attn_k, L0_attn_v, L0_ffn_w1, L0_ffn_conv_w,
           L0_ffn_conv_b, L0_ffn_w2):
    x = np.ascontiguousarray(np.asarray(L0_attn_q, dtype=np.float32))
    k = np.ascontiguousarray(np.asarray(L0_attn_k, dtype=np.float32))
    v = np.ascontiguousarray(np.asarray(L0_attn_v, dtype=np.float32))
    w1 = np.ascontiguousarray(np.asarray(L0_ffn_w1, dtype=np.float32))
    cw = np.ascontiguousarray(
        np.asarray(L0_ffn_conv_w, dtype=np.float32).reshape(HID, 3))
    cb = np.ascontiguousarray(np.asarray(L0_ffn_conv_b, dtype=np.float32))
    w2 = np.ascontiguousarray(np.asarray(L0_ffn_w2, dtype=np.float32))
    ident = np.eye(P, dtype=np.float32)

    zrow = np.zeros((1, D), np.float32)
    in_maps = []
    for c in range(8):
        b, hh = c // 2, c % 2
        tok0 = hh * TOK
        left = x[b, tok0 - 1:tok0] if hh == 1 else zrow
        right = x[b, tok0 + TOK:tok0 + TOK + 1] if hh == 0 else zrow
        xsl = np.concatenate([left, x[b, tok0:tok0 + TOK], right], axis=0)
        in_maps.append(dict(xs=np.ascontiguousarray(xsl), kk=k, vv=v, w1=w1,
                            w2=w2, cw=cw, cb=cb, ident=ident))

    nc = build_nc()
    res = run_bass_kernel_spmd(nc, in_maps, core_ids=list(range(8)))
    outp = np.empty((4, 8192, D), np.float32)
    for c in range(8):
        b, hh = c // 2, c % 2
        outp[b, hh * TOK:(hh + 1) * TOK] = np.asarray(res.results[c]["out"])
    return outp


# version 12
# speedup vs baseline: 1.1114x; 1.1114x over previous
import numpy as np
import concourse.bass as bass
import concourse.mybir as mybir
from concourse.bass_utils import run_bass_kernel_spmd
from contextlib import ExitStack

F32 = mybir.dt.float32
BF16 = mybir.dt.bfloat16
AF = mybir.ActivationFunctionType
ALU = mybir.AluOpType
AX = mybir.AxisListType.X

MAGIC = float(np.float32(3 * 2.0 ** 22))
D = 1024
NB = 8192
HID = 4096
TOK = 4096
T = 512
NT = TOK // T
SQRT_N = float(np.float32(np.sqrt(np.float32(NB))))
EPS = 1e-5
P = 128

ENGS = ("sync", "vector", "scalar", "tensor", "gpsimd")
ESEM = {"sync": "SYD", "vector": "VS", "scalar": "SS", "tensor": "TS",
        "gpsimd": "GSD"}


class Prog:

    def __init__(self):
        self.q = {e: [] for e in ENGS}
        self.cnt = {}
        self.wc = {e: {} for e in ENGS}
        self.W = {}
        self.R = {}

    def op(self, eng, fn, reads=(), writes=(), extra=(), sem=None, n=1,
           track=True):
        sem = sem if sem is not None else ESEM[eng]
        waits = []
        for b in reads:
            waits.append(self.W.get(b))
        for b in writes:
            waits.append(self.W.get(b))
            waits.extend(self.R.get(b, ()))
        waits.extend(extra)
        wl = []
        for w in waits:
            if w is None:
                continue
            s, v = w
            if v <= 0:
                continue
            assert v <= self.cnt.get(s, 0), f"wait on future {s}={v}"
            if self.wc[eng].get(s, 0) >= v:
                continue
            self.wc[eng][s] = v
            wl.append((s, v))
        self.cnt[sem] = self.cnt.get(sem, 0) + n
        ev = (sem, self.cnt[sem])
        self.q[eng].append((fn, tuple(wl), sem, n))
        if track:
            for b in reads:
                self.R.setdefault(b, []).append(ev)
            for b in writes:
                self.W[b] = ev
                self.R[b] = []
        return ev

    def wait(self, eng, evs):
        wl = []
        for w in evs:
            if w is None:
                continue
            s, v = w
            assert v <= self.cnt.get(s, 0)
            if self.wc[eng].get(s, 0) >= v:
                continue
            self.wc[eng][s] = v
            wl.append((s, v))
        if wl:
            self.q[eng].append((None, tuple(wl), None, 0))

    def note_read(self, bufs, ev):
        for b in bufs:
            self.R.setdefault(b, []).append(ev)

    def note_write(self, bufs, ev):
        for b in bufs:
            self.W[b] = ev
            self.R[b] = []

    def barrier(self):
        tgt = self.cnt.get("BAR", 0) + len(ENGS)
        for e in ENGS:
            self.op(e, lambda h: h.nop(), sem="BAR", track=False)
        for e in ENGS:
            self.wait(e, [("BAR", tgt)])

    def emit(self, block, semh):
        decos = {"sync": block.sync, "vector": block.vector,
                 "scalar": block.scalar, "tensor": block.tensor,
                 "gpsimd": block.gpsimd}
        for e in ENGS:
            q = self.q[e]

            def body(h, q=q):
                for fn, waits, sem, n in q:
                    for (s, v) in waits:
                        h.wait_ge(semh[s], v)
                    if fn is not None:
                        ins = fn(h)
                        if sem is not None:
                            ins.then_inc(semh[sem], n)
            decos[e](body)


def build_nc(n_tiles=NT):
    nc = bass.Bass("TRN2", target_bir_lowering=False)

    xs = nc.dram_tensor("xs", [TOK + 2, D], F32, kind="ExternalInput").ap()
    kk = nc.dram_tensor("kk", [NB, D], F32, kind="ExternalInput").ap()
    vv = nc.dram_tensor("vv", [NB, D], F32, kind="ExternalInput").ap()
    w1d = nc.dram_tensor("w1", [HID, D], F32, kind="ExternalInput").ap()
    w2d = nc.dram_tensor("w2", [D, HID], F32, kind="ExternalInput").ap()
    cwd = nc.dram_tensor("cw", [HID, 3], F32, kind="ExternalInput").ap()
    cbd = nc.dram_tensor("cb", [HID], F32, kind="ExternalInput").ap()
    id_in = nc.dram_tensor("ident", [P, P], F32, kind="ExternalInput").ap()
    outd = nc.dram_tensor("out", [TOK, D], F32, kind="ExternalOutput").ap()

    kt_s = nc.dram_tensor("kt_s", [32, P, 2048], BF16, kind="Internal").ap()
    v_s = nc.dram_tensor("v_s", [NB, D], BF16, kind="Internal").ap()
    w1t_s = nc.dram_tensor("w1t_s", [16, P, 2048], BF16, kind="Internal").ap()
    w2t_s = nc.dram_tensor("w2t_s", [16, P, 2048], BF16, kind="Internal").ap()

    p = Prog()

    with ExitStack() as S:
        def sb(name, shape, dtype=F32):
            return S.enter_context(nc.sbuf_tensor(name, shape, dtype))

        semh = {}
        sem_names = (["SYD", "VS", "SS", "TS", "GSD", "BAR", "SSD"]
                     + [f"SY{i}" for i in range(6)]
                     + [f"GP{i}" for i in range(5)]
                     + ["XL0", "XL1", "XH", "XA0", "XA1", "VO0", "VO1",
                        "KC0", "KC1", "WC0", "WC1", "WD0", "WD1",
                        "CS0", "CS1", "CS2", "KTS", "W1S", "W2S", "VCS",
                        "GS"])
        for s_name in sem_names:
            semh[s_name] = S.enter_context(nc.semaphore(s_name))

        ident = sb("identc", [P, P])
        identb = sb("identbc", [P, P], BF16)
        ones_colf = sb("ones_colf", [P, 1])
        ones_sq = sb("ones_sq", [P, P], BF16)
        ones_row = sb("ones_row", [1, P])
        taps = sb("tapsc", [P, 96])
        bias_c = sb("bias_c", [P, 32])
        c1col = sb("c1col", [P, 1])
        c2col = sb("c2col", [P, 1])
        sw1col = sb("sw1col", [P, 1])
        sw2col = sb("sw2col", [P, 1])

        p.op("sync", lambda h: h.dma_start(out=ident[:], in_=id_in),
             writes=["ident"], n=16, sem="CS0")
        p.op("sync", lambda h: h.dma_start(
            out=taps[:], in_=bass.AP(cwd.tensor, 0, [[3, P], [384, 32], [1, 3]])),
            writes=["taps"], n=16, sem="CS1")
        p.op("sync", lambda h: h.dma_start(
            out=bias_c[:], in_=bass.AP(cbd.tensor, 0, [[1, P], [P, 32]]),
            allow_slow_non_contiguous=True),
            writes=["bias"], n=16, sem="CS2")
        p.op("vector", lambda h: h.tensor_copy(identb[:], ident[:]),
             reads=["ident"], writes=["identb"])
        p.op("vector", lambda h: h.memset(ones_colf[:], 1.0), writes=["onesf"])
        p.op("vector", lambda h: h.memset(ones_sq[:], 1.0), writes=["onessq"])
        p.op("vector", lambda h: h.memset(ones_row[:], 1.0), writes=["onesr"])

        with ExitStack() as SU:
            def sbu(name, shape, dtype=F32):
                return SU.enter_context(nc.sbuf_tensor(name, shape, dtype))

            def psu(name, shape, dtype=F32):
                return SU.enter_context(nc.psum_tensor(name, shape, dtype))

            kch = [sbu("kch0", [P, D]), sbu("kch1", [P, D])]
            kbf = [sbu("kbf0", [P, D], BF16), sbu("kbf1", [P, D], BF16)]
            kstage = [sbu("kstage0", [P, 2048], BF16),
                      sbu("kstage1", [P, 2048], BF16)]
            ptk = psu("ptk", [P, 1024], BF16)
            w1ch = [sbu("w1ch0", [P, D]), sbu("w1ch1", [P, D])]
            w1q = sbu("w1q", [P, D])
            w1b = [sbu("w1b0", [P, D], BF16), sbu("w1b1", [P, D], BF16)]
            w1stage = [sbu("w1stage0", [P, 2048], BF16),
                       sbu("w1stage1", [P, 2048], BF16)]
            ptw1 = psu("ptw1", [P, 1024], BF16)
            w2ch = [sbu("w2ch0", [P, HID]), sbu("w2ch1", [P, HID])]
            w2q = sbu("w2q", [P, HID])
            w2full = sbu("w2full", [P, 8 * HID], BF16)
            w2stage = [sbu("w2stage0", [P, 2048], BF16),
                       sbu("w2stage1", [P, 2048], BF16)]
            ptw2 = psu("ptw2", [P, 1024], BF16)
            acc1 = sbu("acc1s", [P, 32])
            accw = sbu("accws", [P, 1])
            rowt = sbu("rowts", [1, 2])
            pmean = psu("pmean", [1, 2])
            pbcs = psu("pbcs", [P, 2])

            for i in range(8):
                p.op("gpsimd", lambda h, i=i: h.dma_start(
                    out=v_s[i * 1024:(i + 1) * 1024, :],
                    in_=vv[i * 1024:(i + 1) * 1024, :]),
                    writes=[f"v_s{i}"], n=16, sem="VCS")

            def mean_pass(wd, nchunks, width, chname, chbufs, queue, sw_col,
                          c_col, swn, cn):
                for i in range(nchunks):
                    p.op(queue, lambda h, i=i: h.dma_start(
                        out=chbufs[i % 2][:, 0:width],
                        in_=wd[i * P:(i + 1) * P, :]),
                        writes=[f"{chname}{i % 2}"], n=16,
                        sem=(f"WC{i % 2}" if queue == "scalar"
                             else f"WD{i % 2}"))
                    p.op("vector", lambda h, i=i: h.reduce_sum(
                        acc1[:, i:i + 1], chbufs[i % 2][:, 0:width], AX,
                        apply_absolute_value=True),
                        reads=[f"{chname}{i % 2}"], writes=[f"{chname}ac{i}"])
                p.op("vector", lambda h, n=nchunks: h.reduce_sum(
                    accw[:], acc1[:, 0:n], AX),
                    reads=[f"{chname}ac{i}" for i in range(nchunks)],
                    writes=["accw"])
                p.op("tensor", lambda h: h.matmul(
                    pmean[0:1, 0:1], ones_colf[:], accw[:],
                    start=True, stop=True),
                    reads=["accw", "onesf"], writes=["pmean"])
                p.op("vector", lambda h: h.tensor_scalar(
                    rowt[0:1, 0:1], pmean[0:1, 0:1], 1.0 / (HID * D), 1e-5,
                    op0=ALU.mult, op1=ALU.max),
                    reads=["pmean"], writes=["rowt0"])
                p.op("vector", lambda h: h.reciprocal(
                    rowt[0:1, 1:2], rowt[0:1, 0:1]),
                    reads=["rowt0"], writes=["rowt1"])
                p.op("tensor", lambda h: h.matmul(
                    pbcs[:, 0:1], ones_row[:], rowt[0:1, 1:2],
                    start=True, stop=True),
                    reads=["rowt1", "onesr"], writes=["pbcs0"])
                p.op("tensor", lambda h: h.matmul(
                    pbcs[:, 1:2], ones_row[:], rowt[0:1, 0:1],
                    start=True, stop=True),
                    reads=["rowt0", "onesr"], writes=["pbcs1"])
                p.op("vector", lambda h: h.tensor_copy(sw_col[:], pbcs[:, 0:1]),
                     reads=["pbcs0"], writes=[swn])
                p.op("vector", lambda h: h.tensor_scalar(
                    c_col[:], pbcs[:, 1:2], 1.0 / 127.0, None, op0=ALU.mult),
                    reads=["pbcs1"], writes=[cn])

            mean_pass(w1d, 32, D, "w1ch", w1ch, "scalar", sw1col, c1col,
                      "sw1", "c1")
            mean_pass(w2d, 8, HID, "w2ch", w2ch, "gpsimd", sw2col, c2col,
                      "sw2", "c2")

            for sl in range(32):
                par = sl % 2
                for j in range(2):
                    kcn = sl * 2 + j
                    bp = kcn % 2
                    p.op("sync", lambda h, kcn=kcn, bp=bp: h.dma_start(
                        out=kch[bp][:], in_=kk[kcn * P:(kcn + 1) * P, :]),
                        writes=[f"kch{bp}"], n=16, sem=f"KC{bp}")
                    p.op("vector", lambda h, bp=bp: h.tensor_copy(
                        kbf[bp][:], kch[bp][:]),
                        reads=[f"kch{bp}"], writes=[f"kbf{bp}"])
                    ev = None
                    for dc in range(8):
                        fn = (lambda h, bp=bp, dc=dc:
                              h.transpose(ptk[:, dc * P:(dc + 1) * P],
                                          kbf[bp][:, dc * P:(dc + 1) * P],
                                          identb[:]))
                        if dc == 0:
                            ev = p.op("tensor", fn,
                                      reads=[f"kbf{bp}", "identb"],
                                      writes=["ptk"], track=False)
                        else:
                            ev = p.op("tensor", fn, track=False)
                    p.note_read([f"kbf{bp}", "identb"], ev)
                    p.note_write(["ptk"], ev)
                    for dc in range(8):
                        p.op("vector",
                             lambda h, par=par, dc=dc, j=j: h.tensor_copy(
                                 kstage[par][:, dc * 256 + j * P:
                                             dc * 256 + (j + 1) * P],
                                 ptk[:, dc * P:(dc + 1) * P]),
                             reads=["ptk"], writes=[f"kst{par}_{j}_{dc}"])
                p.op("sync", lambda h, sl=sl, par=par: h.dma_start(
                    out=kt_s[sl], in_=kstage[par][:]),
                    reads=[f"kst{par}_{j}_{dc}" for j in range(2)
                           for dc in range(8)],
                    writes=[f"kt_s{sl}"], n=16, sem="KTS")

            for sl in range(16):
                par = sl % 2
                for j in range(2):
                    jc = sl * 2 + j
                    bp = jc % 2
                    p.op("scalar", lambda h, jc=jc, bp=bp: h.dma_start(
                        out=w1ch[bp][:], in_=w1d[jc * P:(jc + 1) * P, :]),
                        writes=[f"w1ch{bp}"], n=16, sem=f"WC{bp}")
                    p.op("vector", lambda h, bp=bp: h.tensor_scalar(
                        w1q[:], w1ch[bp][:], sw1col[:], None, op0=ALU.mult),
                        reads=[f"w1ch{bp}", "sw1"], writes=["w1q"])
                    p.op("vector", lambda h: h.tensor_scalar(
                        w1q[:], w1q[:], 1.0, -1.0, op0=ALU.min, op1=ALU.max),
                        reads=["w1q"], writes=["w1q"])
                    p.op("vector", lambda h, bp=bp: h.tensor_scalar(
                        w1b[bp][:], w1q[:], MAGIC, MAGIC,
                        op0=ALU.add, op1=ALU.subtract),
                        reads=["w1q"], writes=[f"w1b{bp}"])
                    ev = None
                    for dc in range(8):
                        fn = (lambda h, bp=bp, dc=dc:
                              h.transpose(ptw1[:, dc * P:(dc + 1) * P],
                                          w1b[bp][:, dc * P:(dc + 1) * P],
                                          identb[:]))
                        if dc == 0:
                            ev = p.op("tensor", fn,
                                      reads=[f"w1b{bp}", "identb"],
                                      writes=["ptw1"], track=False)
                        else:
                            ev = p.op("tensor", fn, track=False)
                    p.note_read([f"w1b{bp}", "identb"], ev)
                    p.note_write(["ptw1"], ev)
                    for dc in range(8):
                        p.op("vector",
                             lambda h, par=par, j=j, dc=dc: h.tensor_copy(
                                 w1stage[par][:, j * 1024 + dc * P:
                                              j * 1024 + (dc + 1) * P],
                                 ptw1[:, dc * P:(dc + 1) * P]),
                             reads=["ptw1"], writes=[f"w1st{par}_{j}_{dc}"])
                p.op("scalar", lambda h, sl=sl, par=par: h.dma_start(
                    out=w1t_s[sl], in_=w1stage[par][:]),
                    reads=[f"w1st{par}_{j}_{dc}" for j in range(2)
                           for dc in range(8)],
                    writes=[f"w1t_s{sl}"], n=16, sem="W1S")

            for i in range(8):
                bp = i % 2
                p.op("gpsimd", lambda h, i=i, bp=bp: h.dma_start(
                    out=w2ch[bp][:], in_=w2d[i * P:(i + 1) * P, :]),
                    writes=[f"w2ch{bp}"], n=16, sem=f"WD{bp}")
                p.op("vector", lambda h, bp=bp: h.tensor_scalar(
                    w2q[:], w2ch[bp][:], sw2col[:], None, op0=ALU.mult),
                    reads=[f"w2ch{bp}", "sw2"], writes=["w2q"])
                p.op("vector", lambda h: h.tensor_scalar(
                    w2q[:], w2q[:], 1.0, -1.0, op0=ALU.min, op1=ALU.max),
                    reads=["w2q"], writes=["w2q"])
                p.op("vector", lambda h, i=i: h.tensor_scalar(
                    w2full[:, i * HID:(i + 1) * HID], w2q[:], MAGIC, MAGIC,
                    op0=ALU.add, op1=ALU.subtract),
                    reads=["w2q"], writes=[f"w2f{i}"])
            for sl in range(16):
                par = sl % 2
                for j in range(2):
                    jc = sl * 2 + j
                    ev = None
                    for ocx in range(8):
                        fn = (lambda h, ocx=ocx, jc=jc:
                              h.transpose(ptw2[:, ocx * P:(ocx + 1) * P],
                                          w2full[:, ocx * HID + jc * P:
                                                 ocx * HID + (jc + 1) * P],
                                          identb[:]))
                        if ocx == 0:
                            ev = p.op("tensor", fn,
                                      reads=[f"w2f{x}" for x in range(8)]
                                      + ["identb"],
                                      writes=["ptw2"], track=False)
                        else:
                            ev = p.op("tensor", fn, track=False)
                    p.note_read([f"w2f{x}" for x in range(8)] + ["identb"], ev)
                    p.note_write(["ptw2"], ev)
                    for ocx in range(8):
                        p.op("vector",
                             lambda h, par=par, j=j, ocx=ocx: h.tensor_copy(
                                 w2stage[par][:, j * 1024 + ocx * P:
                                              j * 1024 + (ocx + 1) * P],
                                 ptw2[:, ocx * P:(ocx + 1) * P]),
                             reads=["ptw2"], writes=[f"w2st{par}_{j}_{ocx}"])
                p.op("gpsimd", lambda h, sl=sl, par=par: h.dma_start(
                    out=w2t_s[sl], in_=w2stage[par][:]),
                    reads=[f"w2st{par}_{j}_{ocx}" for j in range(2)
                           for ocx in range(8)],
                    writes=[f"w2t_s{sl}"], n=16, sem="W2S")

            p.barrier()
            for sl in range(32):
                p.W[f"kt_s{sl}"] = ("KTS", 16 * 32)
            for sl in range(16):
                p.W[f"w1t_s{sl}"] = ("W1S", 16 * 16)
                p.W[f"w2t_s{sl}"] = ("W2S", 16 * 16)
            for i in range(8):
                p.W[f"v_s{i}"] = ("VCS", 16 * 8)

        def ps(name, shape, dtype=F32):
            return S.enter_context(nc.psum_tensor(name, shape, dtype))

        pb = [ps(f"pb{i}", [P, 512]) for i in range(8)]
        PBN = [f"pb{i}" for i in range(8)]

        scT = sb("scT", [P, 64 * 512], BF16)
        y_full = sb("y_full", [P, 32 * 512], BF16)
        wsy = [sb(f"wsy{i}", [P, 2048], BF16) for i in range(6)]
        wgp = [sb(f"wgp{i}", [P, 2048], BF16) for i in range(5)]
        xnT = sb("xnT", [P, 8 * 512], BF16)
        xqT = sb("xqT", [P, 8 * 514], BF16)
        xln = [sb("xln0", [P, D]), sb("xln1", [P, D])]
        xadd = [sb("xadd0", [P, D]), sb("xadd1", [P, D])]
        qtc = [sb("qtc0", [P, D]), sb("qtc1", [P, D])]
        qtmp = sb("qtmp", [P, 512])
        sqz = [sb("sqz0", [P, 512], BF16), sb("sqz1", [P, 512], BF16),
               sb("sqz2", [P, 512], BF16)]
        y_scr = [sb("y_scr0", [P, 256]), sb("y_scr1", [P, 256])]
        mxq = sb("mxq", [P, 512])
        csbc = sb("csbc", [P, 512])
        csbc2 = sb("csbc2", [P, 512])
        invbc = sb("invbc", [P, 512])
        rows = sb("rows", [1, 1024])
        stat = sb("stat", [P, 32])
        x_halo = sb("x_halo", [2, D])
        xq_halo = sb("xq_halo", [2, D], BF16)
        stat_h = sb("stat_h", [2, 8])

        def bslot(tc, which):
            c0 = tc * 2048 + which * 1024
            names = [f"yf{tc * 4 + 2 * which + q}_{hf}"
                     for q in (0, 1) for hf in (0, 1)]
            return y_full[:, c0:c0 + 1024], names

        def mm_group(fns, bank, reads):
            ev = None
            for i, fn in enumerate(fns):
                if i == 0:
                    ev = p.op("tensor", fn, reads=reads, writes=[bank],
                              track=False)
                else:
                    ev = p.op("tensor", fn, track=False)
            p.note_read(reads, ev)
            p.note_write([bank], ev)
            return ev

        def sy_slot(t, i):
            return (t * 64 + i) % 6

        def gp_slot(t, i):
            return (t * 32 + i) % 5

        def emit_B_ln(t):
            g0 = t * T
            p.op("gpsimd", lambda h, g0=g0: h.dma_start(
                out=x_halo[:],
                in_=bass.AP(xs.tensor, g0 * D, [[513 * D, 2], [1, D]])),
                writes=["x_halo"], n=16, sem="XH")
            for tc in range(4):
                sl = tc % 2
                xnb, xnn = bslot(tc, 0)
                xqb, xqn = bslot(tc, 1)
                p.op("gpsimd", lambda h, tc=tc, sl=sl, g0=g0: h.dma_start(
                    out=xln[sl][:],
                    in_=xs[1 + g0 + tc * P: 1 + g0 + (tc + 1) * P, :]),
                    writes=[f"xln{sl}"], n=16, sem=f"XL{sl}")
                st = stat[:, tc * 8:tc * 8 + 8]
                sn = [f"st{tc}_{i}" for i in range(8)]
                p.op("vector", lambda h, sl=sl, st=st: h.reduce_sum(
                    st[:, 0:1], xln[sl][:], AX),
                    reads=[f"xln{sl}"], writes=[sn[0]])
                p.op("vector", lambda h, st=st: h.tensor_scalar(
                    st[:, 0:1], st[:, 0:1], 1.0 / D, None, op0=ALU.mult),
                    reads=[sn[0]], writes=[sn[0]])
                p.op("vector", lambda h, sl=sl, st=st: h.tensor_scalar(
                    xln[sl][:], xln[sl][:], st[:, 0:1], None,
                    op0=ALU.subtract),
                    reads=[sn[0]], writes=[f"xln{sl}"])
                p.op("scalar", lambda h, sl=sl, xqb=xqb, st=st: h.activation(
                    xqb[:], xln[sl][:], AF.Square, accum_out=st[:, 1:2]),
                    reads=[f"xln{sl}"], writes=xqn + [sn[1]])
                p.op("vector", lambda h, st=st: h.tensor_scalar(
                    st[:, 1:2], st[:, 1:2], 1.0 / D, EPS,
                    op0=ALU.mult, op1=ALU.add),
                    reads=[sn[1]], writes=[sn[1]])
                p.op("scalar", lambda h, st=st: h.activation(
                    st[:, 1:2], st[:, 1:2], AF.Sqrt),
                    reads=[sn[1]], writes=[sn[1]])
                p.op("vector", lambda h, st=st: h.reciprocal(
                    st[:, 2:3], st[:, 1:2]), reads=[sn[1]], writes=[sn[2]])
                p.op("vector", lambda h, sl=sl, st=st: h.reduce_max(
                    st[:, 3:4], xln[sl][:], AX, apply_absolute_value=True),
                    reads=[f"xln{sl}"], writes=[sn[3]])
                p.op("vector", lambda h, st=st: h.tensor_tensor(
                    st[:, 3:4], st[:, 3:4], st[:, 2:3], op=ALU.mult),
                    reads=[sn[3], sn[2]], writes=[sn[3]])
                p.op("vector", lambda h, st=st: h.tensor_scalar(
                    st[:, 3:4], st[:, 3:4], 1e-5, None, op0=ALU.max),
                    reads=[sn[3]], writes=[sn[3]])
                p.op("vector", lambda h, st=st: h.reciprocal(
                    st[:, 4:5], st[:, 3:4]), reads=[sn[3]], writes=[sn[4]])
                p.op("vector", lambda h, st=st: h.tensor_scalar(
                    st[:, 4:5], st[:, 4:5], 127.0, None, op0=ALU.mult),
                    reads=[sn[4]], writes=[sn[4]])
                p.op("vector", lambda h, st=st: h.tensor_tensor(
                    st[:, 5:6], st[:, 2:3], st[:, 4:5], op=ALU.mult),
                    reads=[sn[2], sn[4]], writes=[sn[5]])
                p.op("vector", lambda h, st=st: h.tensor_tensor(
                    st[:, 6:7], st[:, 3:4], c1col[:], op=ALU.mult),
                    reads=[sn[3], "c1"], writes=[sn[6]])
                p.op("vector", lambda h, sl=sl, xnb=xnb, st=st: h.tensor_scalar(
                    xnb[:], xln[sl][:], st[:, 2:3], None, op0=ALU.mult),
                    reads=[f"xln{sl}", sn[2]], writes=xnn)
                for hh in (0, 1):
                    p.op("vector", lambda h, sl=sl, st=st, hh=hh: h.tensor_scalar(
                        qtmp[:], xln[sl][:, hh * 512:(hh + 1) * 512],
                        st[:, 5:6], MAGIC, op0=ALU.mult, op1=ALU.add),
                        reads=[f"xln{sl}", sn[5]], writes=["qtmp"])
                    p.op("vector", lambda h: h.tensor_scalar(
                        qtmp[:], qtmp[:], MAGIC, 127.0,
                        op0=ALU.subtract, op1=ALU.min),
                        reads=["qtmp"], writes=["qtmp"])
                    p.op("vector", lambda h, xqb=xqb, st=st, hh=hh: h.tensor_scalar(
                        xqb[:, hh * 512:(hh + 1) * 512], qtmp[:], -128.0,
                        st[:, 6:7], op0=ALU.max, op1=ALU.mult),
                        reads=["qtmp", sn[6]], writes=xqn)

            p.op("vector", lambda h: h.reduce_sum(
                stat_h[:, 0:1], x_halo[:], AX), reads=["x_halo"], writes=["sh0"])
            p.op("vector", lambda h: h.tensor_scalar(
                stat_h[:, 0:1], stat_h[:, 0:1], 1.0 / D, None, op0=ALU.mult),
                reads=["sh0"], writes=["sh0"])
            p.op("vector", lambda h: h.tensor_scalar(
                x_halo[:], x_halo[:], stat_h[:, 0:1], None, op0=ALU.subtract),
                reads=["sh0"], writes=["x_halo"])
            p.op("scalar", lambda h: h.activation(
                xq_halo[:], x_halo[:], AF.Square, accum_out=stat_h[:, 1:2]),
                reads=["x_halo"], writes=["xq_halo", "sh1"])
            p.op("vector", lambda h: h.tensor_scalar(
                stat_h[:, 1:2], stat_h[:, 1:2], 1.0 / D, EPS,
                op0=ALU.mult, op1=ALU.add), reads=["sh1"], writes=["sh1"])
            p.op("scalar", lambda h: h.activation(
                stat_h[:, 1:2], stat_h[:, 1:2], AF.Sqrt),
                reads=["sh1"], writes=["sh1"])
            p.op("vector", lambda h: h.reciprocal(
                stat_h[:, 2:3], stat_h[:, 1:2]), reads=["sh1"], writes=["sh2"])
            p.op("vector", lambda h: h.reduce_max(
                stat_h[:, 3:4], x_halo[:], AX, apply_absolute_value=True),
                reads=["x_halo"], writes=["sh3"])
            p.op("vector", lambda h: h.tensor_tensor(
                stat_h[:, 3:4], stat_h[:, 3:4], stat_h[:, 2:3], op=ALU.mult),
                reads=["sh3", "sh2"], writes=["sh3"])
            p.op("vector", lambda h: h.tensor_scalar(
                stat_h[:, 3:4], stat_h[:, 3:4], 1e-5, None, op0=ALU.max),
                reads=["sh3"], writes=["sh3"])
            p.op("vector", lambda h: h.reciprocal(
                stat_h[:, 4:5], stat_h[:, 3:4]), reads=["sh3"], writes=["sh4"])
            p.op("vector", lambda h: h.tensor_scalar(
                stat_h[:, 4:5], stat_h[:, 4:5], 127.0, None, op0=ALU.mult),
                reads=["sh4"], writes=["sh4"])
            p.op("vector", lambda h: h.tensor_tensor(
                stat_h[:, 5:6], stat_h[:, 2:3], stat_h[:, 4:5], op=ALU.mult),
                reads=["sh2", "sh4"], writes=["sh5"])
            p.op("vector", lambda h: h.tensor_tensor(
                stat_h[:, 6:7], stat_h[:, 3:4], c1col[0:2, :], op=ALU.mult),
                reads=["sh3", "c1"], writes=["sh6"])
            p.op("vector", lambda h: h.tensor_scalar(
                x_halo[:], x_halo[:], stat_h[:, 5:6], MAGIC,
                op0=ALU.mult, op1=ALU.add),
                reads=["sh5"], writes=["x_halo"])
            p.op("vector", lambda h: h.tensor_scalar(
                x_halo[:], x_halo[:], MAGIC, 127.0,
                op0=ALU.subtract, op1=ALU.min),
                reads=["x_halo"], writes=["x_halo"])
            p.op("vector", lambda h: h.tensor_scalar(
                xq_halo[:], x_halo[:], -128.0, stat_h[:, 6:7],
                op0=ALU.max, op1=ALU.mult),
                reads=["x_halo", "sh6"], writes=["xq_halo"])

        def emit_B_tr(t, bc0):
            bcnt = bc0
            for tc in range(4):
                for which in (0, 1):
                    src, srcn = bslot(tc, which)
                    dst = xnT if which == 0 else xqT
                    dname = "xnT" if which == 0 else "xqT"
                    off = 0 if which == 0 else 1
                    w = 512 if which == 0 else 514
                    for hg in (0, 1):
                        bk = pb[2 + (bcnt % 2)]
                        bkn = PBN[2 + (bcnt % 2)]
                        bcnt += 1
                        ev = None
                        for dq in range(4):
                            dc = hg * 4 + dq
                            fn = (lambda h, src=src, dc=dc, dq=dq, bk=bk:
                                  h.matmul(bk[:, dq * P:(dq + 1) * P],
                                           src[:, dc * P:(dc + 1) * P],
                                           identb[:], start=True, stop=True))
                            if dq == 0:
                                ev = p.op("tensor", fn,
                                          reads=srcn + ["identb"],
                                          writes=[bkn], track=False)
                            else:
                                ev = p.op("tensor", fn, track=False)
                        p.note_read(srcn + ["identb"], ev)
                        p.note_write([bkn], ev)
                        c0 = hg * 4 * w + off + tc * P
                        rl = 8 * w
                        p.op("vector",
                             lambda h, dst=dst, c0=c0, w=w, bk=bk, rl=rl:
                             h.tensor_copy(
                                 bass.AP(dst, c0, [[rl, P], [w, 4], [1, P]]),
                                 bass.AP(bk, 0, [[512, P], [P, 4], [1, P]])),
                             reads=[bkn], writes=[dname])
            ev = None
            for dc in range(8):
                fn = (lambda h, dc=dc: h.matmul(
                    pb[2][:, dc * 2:dc * 2 + 2],
                    xq_halo[:, dc * P:(dc + 1) * P], identb[0:2, 0:2],
                    start=True, stop=True))
                if dc == 0:
                    ev = p.op("tensor", fn, reads=["xq_halo", "identb"],
                              writes=[PBN[2]], track=False)
                else:
                    ev = p.op("tensor", fn, track=False)
            p.note_read(["xq_halo", "identb"], ev)
            p.note_write([PBN[2]], ev)
            for dc in range(8):
                p.op("vector", lambda h, dc=dc: h.tensor_copy(
                    xqT[:, dc * 514:dc * 514 + 1], pb[2][:, dc * 2:dc * 2 + 1]),
                    reads=[PBN[2]], writes=["xqT"])
                p.op("vector", lambda h, dc=dc: h.tensor_copy(
                    xqT[:, dc * 514 + 513:dc * 514 + 514],
                    pb[2][:, dc * 2 + 1:dc * 2 + 2]),
                    reads=[PBN[2]], writes=["xqT"])

        def emit_main(t):
            g0 = t * T

            for ncx in range(64):
                sl = ncx // 2
                if ncx % 2 == 0:
                    ss = sy_slot(t, sl)
                    p.op("sync", lambda h, sl=sl, ss=ss: h.dma_start(
                        out=wsy[ss][:], in_=kt_s[sl]),
                        reads=[f"kt_s{sl}"], writes=[f"wsy{ss}"], n=16,
                        sem=f"SY{ss}")
                ss = sy_slot(t, ncx // 2)
                bank = PBN[ncx % 4]
                fns = []
                for dc in range(8):
                    fns.append(lambda h, ss=ss, ncx=ncx, dc=dc:
                               h.matmul(pb[ncx % 4][:],
                                        wsy[ss][
                                            :, dc * 256 + (ncx % 2) * P:
                                            dc * 256 + (ncx % 2 + 1) * P],
                                        xnT[:, dc * 512:(dc + 1) * 512],
                                        start=(dc == 0), stop=(dc == 7)))
                mm_group(fns, bank, [f"wsy{ss}", "xnT"])
                p.op("vector", lambda h, ncx=ncx: h.tensor_copy(
                    scT[:, ncx * 512:(ncx + 1) * 512], pb[ncx % 4][:]),
                    reads=[bank], writes=[f"sc{ncx}"])
                p.op("scalar", lambda h, ncx=ncx: h.activation(
                    sqz[ncx % 3][:], scT[:, ncx * 512:(ncx + 1) * 512],
                    AF.Square),
                    reads=[f"sc{ncx}"], writes=[f"sqz{ncx % 3}"])
                if ncx >= 2:
                    nl = ncx - 2
                    p.op("tensor", lambda h, nl=nl: h.matmul(
                        pb[4][:], ones_sq[:], sqz[nl % 3][:],
                        start=(nl == 0), stop=False),
                        reads=[f"sqz{nl % 3}", "onessq"], writes=[PBN[4]])
            for nl in (62, 63):
                p.op("tensor", lambda h, nl=nl: h.matmul(
                    pb[4][:], ones_sq[:], sqz[nl % 3][:],
                    start=False, stop=(nl == 63)),
                    reads=[f"sqz{nl % 3}", "onessq"], writes=[PBN[4]])

            p.op("vector", lambda h: h.tensor_copy(
                rows[0:1, 0:512], pb[4][0:1, :]),
                reads=[PBN[4]], writes=["rows"])
            p.op("scalar", lambda h: h.activation(
                rows[0:1, 0:512], rows[0:1, 0:512], AF.Sqrt),
                reads=["rows"], writes=["rows"])
            p.op("vector", lambda h: h.reciprocal(
                rows[0:1, 0:512], rows[0:1, 0:512]),
                reads=["rows"], writes=["rows"])
            p.op("vector", lambda h: h.tensor_scalar(
                rows[0:1, 0:512], rows[0:1, 0:512], SQRT_N, None,
                op0=ALU.mult), reads=["rows"], writes=["rows"])

            def minmax(jq):
                if jq == 0:
                    p.op("vector", lambda h: h.tensor_copy(
                        mxq[:], y_full[:, 0:512]),
                        reads=["yf0_0", "yf0_1"], writes=["mxq0", "mxq1"])
                    p.op("vector", lambda h: h.tensor_copy(
                        pb[7][:], y_full[:, 0:512]),
                        reads=["yf0_0", "yf0_1"], writes=[PBN[7]])
                else:
                    p.op("vector", lambda h, jq=jq: h.tensor_tensor(
                        mxq[:], mxq[:],
                        y_full[:, jq * 512:(jq + 1) * 512], op=ALU.max),
                        reads=[f"yf{jq}_0", f"yf{jq}_1", "mxq0", "mxq1"],
                        writes=["mxq0", "mxq1"])
                    p.op("vector", lambda h, jq=jq: h.tensor_tensor(
                        pb[7][:], pb[7][:],
                        y_full[:, jq * 512:(jq + 1) * 512], op=ALU.min),
                        reads=[f"yf{jq}_0", f"yf{jq}_1", PBN[7]],
                        writes=[PBN[7]])

            for jc in range(32):
                sl = jc // 2
                if jc == 1:
                    p.op("tensor", lambda h: h.matmul(
                        pb[5][:], ones_row[:], rows[0:1, 0:512],
                        start=True, stop=True),
                        reads=["rows", "onesr"], writes=[PBN[5]])
                    p.op("vector", lambda h: h.tensor_copy(csbc[:], pb[5][:]),
                         reads=[PBN[5]], writes=["csbc"])
                if jc % 2 == 0:
                    ws = sy_slot(t, 32 + sl)
                    p.op("sync", lambda h, sl=sl, ws=ws: h.dma_start(
                        out=wsy[ws][:], in_=w1t_s[sl]),
                        reads=[f"w1t_s{sl}"], writes=[f"wsy{ws}"], n=16,
                        sem=f"SY{ws}")
                ws = sy_slot(t, 32 + jc // 2)
                for hf in range(2):
                    ph = (2 * jc + hf) % 6
                    bank = PBN[ph]
                    fns = []
                    for dc in range(8):
                        fns.append(lambda h, ws=ws, jc=jc, dc=dc, hf=hf, ph=ph:
                                   h.matmul(pb[ph][:, 0:258],
                                            wsy[ws][
                                                :, (jc % 2) * 1024 + dc * P:
                                                (jc % 2) * 1024 + (dc + 1) * P],
                                            xqT[:, dc * 514 + hf * 256:
                                                dc * 514 + hf * 256 + 258],
                                            start=(dc == 0), stop=(dc == 7)))
                    mm_group(fns, bank, [f"wsy{ws}", "xqT"])
                    p.op("vector", lambda h, ph=ph, jc=jc: h.tensor_scalar(
                        y_scr[ph % 2][:], pb[ph][:, 1:257],
                        taps[:, jc * 3 + 1:jc * 3 + 2], None, op0=ALU.mult),
                        reads=[bank, "taps"], writes=[f"ysc{ph % 2}"])
                    p.op("vector", lambda h, ph=ph, jc=jc: h.scalar_tensor_tensor(
                        y_scr[ph % 2][:], pb[ph][:, 0:256],
                        taps[:, jc * 3:jc * 3 + 1], y_scr[ph % 2][:],
                        op0=ALU.mult, op1=ALU.add),
                        reads=[bank, "taps"], writes=[f"ysc{ph % 2}"])
                    p.op("vector", lambda h, ph=ph, jc=jc: h.scalar_tensor_tensor(
                        y_scr[ph % 2][:], pb[ph][:, 2:258],
                        taps[:, jc * 3 + 2:jc * 3 + 3], y_scr[ph % 2][:],
                        op0=ALU.mult, op1=ALU.add),
                        reads=[bank, "taps"], writes=[f"ysc{ph % 2}"])
                    p.op("scalar", lambda h, ph=ph, jc=jc, hf=hf: h.activation(
                        y_full[:, jc * 512 + hf * 256:jc * 512 + hf * 256 + 256],
                        y_scr[ph % 2][:], AF.Silu, bias=bias_c[:, jc:jc + 1]),
                        reads=[f"ysc{ph % 2}", "bias"], writes=[f"yf{jc}_{hf}"])
                if jc >= 1:
                    minmax(jc - 1)
                if 27 <= jc <= 31:
                    sv = jc - 27
                    gs = gp_slot(t, sv)
                    p.op("gpsimd", lambda h, sv=sv, gs=gs: h.dma_start(
                        out=bass.AP(wgp[gs], 0,
                                    [[2048, P], [1024, 2], [1, 1024]]),
                        in_=bass.AP(v_s.tensor, sv * 256 * D,
                                    [[D, P], [P * D, 2], [1, D]])),
                        reads=[f"v_s{sv // 4}"], writes=[f"wgp{gs}"], n=16,
                        sem=f"GP{gs}")
            minmax(31)
            p.op("vector", lambda h: h.tensor_scalar(
                pb[7][:], pb[7][:], -1.0, None, op0=ALU.mult),
                reads=[PBN[7]], writes=[PBN[7]])
            p.op("vector", lambda h: h.tensor_tensor(
                mxq[:], mxq[:], pb[7][:], op=ALU.max),
                reads=[PBN[7], "mxq0", "mxq1"],
                writes=["mxq0", "mxq1"])

            def hq_mxqT():
                ev = None
                for tc in range(4):
                    fn = (lambda h, tc=tc: h.transpose(
                        pb[6][:, tc * P:(tc + 1) * P],
                        mxq[:, tc * P:(tc + 1) * P], ident[:]))
                    if tc == 0:
                        ev = p.op("tensor", fn,
                                  reads=["mxq0", "mxq1", "ident"],
                                  writes=[PBN[6]], track=False)
                    else:
                        ev = p.op("tensor", fn, track=False)
                p.note_read(["mxq0", "mxq1", "ident"], ev)
                p.note_write([PBN[6]], ev)
                for tc in range(4):
                    st = stat[:, tc * 8:tc * 8 + 8]
                    p.op("vector", lambda h, tc=tc, st=st: h.reduce_max(
                        st[:, 7:8], pb[6][:, tc * P:(tc + 1) * P], AX),
                        reads=[PBN[6]], writes=[f"st{tc}_7"])

            def hq_statT():
                ev = None
                for tc in range(4):
                    ev = p.op("tensor", lambda h, tc=tc: h.transpose(
                        pb[7][0:1, tc * P:(tc + 1) * P],
                        stat[:, tc * 8 + 7:tc * 8 + 8], ident[:]),
                        reads=[f"st{tc}_7", "ident"],
                        writes=[PBN[7]] if tc == 0 else (), track=(tc == 0))
                p.note_write([PBN[7]], ev)
                p.note_read([f"st{tc}_7" for tc in range(4)], ev)
                p.op("vector", lambda h: h.tensor_scalar(
                    rows[0:1, 512:1024], pb[7][0:1, 0:512], 1e-5, None,
                    op0=ALU.max),
                    reads=[PBN[7]], writes=["rowsi"])
                p.op("vector", lambda h: h.reciprocal(
                    rows[0:1, 0:512], rows[0:1, 512:1024]),
                    reads=["rowsi"], writes=["rows"])
                p.op("vector", lambda h: h.tensor_scalar(
                    rows[0:1, 0:512], rows[0:1, 0:512], 127.0, None,
                    op0=ALU.mult),
                    reads=["rows"], writes=["rows"])
                p.op("vector", lambda h: h.tensor_scalar(
                    rows[0:1, 512:1024], rows[0:1, 512:1024], c2col[0:1, 0:1],
                    None, op0=ALU.mult),
                    reads=["rowsi", "c2"], writes=["rowsi"])

            def hq_bc1():
                p.op("tensor", lambda h: h.matmul(
                    pb[6][:], ones_row[:], rows[0:1, 0:512],
                    start=True, stop=True), reads=["rows", "onesr"],
                    writes=[PBN[6]])
                p.op("vector", lambda h: h.tensor_copy(csbc2[:], pb[6][:]),
                     reads=[PBN[6]], writes=["csbc2"])

            def hq_bc2():
                p.op("tensor", lambda h: h.matmul(
                    pb[7][:], ones_row[:], rows[0:1, 512:1024],
                    start=True, stop=True), reads=["rowsi", "onesr"],
                    writes=[PBN[7]])
                p.op("vector", lambda h: h.tensor_copy(
                    invbc[:, 0:512], pb[7][:]), reads=[PBN[7]],
                    writes=["invbc"])

            def emit_F(f):
                p.op("vector", lambda h, f=f: h.tensor_tensor(
                    sqz[f % 3][:], scT[:, f * 512:(f + 1) * 512],
                    csbc[:], op=ALU.mult),
                    reads=[f"sc{f}", "csbc"], writes=[f"sqz{f % 3}"])
                p.op("scalar", lambda h, f=f: h.activation(
                    scT[:, f * 512:(f + 1) * 512], sqz[f % 3][:],
                    AF.Gelu),
                    reads=[f"sqz{f % 3}"], writes=[f"sc{f}"])

            emit_F(0)
            emit_F(1)
            hq_steps = {1: hq_mxqT, 3: hq_statT, 5: hq_bc1, 7: hq_bc2}
            for ncx in range(8):
                for f in (2 * ncx + 2, 2 * ncx + 3):
                    if f < 64:
                        emit_F(f)
                sl = ncx // 2
                if ncx % 2 == 0 and sl >= 5:
                    gs = gp_slot(t, sl)
                    p.op("gpsimd", lambda h, sl=sl, gs=gs: h.dma_start(
                        out=bass.AP(wgp[gs], 0,
                                    [[2048, P], [1024, 2], [1, 1024]]),
                        in_=bass.AP(v_s.tensor, sl * 256 * D,
                                    [[D, P], [P * D, 2], [1, D]])),
                        reads=[f"v_s{sl // 4}"], writes=[f"wgp{gs}"], n=16,
                        sem=f"GP{gs}")
                gs = gp_slot(t, ncx // 2)
                ev = None
                for ol in range(6):
                    fn = (lambda h, gs=gs, ncx=ncx, ol=ol:
                          h.matmul(pb[ol][:],
                                   wgp[gs][
                                       :, (ncx % 2) * 1024 + ol * P:
                                       (ncx % 2) * 1024 + (ol + 1) * P],
                                   scT[:, ncx * 512:(ncx + 1) * 512],
                                   start=(ncx == 0), stop=False))
                    rd = [f"wgp{gs}", f"sc{ncx}"] if ol == 0 else ()
                    wr = [PBN[ol]] if ncx == 0 else ()
                    ev = p.op("tensor", fn, reads=rd, writes=wr,
                              track=False)
                p.note_read([f"wgp{gs}", f"sc{ncx}"], ev)
                if ncx in hq_steps:
                    hq_steps[ncx]()

            for ncx in range(8):
                gs = gp_slot(t, ncx // 2)
                ev = None
                for ol in (6, 7):
                    fn = (lambda h, gs=gs, ncx=ncx, ol=ol:
                          h.matmul(pb[ol][:],
                                   wgp[gs][
                                       :, (ncx % 2) * 1024 + ol * P:
                                       (ncx % 2) * 1024 + (ol + 1) * P],
                                   scT[:, ncx * 512:(ncx + 1) * 512],
                                   start=(ncx == 0), stop=False))
                    rd = [f"wgp{gs}", f"sc{ncx}"] if ol == 6 else ()
                    wr = [PBN[ol]] if ncx == 0 else ()
                    ev = p.op("tensor", fn, reads=rd, writes=wr,
                              track=False)
                p.note_read([f"wgp{gs}", f"sc{ncx}"], ev)

            last_ev = [None] * 8
            def emit_quant(jq):
                p.op("vector", lambda h, jq=jq: h.tensor_tensor(
                    mxq[:], y_full[:, jq * 512:(jq + 1) * 512], csbc2[:],
                    op=ALU.mult),
                    reads=[f"yf{jq}_0", f"yf{jq}_1", "csbc2"],
                    writes=["mxq0", "mxq1"])
                p.op("vector", lambda h: h.tensor_scalar(
                    mxq[:], mxq[:], 127.0, -128.0, op0=ALU.min,
                    op1=ALU.max),
                    reads=["mxq0", "mxq1"], writes=["mxq0", "mxq1"])
                p.op("vector", lambda h: h.tensor_scalar(
                    mxq[:], mxq[:], MAGIC, MAGIC, op0=ALU.add,
                    op1=ALU.subtract),
                    reads=["mxq0", "mxq1"], writes=["mxq0", "mxq1"])
                p.op("vector", lambda h, jq=jq: h.tensor_tensor(
                    y_full[:, jq * 512:(jq + 1) * 512], mxq[:],
                    invbc[:, 0:512], op=ALU.mult),
                    reads=["mxq0", "mxq1", "invbc"],
                    writes=[f"yf{jq}_0", f"yf{jq}_1"])

            for ncx in range(8, 64):
                for f in (2 * ncx + 2, 2 * ncx + 3):
                    if f < 64:
                        emit_F(f)
                if 8 <= ncx < 40:
                    emit_quant(ncx - 8)
                sl = ncx // 2
                if ncx % 2 == 0 and sl >= 5:
                    gs = gp_slot(t, sl)
                    p.op("gpsimd", lambda h, sl=sl, gs=gs: h.dma_start(
                        out=bass.AP(wgp[gs], 0,
                                    [[2048, P], [1024, 2], [1, 1024]]),
                        in_=bass.AP(v_s.tensor, sl * 256 * D,
                                    [[D, P], [P * D, 2], [1, D]])),
                        reads=[f"v_s{sl // 4}"], writes=[f"wgp{gs}"], n=16,
                        sem=f"GP{gs}")
                gs = gp_slot(t, ncx // 2)
                ev = None
                for ol in range(8):
                    fn = (lambda h, gs=gs, ncx=ncx, ol=ol:
                          h.matmul(pb[ol][:],
                                   wgp[gs][
                                       :, (ncx % 2) * 1024 + ol * P:
                                       (ncx % 2) * 1024 + (ol + 1) * P],
                                   scT[:, ncx * 512:(ncx + 1) * 512],
                                   start=False, stop=False))
                    rd = [f"wgp{gs}", f"sc{ncx}"] if ol == 0 else ()
                    ev = p.op("tensor", fn, reads=rd, track=False)
                p.note_read([f"wgp{gs}", f"sc{ncx}"], ev)
            for jc in range(32):
                sl2 = jc // 2
                if jc % 2 == 0:
                    ss = sy_slot(t, 48 + sl2)
                    p.op("sync", lambda h, sl2=sl2, ss=ss: h.dma_start(
                        out=wsy[ss][:], in_=w2t_s[sl2]),
                        reads=[f"w2t_s{sl2}"], writes=[f"wsy{ss}"], n=16,
                        sem=f"SY{ss}")
                ss = sy_slot(t, 48 + jc // 2)
                ev = None
                for ol in range(8):
                    fn = (lambda h, ss=ss, jc=jc, ol=ol:
                          h.matmul(pb[ol][:],
                                   wsy[ss][
                                       :, (jc % 2) * 1024 + ol * P:
                                       (jc % 2) * 1024 + (ol + 1) * P],
                                   y_full[:, jc * 512:(jc + 1) * 512],
                                   start=False, stop=(jc == 31)))
                    ev = p.op("tensor", fn,
                              reads=[f"wsy{ss}", f"yf{jc}_0", f"yf{jc}_1"]
                              if ol == 0 else (), track=False)
                    if jc == 31:
                        last_ev[ol] = ev
                p.note_read([f"wsy{ss}", f"yf{jc}_0", f"yf{jc}_1"], ev)
            for ol in range(8):
                p.note_write([PBN[ol]], last_ev[ol])

        def emit_evac(t):
            g0 = t * T
            xa = [xadd[0], xadd[1], xadd[0], xadd[1]]
            xan = ["xadd0", "xadd1", "xadd0", "xadd1"]
            for tc in range(2):
                p.op("gpsimd", lambda h, tc=tc, g0=g0: h.dma_start(
                    out=xa[tc][:],
                    in_=xs[1 + g0 + tc * P: 1 + g0 + (tc + 1) * P, :]),
                    writes=[xan[tc]], n=16, sem=f"XA{tc % 2}")
            for ol in range(8):
                stg = mxq if ol % 2 == 0 else csbc
                stgn = ["mxq0", "mxq1"] if ol % 2 == 0 else ["csbc"]
                p.op("vector", lambda h, ol=ol, stg=stg: h.tensor_copy(
                    stg[:], pb[ol][:]),
                    reads=[PBN[ol]], writes=stgn)
                ev = None
                for tc in range(4):
                    fn = (lambda h, ol=ol, tc=tc, stg=stg: h.transpose(
                        pb[ol][:, tc * P:(tc + 1) * P],
                        stg[:, tc * P:(tc + 1) * P], ident[:]))
                    if tc == 0:
                        ev = p.op("tensor", fn, reads=stgn + ["ident"],
                                  writes=[PBN[ol]], track=False)
                    else:
                        ev = p.op("tensor", fn, track=False)
                p.note_read(stgn + ["ident"], ev)
                p.note_write([PBN[ol]], ev)
            for tc in range(4):
                for ol in range(8):
                    p.op("vector",
                         lambda h, tc=tc, ol=ol: h.scalar_tensor_tensor(
                             qtc[tc % 2][:, ol * P:(ol + 1) * P],
                             pb[ol][:, tc * P:(tc + 1) * P], 1.0,
                             xa[tc][:, ol * P:(ol + 1) * P],
                             op0=ALU.mult, op1=ALU.add),
                         reads=[PBN[ol], xan[tc]],
                         writes=[f"qtc{tc % 2}"])
                p.op("scalar", lambda h, tc=tc, g0=g0: h.dma_start(
                    out=outd[g0 + tc * P:g0 + (tc + 1) * P, :],
                    in_=qtc[tc % 2][:]),
                    reads=[f"qtc{tc % 2}"], writes=[f"o{t}_{tc}"],
                    n=16, sem=f"VO{tc % 2}")
                if tc < 2:
                    p.op("gpsimd", lambda h, tc=tc, g0=g0: h.dma_start(
                        out=xa[tc + 2][:],
                        in_=xs[1 + g0 + (tc + 2) * P: 1 + g0 + (tc + 3) * P, :]),
                        writes=[xan[tc + 2]], n=16, sem=f"XA{tc % 2}")

        emit_B_ln(0)
        emit_B_tr(0, 0)
        for t in range(n_tiles):
            emit_main(t)
            if t + 1 < n_tiles:
                emit_B_ln(t + 1)
            emit_evac(t)
            if t + 1 < n_tiles:
                emit_B_tr(t + 1, (t + 1) % 2)

        sync_sems = [f"SY{i}" for i in range(6)] + ["KC0", "KC1",
                     "CS0", "CS1", "CS2", "KTS"]
        gp_sems = ([f"GP{i}" for i in range(5)]
                   + ["XL0", "XL1", "XH", "XA0", "XA1",
                      "WD0", "WD1", "W2S", "VCS"])
        sc_sems = ["WC0", "WC1", "W1S", "VO0", "VO1"]
        p.wait("sync", [(s, p.cnt.get(s, 0)) for s in sync_sems])
        p.wait("gpsimd", [(s, p.cnt.get(s, 0)) for s in gp_sems])
        p.wait("scalar", [(s, p.cnt.get(s, 0)) for s in sc_sems])

        block = S.enter_context(nc.Block())
        p.emit(block, semh)

    return nc


def kernel(L0_attn_q, L0_attn_k, L0_attn_v, L0_ffn_w1, L0_ffn_conv_w,
           L0_ffn_conv_b, L0_ffn_w2):
    x = np.ascontiguousarray(np.asarray(L0_attn_q, dtype=np.float32))
    k = np.ascontiguousarray(np.asarray(L0_attn_k, dtype=np.float32))
    v = np.ascontiguousarray(np.asarray(L0_attn_v, dtype=np.float32))
    w1 = np.ascontiguousarray(np.asarray(L0_ffn_w1, dtype=np.float32))
    cw = np.ascontiguousarray(
        np.asarray(L0_ffn_conv_w, dtype=np.float32).reshape(HID, 3))
    cb = np.ascontiguousarray(np.asarray(L0_ffn_conv_b, dtype=np.float32))
    w2 = np.ascontiguousarray(np.asarray(L0_ffn_w2, dtype=np.float32))
    ident = np.eye(P, dtype=np.float32)

    zrow = np.zeros((1, D), np.float32)
    in_maps = []
    for c in range(8):
        b, hh = c // 2, c % 2
        tok0 = hh * TOK
        left = x[b, tok0 - 1:tok0] if hh == 1 else zrow
        right = x[b, tok0 + TOK:tok0 + TOK + 1] if hh == 0 else zrow
        xsl = np.concatenate([left, x[b, tok0:tok0 + TOK], right], axis=0)
        in_maps.append(dict(xs=np.ascontiguousarray(xsl), kk=k, vv=v, w1=w1,
                            w2=w2, cw=cw, cb=cb, ident=ident))

    nc = build_nc()
    res = run_bass_kernel_spmd(nc, in_maps, core_ids=list(range(8)))
    outp = np.empty((4, 8192, D), np.float32)
    for c in range(8):
        b, hh = c // 2, c % 2
        outp[b, hh * TOK:(hh + 1) * TOK] = np.asarray(res.results[c]["out"])
    return outp


# version 13
# speedup vs baseline: 1.1179x; 1.0059x over previous
import numpy as np
import concourse.bass as bass
import concourse.mybir as mybir
from concourse.bass_utils import run_bass_kernel_spmd
from contextlib import ExitStack

F32 = mybir.dt.float32
BF16 = mybir.dt.bfloat16
AF = mybir.ActivationFunctionType
ALU = mybir.AluOpType
AX = mybir.AxisListType.X

MAGIC = float(np.float32(3 * 2.0 ** 22))
D = 1024
NB = 8192
HID = 4096
TOK = 4096
T = 512
NT = TOK // T
SQRT_N = float(np.float32(np.sqrt(np.float32(NB))))
EPS = 1e-5
P = 128

ENGS = ("sync", "vector", "scalar", "tensor", "gpsimd")
ESEM = {"sync": "SYD", "vector": "VS", "scalar": "SS", "tensor": "TS",
        "gpsimd": "GSD"}


class Prog:

    def __init__(self):
        self.q = {e: [] for e in ENGS}
        self.cnt = {}
        self.wc = {e: {} for e in ENGS}
        self.W = {}
        self.R = {}

    def op(self, eng, fn, reads=(), writes=(), extra=(), sem=None, n=1,
           track=True):
        sem = sem if sem is not None else ESEM[eng]
        waits = []
        for b in reads:
            waits.append(self.W.get(b))
        for b in writes:
            waits.append(self.W.get(b))
            waits.extend(self.R.get(b, ()))
        waits.extend(extra)
        wl = []
        for w in waits:
            if w is None:
                continue
            s, v = w
            if v <= 0:
                continue
            assert v <= self.cnt.get(s, 0), f"wait on future {s}={v}"
            if self.wc[eng].get(s, 0) >= v:
                continue
            self.wc[eng][s] = v
            wl.append((s, v))
        self.cnt[sem] = self.cnt.get(sem, 0) + n
        ev = (sem, self.cnt[sem])
        self.q[eng].append((fn, tuple(wl), sem, n))
        if track:
            for b in reads:
                self.R.setdefault(b, []).append(ev)
            for b in writes:
                self.W[b] = ev
                self.R[b] = []
        return ev

    def wait(self, eng, evs):
        wl = []
        for w in evs:
            if w is None:
                continue
            s, v = w
            assert v <= self.cnt.get(s, 0)
            if self.wc[eng].get(s, 0) >= v:
                continue
            self.wc[eng][s] = v
            wl.append((s, v))
        if wl:
            self.q[eng].append((None, tuple(wl), None, 0))

    def note_read(self, bufs, ev):
        for b in bufs:
            self.R.setdefault(b, []).append(ev)

    def note_write(self, bufs, ev):
        for b in bufs:
            self.W[b] = ev
            self.R[b] = []

    def barrier(self):
        tgt = self.cnt.get("BAR", 0) + len(ENGS)
        for e in ENGS:
            self.op(e, lambda h: h.nop(), sem="BAR", track=False)
        for e in ENGS:
            self.wait(e, [("BAR", tgt)])

    def emit(self, block, semh):
        decos = {"sync": block.sync, "vector": block.vector,
                 "scalar": block.scalar, "tensor": block.tensor,
                 "gpsimd": block.gpsimd}
        for e in ENGS:
            q = self.q[e]

            def body(h, q=q):
                for fn, waits, sem, n in q:
                    for (s, v) in waits:
                        h.wait_ge(semh[s], v)
                    if fn is not None:
                        ins = fn(h)
                        if sem is not None:
                            ins.then_inc(semh[sem], n)
            decos[e](body)


def build_nc(n_tiles=NT):
    nc = bass.Bass("TRN2", target_bir_lowering=False)

    xs = nc.dram_tensor("xs", [TOK + 2, D], F32, kind="ExternalInput").ap()
    kk = nc.dram_tensor("kk", [NB, D], F32, kind="ExternalInput").ap()
    vv = nc.dram_tensor("vv", [NB, D], F32, kind="ExternalInput").ap()
    w1d = nc.dram_tensor("w1", [HID, D], F32, kind="ExternalInput").ap()
    w2d = nc.dram_tensor("w2", [D, HID], F32, kind="ExternalInput").ap()
    cwd = nc.dram_tensor("cw", [HID, 3], F32, kind="ExternalInput").ap()
    cbd = nc.dram_tensor("cb", [HID], F32, kind="ExternalInput").ap()
    id_in = nc.dram_tensor("ident", [P, P], F32, kind="ExternalInput").ap()
    outd = nc.dram_tensor("out", [TOK, D], F32, kind="ExternalOutput").ap()

    kt_s = nc.dram_tensor("kt_s", [32, P, 2048], BF16, kind="Internal").ap()
    v_s = nc.dram_tensor("v_s", [NB, D], BF16, kind="Internal").ap()
    w1t_s = nc.dram_tensor("w1t_s", [16, P, 2048], BF16, kind="Internal").ap()
    w2t_s = nc.dram_tensor("w2t_s", [16, P, 2048], BF16, kind="Internal").ap()

    p = Prog()

    with ExitStack() as S:
        def sb(name, shape, dtype=F32):
            return S.enter_context(nc.sbuf_tensor(name, shape, dtype))

        semh = {}
        sem_names = (["VS", "SS", "TS", "BAR"]
                     + [f"SY{i}" for i in range(6)]
                     + [f"GP{i}" for i in range(5)]
                     + ["XL0", "XL1", "XH", "XA0", "XA1", "VO0", "VO1",
                        "KC0", "KC1", "WC0", "WC1", "WD0", "WD1",
                        "CS0", "CS1", "CS2", "KT0", "KT1", "W1S", "W2S",
                        "VC0", "VC1"])
        for s_name in sem_names:
            semh[s_name] = S.enter_context(nc.semaphore(s_name))

        ident = sb("identc", [P, P])
        identb = sb("identbc", [P, P], BF16)
        ones_colf = sb("ones_colf", [P, 1])
        ones_sq = sb("ones_sq", [P, P], BF16)
        ones_row = sb("ones_row", [1, P])
        taps = sb("tapsc", [P, 96])
        bias_c = sb("bias_c", [P, 32])
        c1col = sb("c1col", [P, 1])
        c2col = sb("c2col", [P, 1])
        sw1col = sb("sw1col", [P, 1])
        sw2col = sb("sw2col", [P, 1])

        p.op("sync", lambda h: h.dma_start(out=ident[:], in_=id_in),
             writes=["ident"], n=16, sem="CS0")
        p.op("sync", lambda h: h.dma_start(
            out=taps[:], in_=bass.AP(cwd.tensor, 0, [[3, P], [384, 32], [1, 3]])),
            writes=["taps"], n=16, sem="CS1")
        p.op("sync", lambda h: h.dma_start(
            out=bias_c[:], in_=bass.AP(cbd.tensor, 0, [[1, P], [P, 32]]),
            allow_slow_non_contiguous=True),
            writes=["bias"], n=16, sem="CS2")
        p.op("vector", lambda h: h.tensor_copy(identb[:], ident[:]),
             reads=["ident"], writes=["identb"])
        p.op("vector", lambda h: h.memset(ones_colf[:], 1.0), writes=["onesf"])
        p.op("vector", lambda h: h.memset(ones_sq[:], 1.0), writes=["onessq"])
        p.op("vector", lambda h: h.memset(ones_row[:], 1.0), writes=["onesr"])

        with ExitStack() as SU:
            def sbu(name, shape, dtype=F32):
                return SU.enter_context(nc.sbuf_tensor(name, shape, dtype))

            def psu(name, shape, dtype=F32):
                return SU.enter_context(nc.psum_tensor(name, shape, dtype))

            kch = [sbu("kch0", [P, D]), sbu("kch1", [P, D])]
            kbf = [sbu("kbf0", [P, D], BF16), sbu("kbf1", [P, D], BF16)]
            kstage = [sbu("kstage0", [P, 2048], BF16),
                      sbu("kstage1", [P, 2048], BF16)]
            ptk = psu("ptk", [P, 1024], BF16)
            w1ch = [sbu("w1ch0", [P, D]), sbu("w1ch1", [P, D])]
            w1q = sbu("w1q", [P, D])
            w1b = [sbu("w1b0", [P, D], BF16), sbu("w1b1", [P, D], BF16)]
            w1stage = [sbu("w1stage0", [P, 2048], BF16),
                       sbu("w1stage1", [P, 2048], BF16)]
            ptw1 = psu("ptw1", [P, 1024], BF16)
            w2ch = [sbu("w2ch0", [P, HID]), sbu("w2ch1", [P, HID])]
            w2q = sbu("w2q", [P, HID])
            w2full = sbu("w2full", [P, 8 * HID], BF16)
            w2stage = [sbu("w2stage0", [P, 2048], BF16),
                       sbu("w2stage1", [P, 2048], BF16)]
            ptw2 = psu("ptw2", [P, 1024], BF16)
            acc1 = sbu("acc1s", [P, 32])
            accw = sbu("accws", [P, 1])
            rowt = sbu("rowts", [1, 2])
            pmean = psu("pmean", [1, 2])
            pbcs = psu("pbcs", [P, 2])

            def mean_pass(wd, nchunks, width, chname, chbufs, queue, sw_col,
                          c_col, swn, cn):
                for i in range(nchunks):
                    p.op(queue, lambda h, i=i: h.dma_start(
                        out=chbufs[i % 2][:, 0:width],
                        in_=wd[i * P:(i + 1) * P, :]),
                        writes=[f"{chname}{i % 2}"], n=16,
                        sem=(f"WC{i % 2}" if queue == "scalar"
                             else f"WD{i % 2}"))
                    p.op("vector", lambda h, i=i: h.reduce_sum(
                        acc1[:, i:i + 1], chbufs[i % 2][:, 0:width], AX,
                        apply_absolute_value=True),
                        reads=[f"{chname}{i % 2}"], writes=[f"{chname}ac{i}"])
                p.op("vector", lambda h, n=nchunks: h.reduce_sum(
                    accw[:], acc1[:, 0:n], AX),
                    reads=[f"{chname}ac{i}" for i in range(nchunks)],
                    writes=["accw"])
                p.op("tensor", lambda h: h.matmul(
                    pmean[0:1, 0:1], ones_colf[:], accw[:],
                    start=True, stop=True),
                    reads=["accw", "onesf"], writes=["pmean"])
                p.op("vector", lambda h: h.tensor_scalar(
                    rowt[0:1, 0:1], pmean[0:1, 0:1], 1.0 / (HID * D), 1e-5,
                    op0=ALU.mult, op1=ALU.max),
                    reads=["pmean"], writes=["rowt0"])
                p.op("vector", lambda h: h.reciprocal(
                    rowt[0:1, 1:2], rowt[0:1, 0:1]),
                    reads=["rowt0"], writes=["rowt1"])
                p.op("tensor", lambda h: h.matmul(
                    pbcs[:, 0:1], ones_row[:], rowt[0:1, 1:2],
                    start=True, stop=True),
                    reads=["rowt1", "onesr"], writes=["pbcs0"])
                p.op("tensor", lambda h: h.matmul(
                    pbcs[:, 1:2], ones_row[:], rowt[0:1, 0:1],
                    start=True, stop=True),
                    reads=["rowt0", "onesr"], writes=["pbcs1"])
                p.op("vector", lambda h: h.tensor_copy(sw_col[:], pbcs[:, 0:1]),
                     reads=["pbcs0"], writes=[swn])
                p.op("vector", lambda h: h.tensor_scalar(
                    c_col[:], pbcs[:, 1:2], 1.0 / 127.0, None, op0=ALU.mult),
                    reads=["pbcs1"], writes=[cn])

            mean_pass(w1d, 32, D, "w1ch", w1ch, "scalar", sw1col, c1col,
                      "sw1", "c1")
            mean_pass(w2d, 8, HID, "w2ch", w2ch, "gpsimd", sw2col, c2col,
                      "sw2", "c2")

            for sl in range(32):
                par = sl % 2
                for j in range(2):
                    kcn = sl * 2 + j
                    bp = kcn % 2
                    p.op("sync", lambda h, kcn=kcn, bp=bp: h.dma_start(
                        out=kch[bp][:], in_=kk[kcn * P:(kcn + 1) * P, :]),
                        writes=[f"kch{bp}"], n=16, sem=f"KC{bp}")
                    p.op("vector", lambda h, bp=bp: h.tensor_copy(
                        kbf[bp][:], kch[bp][:]),
                        reads=[f"kch{bp}"], writes=[f"kbf{bp}"])
                    ev = None
                    for dc in range(8):
                        fn = (lambda h, bp=bp, dc=dc:
                              h.transpose(ptk[:, dc * P:(dc + 1) * P],
                                          kbf[bp][:, dc * P:(dc + 1) * P],
                                          identb[:]))
                        if dc == 0:
                            ev = p.op("tensor", fn,
                                      reads=[f"kbf{bp}", "identb"],
                                      writes=["ptk"], track=False)
                        else:
                            ev = p.op("tensor", fn, track=False)
                    p.note_read([f"kbf{bp}", "identb"], ev)
                    p.note_write(["ptk"], ev)
                    for dc in range(8):
                        p.op("vector",
                             lambda h, par=par, dc=dc, j=j: h.tensor_copy(
                                 kstage[par][:, dc * 256 + j * P:
                                             dc * 256 + (j + 1) * P],
                                 ptk[:, dc * P:(dc + 1) * P]),
                             reads=["ptk"], writes=[f"kst{par}_{j}_{dc}"])
                p.op("sync", lambda h, sl=sl, par=par: h.dma_start(
                    out=kt_s[sl], in_=kstage[par][:]),
                    reads=[f"kst{par}_{j}_{dc}" for j in range(2)
                           for dc in range(8)],
                    writes=[f"kt_s{sl}"], n=16, sem=f"KT{sl // 16}")

            for sl in range(16):
                par = sl % 2
                for j in range(2):
                    jc = sl * 2 + j
                    bp = jc % 2
                    p.op("scalar", lambda h, jc=jc, bp=bp: h.dma_start(
                        out=w1ch[bp][:], in_=w1d[jc * P:(jc + 1) * P, :]),
                        writes=[f"w1ch{bp}"], n=16, sem=f"WC{bp}")
                    p.op("vector", lambda h, bp=bp: h.tensor_scalar(
                        w1q[:], w1ch[bp][:], sw1col[:], None, op0=ALU.mult),
                        reads=[f"w1ch{bp}", "sw1"], writes=["w1q"])
                    p.op("vector", lambda h: h.tensor_scalar(
                        w1q[:], w1q[:], 1.0, -1.0, op0=ALU.min, op1=ALU.max),
                        reads=["w1q"], writes=["w1q"])
                    p.op("vector", lambda h, bp=bp: h.tensor_scalar(
                        w1b[bp][:], w1q[:], MAGIC, MAGIC,
                        op0=ALU.add, op1=ALU.subtract),
                        reads=["w1q"], writes=[f"w1b{bp}"])
                    ev = None
                    for dc in range(8):
                        fn = (lambda h, bp=bp, dc=dc:
                              h.transpose(ptw1[:, dc * P:(dc + 1) * P],
                                          w1b[bp][:, dc * P:(dc + 1) * P],
                                          identb[:]))
                        if dc == 0:
                            ev = p.op("tensor", fn,
                                      reads=[f"w1b{bp}", "identb"],
                                      writes=["ptw1"], track=False)
                        else:
                            ev = p.op("tensor", fn, track=False)
                    p.note_read([f"w1b{bp}", "identb"], ev)
                    p.note_write(["ptw1"], ev)
                    for dc in range(8):
                        p.op("vector",
                             lambda h, par=par, j=j, dc=dc: h.tensor_copy(
                                 w1stage[par][:, j * 1024 + dc * P:
                                              j * 1024 + (dc + 1) * P],
                                 ptw1[:, dc * P:(dc + 1) * P]),
                             reads=["ptw1"], writes=[f"w1st{par}_{j}_{dc}"])
                p.op("scalar", lambda h, sl=sl, par=par: h.dma_start(
                    out=w1t_s[sl], in_=w1stage[par][:]),
                    reads=[f"w1st{par}_{j}_{dc}" for j in range(2)
                           for dc in range(8)],
                    writes=[f"w1t_s{sl}"], n=16, sem="W1S")

            for i in range(8):
                bp = i % 2
                p.op("gpsimd", lambda h, i=i, bp=bp: h.dma_start(
                    out=w2ch[bp][:], in_=w2d[i * P:(i + 1) * P, :]),
                    writes=[f"w2ch{bp}"], n=16, sem=f"WD{bp}")
                p.op("vector", lambda h, bp=bp: h.tensor_scalar(
                    w2q[:], w2ch[bp][:], sw2col[:], None, op0=ALU.mult),
                    reads=[f"w2ch{bp}", "sw2"], writes=["w2q"])
                p.op("vector", lambda h: h.tensor_scalar(
                    w2q[:], w2q[:], 1.0, -1.0, op0=ALU.min, op1=ALU.max),
                    reads=["w2q"], writes=["w2q"])
                p.op("vector", lambda h, i=i: h.tensor_scalar(
                    w2full[:, i * HID:(i + 1) * HID], w2q[:], MAGIC, MAGIC,
                    op0=ALU.add, op1=ALU.subtract),
                    reads=["w2q"], writes=[f"w2f{i}"])
            for sl in range(16):
                par = sl % 2
                for j in range(2):
                    jc = sl * 2 + j
                    ev = None
                    for ocx in range(8):
                        fn = (lambda h, ocx=ocx, jc=jc:
                              h.transpose(ptw2[:, ocx * P:(ocx + 1) * P],
                                          w2full[:, ocx * HID + jc * P:
                                                 ocx * HID + (jc + 1) * P],
                                          identb[:]))
                        if ocx == 0:
                            ev = p.op("tensor", fn,
                                      reads=[f"w2f{x}" for x in range(8)]
                                      + ["identb"],
                                      writes=["ptw2"], track=False)
                        else:
                            ev = p.op("tensor", fn, track=False)
                    p.note_read([f"w2f{x}" for x in range(8)] + ["identb"], ev)
                    p.note_write(["ptw2"], ev)
                    for ocx in range(8):
                        p.op("vector",
                             lambda h, par=par, j=j, ocx=ocx: h.tensor_copy(
                                 w2stage[par][:, j * 1024 + ocx * P:
                                              j * 1024 + (ocx + 1) * P],
                                 ptw2[:, ocx * P:(ocx + 1) * P]),
                             reads=["ptw2"], writes=[f"w2st{par}_{j}_{ocx}"])
                p.op("gpsimd", lambda h, sl=sl, par=par: h.dma_start(
                    out=w2t_s[sl], in_=w2stage[par][:]),
                    reads=[f"w2st{par}_{j}_{ocx}" for j in range(2)
                           for ocx in range(8)],
                    writes=[f"w2t_s{sl}"], n=16, sem="W2S")

            for i in range(8):
                p.op("gpsimd", lambda h, i=i: h.dma_start(
                    out=v_s[i * 1024:(i + 1) * 1024, :],
                    in_=vv[i * 1024:(i + 1) * 1024, :]),
                    writes=[f"v_s{i}"], n=16, sem=f"VC{i // 4}")

            p.barrier()
            for sl in range(32):
                p.W[f"kt_s{sl}"] = (f"KT{sl // 16}", 16 * 16)
            for sl in range(16):
                p.W[f"w1t_s{sl}"] = ("W1S", 16 * 16)
                p.W[f"w2t_s{sl}"] = ("W2S", 16 * 16)
            for i in range(8):
                p.W[f"v_s{i}"] = (f"VC{i // 4}", 16 * 4)

        def ps(name, shape, dtype=F32):
            return S.enter_context(nc.psum_tensor(name, shape, dtype))

        pb = [ps(f"pb{i}", [P, 512]) for i in range(8)]
        PBN = [f"pb{i}" for i in range(8)]

        scT = sb("scT", [P, 64 * 512], BF16)
        y_full = sb("y_full", [P, 32 * 512], BF16)
        wsy = [sb(f"wsy{i}", [P, 2048], BF16) for i in range(6)]
        wgp = [sb(f"wgp{i}", [P, 2048], BF16) for i in range(5)]
        xnT = sb("xnT", [P, 8 * 512], BF16)
        xqT = sb("xqT", [P, 8 * 514], BF16)
        xln = [sb("xln0", [P, D]), sb("xln1", [P, D])]
        xadd = [sb("xadd0", [P, D]), sb("xadd1", [P, D])]
        qtc = [sb("qtc0", [P, D]), sb("qtc1", [P, D])]
        qtmp = sb("qtmp", [P, 512])
        sqz = [sb("sqz0", [P, 512], BF16), sb("sqz1", [P, 512], BF16),
               sb("sqz2", [P, 512], BF16)]
        y_scr = [sb("y_scr0", [P, 256]), sb("y_scr1", [P, 256])]
        mxq = sb("mxq", [P, 512])
        csbc = sb("csbc", [P, 512])
        csbc2 = sb("csbc2", [P, 512])
        invbc = sb("invbc", [P, 512])
        rows = sb("rows", [1, 1024])
        stat = sb("stat", [P, 32])
        x_halo = sb("x_halo", [2, D])
        xq_halo = sb("xq_halo", [2, D], BF16)
        stat_h = sb("stat_h", [2, 8])

        def bslot(tc, which):
            c0 = tc * 2048 + which * 1024
            names = [f"yf{tc * 4 + 2 * which + q}_{hf}"
                     for q in (0, 1) for hf in (0, 1)]
            return y_full[:, c0:c0 + 1024], names

        def mm_group(fns, bank, reads):
            ev = None
            for i, fn in enumerate(fns):
                if i == 0:
                    ev = p.op("tensor", fn, reads=reads, writes=[bank],
                              track=False)
                else:
                    ev = p.op("tensor", fn, track=False)
            p.note_read(reads, ev)
            p.note_write([bank], ev)
            return ev

        def sy_slot(t, i):
            return (t * 64 + i) % 6

        def gp_slot(t, i):
            return (t * 32 + i) % 5

        def emit_B_ln(t, xq="gpsimd"):
            g0 = t * T
            p.op(xq, lambda h, g0=g0: h.dma_start(
                out=x_halo[:],
                in_=bass.AP(xs.tensor, g0 * D, [[513 * D, 2], [1, D]])),
                writes=["x_halo"], n=16, sem="XH")
            for tc in range(4):
                sl = tc % 2
                xnb, xnn = bslot(tc, 0)
                xqb, xqn = bslot(tc, 1)
                p.op(xq, lambda h, tc=tc, sl=sl, g0=g0: h.dma_start(
                    out=xln[sl][:],
                    in_=xs[1 + g0 + tc * P: 1 + g0 + (tc + 1) * P, :]),
                    writes=[f"xln{sl}"], n=16, sem=f"XL{sl}")
                st = stat[:, tc * 8:tc * 8 + 8]
                sn = [f"st{tc}_{i}" for i in range(8)]
                p.op("vector", lambda h, sl=sl, st=st: h.reduce_sum(
                    st[:, 0:1], xln[sl][:], AX),
                    reads=[f"xln{sl}"], writes=[sn[0]])
                p.op("vector", lambda h, st=st: h.tensor_scalar(
                    st[:, 0:1], st[:, 0:1], 1.0 / D, None, op0=ALU.mult),
                    reads=[sn[0]], writes=[sn[0]])
                p.op("vector", lambda h, sl=sl, st=st: h.tensor_scalar(
                    xln[sl][:], xln[sl][:], st[:, 0:1], None,
                    op0=ALU.subtract),
                    reads=[sn[0]], writes=[f"xln{sl}"])
                p.op("scalar", lambda h, sl=sl, xqb=xqb, st=st: h.activation(
                    xqb[:], xln[sl][:], AF.Square, accum_out=st[:, 1:2]),
                    reads=[f"xln{sl}"], writes=xqn + [sn[1]])
                p.op("vector", lambda h, st=st: h.tensor_scalar(
                    st[:, 1:2], st[:, 1:2], 1.0 / D, EPS,
                    op0=ALU.mult, op1=ALU.add),
                    reads=[sn[1]], writes=[sn[1]])
                p.op("scalar", lambda h, st=st: h.activation(
                    st[:, 1:2], st[:, 1:2], AF.Sqrt),
                    reads=[sn[1]], writes=[sn[1]])
                p.op("vector", lambda h, st=st: h.reciprocal(
                    st[:, 2:3], st[:, 1:2]), reads=[sn[1]], writes=[sn[2]])
                p.op("vector", lambda h, sl=sl, st=st: h.reduce_max(
                    st[:, 3:4], xln[sl][:], AX, apply_absolute_value=True),
                    reads=[f"xln{sl}"], writes=[sn[3]])
                p.op("vector", lambda h, st=st: h.tensor_tensor(
                    st[:, 3:4], st[:, 3:4], st[:, 2:3], op=ALU.mult),
                    reads=[sn[3], sn[2]], writes=[sn[3]])
                p.op("vector", lambda h, st=st: h.tensor_scalar(
                    st[:, 3:4], st[:, 3:4], 1e-5, None, op0=ALU.max),
                    reads=[sn[3]], writes=[sn[3]])
                p.op("vector", lambda h, st=st: h.reciprocal(
                    st[:, 4:5], st[:, 3:4]), reads=[sn[3]], writes=[sn[4]])
                p.op("vector", lambda h, st=st: h.tensor_scalar(
                    st[:, 4:5], st[:, 4:5], 127.0, None, op0=ALU.mult),
                    reads=[sn[4]], writes=[sn[4]])
                p.op("vector", lambda h, st=st: h.tensor_tensor(
                    st[:, 5:6], st[:, 2:3], st[:, 4:5], op=ALU.mult),
                    reads=[sn[2], sn[4]], writes=[sn[5]])
                p.op("vector", lambda h, st=st: h.tensor_tensor(
                    st[:, 6:7], st[:, 3:4], c1col[:], op=ALU.mult),
                    reads=[sn[3], "c1"], writes=[sn[6]])
                p.op("vector", lambda h, sl=sl, xnb=xnb, st=st: h.tensor_scalar(
                    xnb[:], xln[sl][:], st[:, 2:3], None, op0=ALU.mult),
                    reads=[f"xln{sl}", sn[2]], writes=xnn)
                for hh in (0, 1):
                    p.op("vector", lambda h, sl=sl, st=st, hh=hh: h.tensor_scalar(
                        qtmp[:], xln[sl][:, hh * 512:(hh + 1) * 512],
                        st[:, 5:6], MAGIC, op0=ALU.mult, op1=ALU.add),
                        reads=[f"xln{sl}", sn[5]], writes=["qtmp"])
                    p.op("vector", lambda h: h.tensor_scalar(
                        qtmp[:], qtmp[:], MAGIC, 127.0,
                        op0=ALU.subtract, op1=ALU.min),
                        reads=["qtmp"], writes=["qtmp"])
                    p.op("vector", lambda h, xqb=xqb, st=st, hh=hh: h.tensor_scalar(
                        xqb[:, hh * 512:(hh + 1) * 512], qtmp[:], -128.0,
                        st[:, 6:7], op0=ALU.max, op1=ALU.mult),
                        reads=["qtmp", sn[6]], writes=xqn)

            p.op("vector", lambda h: h.reduce_sum(
                stat_h[:, 0:1], x_halo[:], AX), reads=["x_halo"], writes=["sh0"])
            p.op("vector", lambda h: h.tensor_scalar(
                stat_h[:, 0:1], stat_h[:, 0:1], 1.0 / D, None, op0=ALU.mult),
                reads=["sh0"], writes=["sh0"])
            p.op("vector", lambda h: h.tensor_scalar(
                x_halo[:], x_halo[:], stat_h[:, 0:1], None, op0=ALU.subtract),
                reads=["sh0"], writes=["x_halo"])
            p.op("scalar", lambda h: h.activation(
                xq_halo[:], x_halo[:], AF.Square, accum_out=stat_h[:, 1:2]),
                reads=["x_halo"], writes=["xq_halo", "sh1"])
            p.op("vector", lambda h: h.tensor_scalar(
                stat_h[:, 1:2], stat_h[:, 1:2], 1.0 / D, EPS,
                op0=ALU.mult, op1=ALU.add), reads=["sh1"], writes=["sh1"])
            p.op("scalar", lambda h: h.activation(
                stat_h[:, 1:2], stat_h[:, 1:2], AF.Sqrt),
                reads=["sh1"], writes=["sh1"])
            p.op("vector", lambda h: h.reciprocal(
                stat_h[:, 2:3], stat_h[:, 1:2]), reads=["sh1"], writes=["sh2"])
            p.op("vector", lambda h: h.reduce_max(
                stat_h[:, 3:4], x_halo[:], AX, apply_absolute_value=True),
                reads=["x_halo"], writes=["sh3"])
            p.op("vector", lambda h: h.tensor_tensor(
                stat_h[:, 3:4], stat_h[:, 3:4], stat_h[:, 2:3], op=ALU.mult),
                reads=["sh3", "sh2"], writes=["sh3"])
            p.op("vector", lambda h: h.tensor_scalar(
                stat_h[:, 3:4], stat_h[:, 3:4], 1e-5, None, op0=ALU.max),
                reads=["sh3"], writes=["sh3"])
            p.op("vector", lambda h: h.reciprocal(
                stat_h[:, 4:5], stat_h[:, 3:4]), reads=["sh3"], writes=["sh4"])
            p.op("vector", lambda h: h.tensor_scalar(
                stat_h[:, 4:5], stat_h[:, 4:5], 127.0, None, op0=ALU.mult),
                reads=["sh4"], writes=["sh4"])
            p.op("vector", lambda h: h.tensor_tensor(
                stat_h[:, 5:6], stat_h[:, 2:3], stat_h[:, 4:5], op=ALU.mult),
                reads=["sh2", "sh4"], writes=["sh5"])
            p.op("vector", lambda h: h.tensor_tensor(
                stat_h[:, 6:7], stat_h[:, 3:4], c1col[0:2, :], op=ALU.mult),
                reads=["sh3", "c1"], writes=["sh6"])
            p.op("vector", lambda h: h.tensor_scalar(
                x_halo[:], x_halo[:], stat_h[:, 5:6], MAGIC,
                op0=ALU.mult, op1=ALU.add),
                reads=["sh5"], writes=["x_halo"])
            p.op("vector", lambda h: h.tensor_scalar(
                x_halo[:], x_halo[:], MAGIC, 127.0,
                op0=ALU.subtract, op1=ALU.min),
                reads=["x_halo"], writes=["x_halo"])
            p.op("vector", lambda h: h.tensor_scalar(
                xq_halo[:], x_halo[:], -128.0, stat_h[:, 6:7],
                op0=ALU.max, op1=ALU.mult),
                reads=["x_halo", "sh6"], writes=["xq_halo"])

        def emit_B_tr(t, bc0):
            bcnt = bc0
            for tc in range(4):
                for which in (0, 1):
                    src, srcn = bslot(tc, which)
                    dst = xnT if which == 0 else xqT
                    dname = "xnT" if which == 0 else "xqT"
                    off = 0 if which == 0 else 1
                    w = 512 if which == 0 else 514
                    for hg in (0, 1):
                        bk = pb[2 + (bcnt % 2)]
                        bkn = PBN[2 + (bcnt % 2)]
                        bcnt += 1
                        ev = None
                        for dq in range(4):
                            dc = hg * 4 + dq
                            fn = (lambda h, src=src, dc=dc, dq=dq, bk=bk:
                                  h.matmul(bk[:, dq * P:(dq + 1) * P],
                                           src[:, dc * P:(dc + 1) * P],
                                           identb[:], start=True, stop=True))
                            if dq == 0:
                                ev = p.op("tensor", fn,
                                          reads=srcn + ["identb"],
                                          writes=[bkn], track=False)
                            else:
                                ev = p.op("tensor", fn, track=False)
                        p.note_read(srcn + ["identb"], ev)
                        p.note_write([bkn], ev)
                        c0 = hg * 4 * w + off + tc * P
                        rl = 8 * w
                        p.op("vector",
                             lambda h, dst=dst, c0=c0, w=w, bk=bk, rl=rl:
                             h.tensor_copy(
                                 bass.AP(dst, c0, [[rl, P], [w, 4], [1, P]]),
                                 bass.AP(bk, 0, [[512, P], [P, 4], [1, P]])),
                             reads=[bkn], writes=[dname])
            ev = None
            for dc in range(8):
                fn = (lambda h, dc=dc: h.matmul(
                    pb[2][:, dc * 2:dc * 2 + 2],
                    xq_halo[:, dc * P:(dc + 1) * P], identb[0:2, 0:2],
                    start=True, stop=True))
                if dc == 0:
                    ev = p.op("tensor", fn, reads=["xq_halo", "identb"],
                              writes=[PBN[2]], track=False)
                else:
                    ev = p.op("tensor", fn, track=False)
            p.note_read(["xq_halo", "identb"], ev)
            p.note_write([PBN[2]], ev)
            for dc in range(8):
                p.op("vector", lambda h, dc=dc: h.tensor_copy(
                    xqT[:, dc * 514:dc * 514 + 1], pb[2][:, dc * 2:dc * 2 + 1]),
                    reads=[PBN[2]], writes=["xqT"])
                p.op("vector", lambda h, dc=dc: h.tensor_copy(
                    xqT[:, dc * 514 + 513:dc * 514 + 514],
                    pb[2][:, dc * 2 + 1:dc * 2 + 2]),
                    reads=[PBN[2]], writes=["xqT"])

        def emit_main(t):
            g0 = t * T

            for ncx in range(64):
                sl = ncx // 2
                if ncx % 2 == 0:
                    ss = sy_slot(t, sl)
                    p.op("sync", lambda h, sl=sl, ss=ss: h.dma_start(
                        out=wsy[ss][:], in_=kt_s[sl]),
                        reads=[f"kt_s{sl}"], writes=[f"wsy{ss}"], n=16,
                        sem=f"SY{ss}")
                ss = sy_slot(t, ncx // 2)
                bank = PBN[ncx % 4]
                fns = []
                for dc in range(8):
                    fns.append(lambda h, ss=ss, ncx=ncx, dc=dc:
                               h.matmul(pb[ncx % 4][:],
                                        wsy[ss][
                                            :, dc * 256 + (ncx % 2) * P:
                                            dc * 256 + (ncx % 2 + 1) * P],
                                        xnT[:, dc * 512:(dc + 1) * 512],
                                        start=(dc == 0), stop=(dc == 7)))
                mm_group(fns, bank, [f"wsy{ss}", "xnT"])
                p.op("vector", lambda h, ncx=ncx: h.tensor_copy(
                    scT[:, ncx * 512:(ncx + 1) * 512], pb[ncx % 4][:]),
                    reads=[bank], writes=[f"sc{ncx}"])
                p.op("scalar", lambda h, ncx=ncx: h.activation(
                    sqz[ncx % 3][:], scT[:, ncx * 512:(ncx + 1) * 512],
                    AF.Square),
                    reads=[f"sc{ncx}"], writes=[f"sqz{ncx % 3}"])
                if ncx >= 2:
                    nl = ncx - 2
                    p.op("tensor", lambda h, nl=nl: h.matmul(
                        pb[4][:], ones_sq[:], sqz[nl % 3][:],
                        start=(nl == 0), stop=False),
                        reads=[f"sqz{nl % 3}", "onessq"], writes=[PBN[4]])
            for nl in (62, 63):
                p.op("tensor", lambda h, nl=nl: h.matmul(
                    pb[4][:], ones_sq[:], sqz[nl % 3][:],
                    start=False, stop=(nl == 63)),
                    reads=[f"sqz{nl % 3}", "onessq"], writes=[PBN[4]])

            p.op("vector", lambda h: h.tensor_copy(
                rows[0:1, 0:512], pb[4][0:1, :]),
                reads=[PBN[4]], writes=["rows"])
            p.op("scalar", lambda h: h.activation(
                rows[0:1, 0:512], rows[0:1, 0:512], AF.Sqrt),
                reads=["rows"], writes=["rows"])
            p.op("vector", lambda h: h.reciprocal(
                rows[0:1, 0:512], rows[0:1, 0:512]),
                reads=["rows"], writes=["rows"])
            p.op("vector", lambda h: h.tensor_scalar(
                rows[0:1, 0:512], rows[0:1, 0:512], SQRT_N, None,
                op0=ALU.mult), reads=["rows"], writes=["rows"])

            def minmax(jq):
                if jq == 0:
                    p.op("vector", lambda h: h.tensor_copy(
                        mxq[:], y_full[:, 0:512]),
                        reads=["yf0_0", "yf0_1"], writes=["mxq0", "mxq1"])
                    p.op("vector", lambda h: h.tensor_copy(
                        pb[7][:], y_full[:, 0:512]),
                        reads=["yf0_0", "yf0_1"], writes=[PBN[7]])
                else:
                    p.op("vector", lambda h, jq=jq: h.tensor_tensor(
                        mxq[:], mxq[:],
                        y_full[:, jq * 512:(jq + 1) * 512], op=ALU.max),
                        reads=[f"yf{jq}_0", f"yf{jq}_1", "mxq0", "mxq1"],
                        writes=["mxq0", "mxq1"])
                    p.op("vector", lambda h, jq=jq: h.tensor_tensor(
                        pb[7][:], pb[7][:],
                        y_full[:, jq * 512:(jq + 1) * 512], op=ALU.min),
                        reads=[f"yf{jq}_0", f"yf{jq}_1", PBN[7]],
                        writes=[PBN[7]])

            for jc in range(32):
                sl = jc // 2
                if jc == 1:
                    p.op("tensor", lambda h: h.matmul(
                        pb[5][:], ones_row[:], rows[0:1, 0:512],
                        start=True, stop=True),
                        reads=["rows", "onesr"], writes=[PBN[5]])
                    p.op("vector", lambda h: h.tensor_copy(csbc[:], pb[5][:]),
                         reads=[PBN[5]], writes=["csbc"])
                if jc % 2 == 0:
                    ws = sy_slot(t, 32 + sl)
                    p.op("sync", lambda h, sl=sl, ws=ws: h.dma_start(
                        out=wsy[ws][:], in_=w1t_s[sl]),
                        reads=[f"w1t_s{sl}"], writes=[f"wsy{ws}"], n=16,
                        sem=f"SY{ws}")
                ws = sy_slot(t, 32 + jc // 2)
                for hf in range(2):
                    ph = (2 * jc + hf) % 6
                    bank = PBN[ph]
                    fns = []
                    for dc in range(8):
                        fns.append(lambda h, ws=ws, jc=jc, dc=dc, hf=hf, ph=ph:
                                   h.matmul(pb[ph][:, 0:258],
                                            wsy[ws][
                                                :, (jc % 2) * 1024 + dc * P:
                                                (jc % 2) * 1024 + (dc + 1) * P],
                                            xqT[:, dc * 514 + hf * 256:
                                                dc * 514 + hf * 256 + 258],
                                            start=(dc == 0), stop=(dc == 7)))
                    mm_group(fns, bank, [f"wsy{ws}", "xqT"])
                    p.op("vector", lambda h, ph=ph, jc=jc: h.tensor_scalar(
                        y_scr[ph % 2][:], pb[ph][:, 1:257],
                        taps[:, jc * 3 + 1:jc * 3 + 2], None, op0=ALU.mult),
                        reads=[bank, "taps"], writes=[f"ysc{ph % 2}"])
                    p.op("vector", lambda h, ph=ph, jc=jc: h.scalar_tensor_tensor(
                        y_scr[ph % 2][:], pb[ph][:, 0:256],
                        taps[:, jc * 3:jc * 3 + 1], y_scr[ph % 2][:],
                        op0=ALU.mult, op1=ALU.add),
                        reads=[bank, "taps"], writes=[f"ysc{ph % 2}"])
                    p.op("vector", lambda h, ph=ph, jc=jc: h.scalar_tensor_tensor(
                        y_scr[ph % 2][:], pb[ph][:, 2:258],
                        taps[:, jc * 3 + 2:jc * 3 + 3], y_scr[ph % 2][:],
                        op0=ALU.mult, op1=ALU.add),
                        reads=[bank, "taps"], writes=[f"ysc{ph % 2}"])
                    p.op("scalar", lambda h, ph=ph, jc=jc, hf=hf: h.activation(
                        y_full[:, jc * 512 + hf * 256:jc * 512 + hf * 256 + 256],
                        y_scr[ph % 2][:], AF.Silu, bias=bias_c[:, jc:jc + 1]),
                        reads=[f"ysc{ph % 2}", "bias"], writes=[f"yf{jc}_{hf}"])
                if jc >= 1:
                    minmax(jc - 1)
                if 27 <= jc <= 31:
                    sv = jc - 27
                    gs = gp_slot(t, sv)
                    p.op("gpsimd", lambda h, sv=sv, gs=gs: h.dma_start(
                        out=bass.AP(wgp[gs], 0,
                                    [[2048, P], [1024, 2], [1, 1024]]),
                        in_=bass.AP(v_s.tensor, sv * 256 * D,
                                    [[D, P], [P * D, 2], [1, D]])),
                        reads=[f"v_s{sv // 4}"], writes=[f"wgp{gs}"], n=16,
                        sem=f"GP{gs}")
            minmax(31)
            p.op("vector", lambda h: h.tensor_scalar(
                pb[7][:], pb[7][:], -1.0, None, op0=ALU.mult),
                reads=[PBN[7]], writes=[PBN[7]])
            p.op("vector", lambda h: h.tensor_tensor(
                mxq[:], mxq[:], pb[7][:], op=ALU.max),
                reads=[PBN[7], "mxq0", "mxq1"],
                writes=["mxq0", "mxq1"])

            def hq_mxqT():
                ev = None
                for tc in range(4):
                    fn = (lambda h, tc=tc: h.transpose(
                        pb[6][:, tc * P:(tc + 1) * P],
                        mxq[:, tc * P:(tc + 1) * P], ident[:]))
                    if tc == 0:
                        ev = p.op("tensor", fn,
                                  reads=["mxq0", "mxq1", "ident"],
                                  writes=[PBN[6]], track=False)
                    else:
                        ev = p.op("tensor", fn, track=False)
                p.note_read(["mxq0", "mxq1", "ident"], ev)
                p.note_write([PBN[6]], ev)
                for tc in range(4):
                    st = stat[:, tc * 8:tc * 8 + 8]
                    p.op("vector", lambda h, tc=tc, st=st: h.reduce_max(
                        st[:, 7:8], pb[6][:, tc * P:(tc + 1) * P], AX),
                        reads=[PBN[6]], writes=[f"st{tc}_7"])

            def hq_statT():
                ev = None
                for tc in range(4):
                    ev = p.op("tensor", lambda h, tc=tc: h.transpose(
                        pb[7][0:1, tc * P:(tc + 1) * P],
                        stat[:, tc * 8 + 7:tc * 8 + 8], ident[:]),
                        reads=[f"st{tc}_7", "ident"],
                        writes=[PBN[7]] if tc == 0 else (), track=(tc == 0))
                p.note_write([PBN[7]], ev)
                p.note_read([f"st{tc}_7" for tc in range(4)], ev)
                p.op("vector", lambda h: h.tensor_scalar(
                    rows[0:1, 512:1024], pb[7][0:1, 0:512], 1e-5, None,
                    op0=ALU.max),
                    reads=[PBN[7]], writes=["rowsi"])
                p.op("vector", lambda h: h.reciprocal(
                    rows[0:1, 0:512], rows[0:1, 512:1024]),
                    reads=["rowsi"], writes=["rows"])
                p.op("vector", lambda h: h.tensor_scalar(
                    rows[0:1, 0:512], rows[0:1, 0:512], 127.0, None,
                    op0=ALU.mult),
                    reads=["rows"], writes=["rows"])
                p.op("vector", lambda h: h.tensor_scalar(
                    rows[0:1, 512:1024], rows[0:1, 512:1024], c2col[0:1, 0:1],
                    None, op0=ALU.mult),
                    reads=["rowsi", "c2"], writes=["rowsi"])

            def hq_bc1():
                p.op("tensor", lambda h: h.matmul(
                    pb[6][:], ones_row[:], rows[0:1, 0:512],
                    start=True, stop=True), reads=["rows", "onesr"],
                    writes=[PBN[6]])
                p.op("vector", lambda h: h.tensor_copy(csbc2[:], pb[6][:]),
                     reads=[PBN[6]], writes=["csbc2"])

            def hq_bc2():
                p.op("tensor", lambda h: h.matmul(
                    pb[7][:], ones_row[:], rows[0:1, 512:1024],
                    start=True, stop=True), reads=["rowsi", "onesr"],
                    writes=[PBN[7]])
                p.op("vector", lambda h: h.tensor_copy(
                    invbc[:, 0:512], pb[7][:]), reads=[PBN[7]],
                    writes=["invbc"])

            def emit_F(f):
                p.op("vector", lambda h, f=f: h.tensor_tensor(
                    sqz[f % 3][:], scT[:, f * 512:(f + 1) * 512],
                    csbc[:], op=ALU.mult),
                    reads=[f"sc{f}", "csbc"], writes=[f"sqz{f % 3}"])
                p.op("scalar", lambda h, f=f: h.activation(
                    scT[:, f * 512:(f + 1) * 512], sqz[f % 3][:],
                    AF.Gelu),
                    reads=[f"sqz{f % 3}"], writes=[f"sc{f}"])

            emit_F(0)
            emit_F(1)
            hq_steps = {1: hq_mxqT, 3: hq_statT, 5: hq_bc1, 7: hq_bc2}
            for ncx in range(8):
                for f in (2 * ncx + 2, 2 * ncx + 3):
                    if f < 64:
                        emit_F(f)
                sl = ncx // 2
                if ncx % 2 == 0 and sl >= 5:
                    gs = gp_slot(t, sl)
                    p.op("gpsimd", lambda h, sl=sl, gs=gs: h.dma_start(
                        out=bass.AP(wgp[gs], 0,
                                    [[2048, P], [1024, 2], [1, 1024]]),
                        in_=bass.AP(v_s.tensor, sl * 256 * D,
                                    [[D, P], [P * D, 2], [1, D]])),
                        reads=[f"v_s{sl // 4}"], writes=[f"wgp{gs}"], n=16,
                        sem=f"GP{gs}")
                gs = gp_slot(t, ncx // 2)
                ev = None
                for ol in range(6):
                    fn = (lambda h, gs=gs, ncx=ncx, ol=ol:
                          h.matmul(pb[ol][:],
                                   wgp[gs][
                                       :, (ncx % 2) * 1024 + ol * P:
                                       (ncx % 2) * 1024 + (ol + 1) * P],
                                   scT[:, ncx * 512:(ncx + 1) * 512],
                                   start=(ncx == 0), stop=False))
                    rd = [f"wgp{gs}", f"sc{ncx}"] if ol == 0 else ()
                    wr = [PBN[ol]] if ncx == 0 else ()
                    ev = p.op("tensor", fn, reads=rd, writes=wr,
                              track=False)
                p.note_read([f"wgp{gs}", f"sc{ncx}"], ev)
                if ncx in hq_steps:
                    hq_steps[ncx]()

            for ncx in range(8):
                gs = gp_slot(t, ncx // 2)
                ev = None
                for ol in (6, 7):
                    fn = (lambda h, gs=gs, ncx=ncx, ol=ol:
                          h.matmul(pb[ol][:],
                                   wgp[gs][
                                       :, (ncx % 2) * 1024 + ol * P:
                                       (ncx % 2) * 1024 + (ol + 1) * P],
                                   scT[:, ncx * 512:(ncx + 1) * 512],
                                   start=(ncx == 0), stop=False))
                    rd = [f"wgp{gs}", f"sc{ncx}"] if ol == 6 else ()
                    wr = [PBN[ol]] if ncx == 0 else ()
                    ev = p.op("tensor", fn, reads=rd, writes=wr,
                              track=False)
                p.note_read([f"wgp{gs}", f"sc{ncx}"], ev)

            last_ev = [None] * 8
            def emit_quant(jq):
                p.op("vector", lambda h, jq=jq: h.tensor_tensor(
                    mxq[:], y_full[:, jq * 512:(jq + 1) * 512], csbc2[:],
                    op=ALU.mult),
                    reads=[f"yf{jq}_0", f"yf{jq}_1", "csbc2"],
                    writes=["mxq0", "mxq1"])
                p.op("vector", lambda h: h.tensor_scalar(
                    mxq[:], mxq[:], 127.0, -128.0, op0=ALU.min,
                    op1=ALU.max),
                    reads=["mxq0", "mxq1"], writes=["mxq0", "mxq1"])
                p.op("vector", lambda h: h.tensor_scalar(
                    mxq[:], mxq[:], MAGIC, MAGIC, op0=ALU.add,
                    op1=ALU.subtract),
                    reads=["mxq0", "mxq1"], writes=["mxq0", "mxq1"])
                p.op("vector", lambda h, jq=jq: h.tensor_tensor(
                    y_full[:, jq * 512:(jq + 1) * 512], mxq[:],
                    invbc[:, 0:512], op=ALU.mult),
                    reads=["mxq0", "mxq1", "invbc"],
                    writes=[f"yf{jq}_0", f"yf{jq}_1"])

            for ncx in range(8, 64):
                for f in (2 * ncx + 2, 2 * ncx + 3):
                    if f < 64:
                        emit_F(f)
                if 8 <= ncx < 40:
                    emit_quant(ncx - 8)
                sl = ncx // 2
                if ncx % 2 == 0 and sl >= 5:
                    gs = gp_slot(t, sl)
                    p.op("gpsimd", lambda h, sl=sl, gs=gs: h.dma_start(
                        out=bass.AP(wgp[gs], 0,
                                    [[2048, P], [1024, 2], [1, 1024]]),
                        in_=bass.AP(v_s.tensor, sl * 256 * D,
                                    [[D, P], [P * D, 2], [1, D]])),
                        reads=[f"v_s{sl // 4}"], writes=[f"wgp{gs}"], n=16,
                        sem=f"GP{gs}")
                gs = gp_slot(t, ncx // 2)
                ev = None
                for ol in range(8):
                    fn = (lambda h, gs=gs, ncx=ncx, ol=ol:
                          h.matmul(pb[ol][:],
                                   wgp[gs][
                                       :, (ncx % 2) * 1024 + ol * P:
                                       (ncx % 2) * 1024 + (ol + 1) * P],
                                   scT[:, ncx * 512:(ncx + 1) * 512],
                                   start=False, stop=False))
                    rd = [f"wgp{gs}", f"sc{ncx}"] if ol == 0 else ()
                    ev = p.op("tensor", fn, reads=rd, track=False)
                p.note_read([f"wgp{gs}", f"sc{ncx}"], ev)
            for jc in range(32):
                sl2 = jc // 2
                if jc % 2 == 0:
                    ss = sy_slot(t, 48 + sl2)
                    p.op("sync", lambda h, sl2=sl2, ss=ss: h.dma_start(
                        out=wsy[ss][:], in_=w2t_s[sl2]),
                        reads=[f"w2t_s{sl2}"], writes=[f"wsy{ss}"], n=16,
                        sem=f"SY{ss}")
                ss = sy_slot(t, 48 + jc // 2)
                ev = None
                for ol in range(8):
                    fn = (lambda h, ss=ss, jc=jc, ol=ol:
                          h.matmul(pb[ol][:],
                                   wsy[ss][
                                       :, (jc % 2) * 1024 + ol * P:
                                       (jc % 2) * 1024 + (ol + 1) * P],
                                   y_full[:, jc * 512:(jc + 1) * 512],
                                   start=False, stop=(jc == 31)))
                    ev = p.op("tensor", fn,
                              reads=[f"wsy{ss}", f"yf{jc}_0", f"yf{jc}_1"]
                              if ol == 0 else (), track=False)
                    if jc == 31:
                        last_ev[ol] = ev
                p.note_read([f"wsy{ss}", f"yf{jc}_0", f"yf{jc}_1"], ev)
            for ol in range(8):
                p.note_write([PBN[ol]], last_ev[ol])

        def emit_evac(t):
            g0 = t * T
            xa = [xadd[0], xadd[1], xadd[0], xadd[1]]
            xan = ["xadd0", "xadd1", "xadd0", "xadd1"]
            for tc in range(2):
                p.op("gpsimd", lambda h, tc=tc, g0=g0: h.dma_start(
                    out=xa[tc][:],
                    in_=xs[1 + g0 + tc * P: 1 + g0 + (tc + 1) * P, :]),
                    writes=[xan[tc]], n=16, sem=f"XA{tc % 2}")
            for ol in range(8):
                stg = mxq if ol % 2 == 0 else csbc
                stgn = ["mxq0", "mxq1"] if ol % 2 == 0 else ["csbc"]
                p.op("vector", lambda h, ol=ol, stg=stg: h.tensor_copy(
                    stg[:], pb[ol][:]),
                    reads=[PBN[ol]], writes=stgn)
                ev = None
                for tc in range(4):
                    fn = (lambda h, ol=ol, tc=tc, stg=stg: h.transpose(
                        pb[ol][:, tc * P:(tc + 1) * P],
                        stg[:, tc * P:(tc + 1) * P], ident[:]))
                    if tc == 0:
                        ev = p.op("tensor", fn, reads=stgn + ["ident"],
                                  writes=[PBN[ol]], track=False)
                    else:
                        ev = p.op("tensor", fn, track=False)
                p.note_read(stgn + ["ident"], ev)
                p.note_write([PBN[ol]], ev)
            for tc in range(4):
                for ol in range(8):
                    p.op("vector",
                         lambda h, tc=tc, ol=ol: h.scalar_tensor_tensor(
                             qtc[tc % 2][:, ol * P:(ol + 1) * P],
                             pb[ol][:, tc * P:(tc + 1) * P], 1.0,
                             xa[tc][:, ol * P:(ol + 1) * P],
                             op0=ALU.mult, op1=ALU.add),
                         reads=[PBN[ol], xan[tc]],
                         writes=[f"qtc{tc % 2}"])
                p.op("scalar", lambda h, tc=tc, g0=g0: h.dma_start(
                    out=outd[g0 + tc * P:g0 + (tc + 1) * P, :],
                    in_=qtc[tc % 2][:]),
                    reads=[f"qtc{tc % 2}"], writes=[f"o{t}_{tc}"],
                    n=16, sem=f"VO{tc % 2}")
                if tc < 2:
                    p.op("gpsimd", lambda h, tc=tc, g0=g0: h.dma_start(
                        out=xa[tc + 2][:],
                        in_=xs[1 + g0 + (tc + 2) * P: 1 + g0 + (tc + 3) * P, :]),
                        writes=[xan[tc + 2]], n=16, sem=f"XA{tc % 2}")

        emit_B_ln(0, xq="sync")
        emit_B_tr(0, 0)
        for t in range(n_tiles):
            emit_main(t)
            if t + 1 < n_tiles:
                emit_B_ln(t + 1)
            emit_evac(t)
            if t + 1 < n_tiles:
                emit_B_tr(t + 1, (t + 1) % 2)

        sync_sems = [f"SY{i}" for i in range(6)] + ["KC0", "KC1",
                     "CS0", "CS1", "CS2", "KT0", "KT1"]
        gp_sems = ([f"GP{i}" for i in range(5)]
                   + ["XL0", "XL1", "XH", "XA0", "XA1",
                      "WD0", "WD1", "W2S", "VC0", "VC1"])
        sc_sems = ["WC0", "WC1", "W1S", "VO0", "VO1"]
        p.wait("sync", [(s, p.cnt.get(s, 0)) for s in sync_sems])
        p.wait("gpsimd", [(s, p.cnt.get(s, 0)) for s in gp_sems])
        p.wait("scalar", [(s, p.cnt.get(s, 0)) for s in sc_sems])

        block = S.enter_context(nc.Block())
        p.emit(block, semh)

    return nc


def kernel(L0_attn_q, L0_attn_k, L0_attn_v, L0_ffn_w1, L0_ffn_conv_w,
           L0_ffn_conv_b, L0_ffn_w2):
    x = np.ascontiguousarray(np.asarray(L0_attn_q, dtype=np.float32))
    k = np.ascontiguousarray(np.asarray(L0_attn_k, dtype=np.float32))
    v = np.ascontiguousarray(np.asarray(L0_attn_v, dtype=np.float32))
    w1 = np.ascontiguousarray(np.asarray(L0_ffn_w1, dtype=np.float32))
    cw = np.ascontiguousarray(
        np.asarray(L0_ffn_conv_w, dtype=np.float32).reshape(HID, 3))
    cb = np.ascontiguousarray(np.asarray(L0_ffn_conv_b, dtype=np.float32))
    w2 = np.ascontiguousarray(np.asarray(L0_ffn_w2, dtype=np.float32))
    ident = np.eye(P, dtype=np.float32)

    zrow = np.zeros((1, D), np.float32)
    in_maps = []
    for c in range(8):
        b, hh = c // 2, c % 2
        tok0 = hh * TOK
        left = x[b, tok0 - 1:tok0] if hh == 1 else zrow
        right = x[b, tok0 + TOK:tok0 + TOK + 1] if hh == 0 else zrow
        xsl = np.concatenate([left, x[b, tok0:tok0 + TOK], right], axis=0)
        in_maps.append(dict(xs=np.ascontiguousarray(xsl), kk=k, vv=v, w1=w1,
                            w2=w2, cw=cw, cb=cb, ident=ident))

    nc = build_nc()
    res = run_bass_kernel_spmd(nc, in_maps, core_ids=list(range(8)))
    outp = np.empty((4, 8192, D), np.float32)
    for c in range(8):
        b, hh = c // 2, c % 2
        outp[b, hh * TOK:(hh + 1) * TOK] = np.asarray(res.results[c]["out"])
    return outp


# version 14
# speedup vs baseline: 1.1861x; 1.0610x over previous
import numpy as np
import concourse.bass as bass
import concourse.mybir as mybir
from concourse.bass_utils import run_bass_kernel_spmd
from contextlib import ExitStack

F32 = mybir.dt.float32
BF16 = mybir.dt.bfloat16
AF = mybir.ActivationFunctionType
ALU = mybir.AluOpType
AX = mybir.AxisListType.X

MAGIC = float(np.float32(3 * 2.0 ** 22))
D = 1024
NB = 8192
HID = 4096
TOK = 4096
T = 512
NT = TOK // T
SQRT_N = float(np.float32(np.sqrt(np.float32(NB))))
EPS = 1e-5
P = 128

ENGS = ("sync", "vector", "scalar", "tensor", "gpsimd")
ESEM = {"sync": "SYD", "vector": "VS", "scalar": "SS", "tensor": "TS",
        "gpsimd": "GSD"}


class Prog:

    def __init__(self):
        self.q = {e: [] for e in ENGS}
        self.cnt = {}
        self.wc = {e: {} for e in ENGS}
        self.W = {}
        self.R = {}

    def op(self, eng, fn, reads=(), writes=(), extra=(), sem=None, n=1,
           track=True):
        sem = sem if sem is not None else ESEM[eng]
        waits = []
        for b in reads:
            waits.append(self.W.get(b))
        for b in writes:
            waits.append(self.W.get(b))
            waits.extend(self.R.get(b, ()))
        waits.extend(extra)
        wl = []
        for w in waits:
            if w is None:
                continue
            s, v = w
            if v <= 0:
                continue
            assert v <= self.cnt.get(s, 0), f"wait on future {s}={v}"
            if self.wc[eng].get(s, 0) >= v:
                continue
            self.wc[eng][s] = v
            wl.append((s, v))
        self.cnt[sem] = self.cnt.get(sem, 0) + n
        ev = (sem, self.cnt[sem])
        self.q[eng].append((fn, tuple(wl), sem, n))
        if track:
            for b in reads:
                self.R.setdefault(b, []).append(ev)
            for b in writes:
                self.W[b] = ev
                self.R[b] = []
        return ev

    def wait(self, eng, evs):
        wl = []
        for w in evs:
            if w is None:
                continue
            s, v = w
            assert v <= self.cnt.get(s, 0)
            if self.wc[eng].get(s, 0) >= v:
                continue
            self.wc[eng][s] = v
            wl.append((s, v))
        if wl:
            self.q[eng].append((None, tuple(wl), None, 0))

    def note_read(self, bufs, ev):
        for b in bufs:
            self.R.setdefault(b, []).append(ev)

    def note_write(self, bufs, ev):
        for b in bufs:
            self.W[b] = ev
            self.R[b] = []

    def barrier(self):
        tgt = self.cnt.get("BAR", 0) + len(ENGS)
        for e in ENGS:
            self.op(e, lambda h: h.nop(), sem="BAR", track=False)
        for e in ENGS:
            self.wait(e, [("BAR", tgt)])

    def emit(self, block, semh):
        decos = {"sync": block.sync, "vector": block.vector,
                 "scalar": block.scalar, "tensor": block.tensor,
                 "gpsimd": block.gpsimd}
        for e in ENGS:
            q = self.q[e]

            def body(h, q=q):
                for fn, waits, sem, n in q:
                    for (s, v) in waits:
                        h.wait_ge(semh[s], v)
                    if fn is not None:
                        ins = fn(h)
                        if sem is not None:
                            ins.then_inc(semh[sem], n)
            decos[e](body)


def build_nc(n_tiles=NT):
    nc = bass.Bass("TRN2", target_bir_lowering=False)

    xs = nc.dram_tensor("xs", [TOK + 2, D], F32, kind="ExternalInput").ap()
    kk = nc.dram_tensor("kk", [NB, D], F32, kind="ExternalInput").ap()
    vv = nc.dram_tensor("vv", [NB, D], F32, kind="ExternalInput").ap()
    w1d = nc.dram_tensor("w1", [HID, D], F32, kind="ExternalInput").ap()
    w2d = nc.dram_tensor("w2", [D, HID], F32, kind="ExternalInput").ap()
    cwd = nc.dram_tensor("cw", [HID, 3], F32, kind="ExternalInput").ap()
    cbd = nc.dram_tensor("cb", [HID], F32, kind="ExternalInput").ap()
    id_in = nc.dram_tensor("ident", [P, P], F32, kind="ExternalInput").ap()
    outd = nc.dram_tensor("out", [TOK, D], F32, kind="ExternalOutput").ap()

    kt_s = nc.dram_tensor("kt_s", [32, P, 2048], BF16, kind="Internal").ap()
    v_s = nc.dram_tensor("v_s", [NB, D], BF16, kind="Internal").ap()
    w1t_s = nc.dram_tensor("w1t_s", [16, P, 2048], BF16, kind="Internal").ap()
    w2t_s = nc.dram_tensor("w2t_s", [16, P, 2048], BF16, kind="Internal").ap()

    p = Prog()

    with ExitStack() as S:
        def sb(name, shape, dtype=F32):
            return S.enter_context(nc.sbuf_tensor(name, shape, dtype))

        semh = {}
        sem_names = (["VS", "SS", "TS", "BAR"]
                     + [f"SY{i}" for i in range(6)]
                     + [f"GP{i}" for i in range(5)]
                     + ["XL0", "XL1", "XH", "XA0", "XA1", "VO0", "VO1",
                        "KC0", "KC1", "WC0", "WC1", "WD0", "WD1",
                        "CS0", "CS1", "CS2", "KT0", "KT1", "W1S", "W2S",
                        "VC0", "VC1"])
        for s_name in sem_names:
            semh[s_name] = S.enter_context(nc.semaphore(s_name))

        ident = sb("identc", [P, P])
        identb = sb("identbc", [P, P], BF16)
        ones_colf = sb("ones_colf", [P, 1])
        ones_sq = sb("ones_sq", [P, P], BF16)
        ones_row = sb("ones_row", [1, P])
        taps = sb("tapsc", [P, 96])
        bias_c = sb("bias_c", [P, 32])
        c1col = sb("c1col", [P, 1])
        c2col = sb("c2col", [P, 1])
        sw1col = sb("sw1col", [P, 1])
        sw2col = sb("sw2col", [P, 1])

        p.op("sync", lambda h: h.dma_start(out=ident[:], in_=id_in),
             writes=["ident"], n=16, sem="CS0")
        p.op("sync", lambda h: h.dma_start(
            out=taps[:], in_=bass.AP(cwd.tensor, 0, [[3, P], [384, 32], [1, 3]])),
            writes=["taps"], n=16, sem="CS1")
        p.op("sync", lambda h: h.dma_start(
            out=bias_c[:], in_=bass.AP(cbd.tensor, 0, [[1, P], [P, 32]]),
            allow_slow_non_contiguous=True),
            writes=["bias"], n=16, sem="CS2")
        p.op("vector", lambda h: h.tensor_copy(identb[:], ident[:]),
             reads=["ident"], writes=["identb"])
        p.op("vector", lambda h: h.memset(ones_colf[:], 1.0), writes=["onesf"])
        p.op("vector", lambda h: h.memset(ones_sq[:], 1.0), writes=["onessq"])
        p.op("vector", lambda h: h.memset(ones_row[:], 1.0), writes=["onesr"])

        with ExitStack() as SU:
            def sbu(name, shape, dtype=F32):
                return SU.enter_context(nc.sbuf_tensor(name, shape, dtype))

            def psu(name, shape, dtype=F32):
                return SU.enter_context(nc.psum_tensor(name, shape, dtype))

            kch = [sbu(f"kch{i}", [P, D]) for i in range(4)]
            kbf = [sbu("kbf0", [P, D], BF16), sbu("kbf1", [P, D], BF16)]
            kstage = [sbu("kstage0", [P, 2048], BF16),
                      sbu("kstage1", [P, 2048], BF16)]
            ptk = psu("ptk", [P, 1024], BF16)
            w1ch = [sbu(f"w1ch{i}", [P, D]) for i in range(4)]
            w1q = sbu("w1q", [P, D])
            w1b = [sbu("w1b0", [P, D], BF16), sbu("w1b1", [P, D], BF16)]
            w1stage = [sbu("w1stage0", [P, 2048], BF16),
                       sbu("w1stage1", [P, 2048], BF16)]
            ptw1 = psu("ptw1", [P, 1024], BF16)
            w2ch = [sbu("w2ch0", [P, HID]), sbu("w2ch1", [P, HID])]
            w2q = sbu("w2q", [P, HID])
            w2full = sbu("w2full", [P, 8 * HID], BF16)
            w2stage = [sbu("w2stage0", [P, 2048], BF16),
                       sbu("w2stage1", [P, 2048], BF16)]
            ptw2 = psu("ptw2", [P, 1024], BF16)
            acc1 = sbu("acc1s", [P, 40])
            accw = sbu("accws", [P, 1])
            rowt = sbu("rowts", [1, 2])
            pmean = psu("pmean", [1, 2])
            pbcs = psu("pbcs", [P, 2])

            def mean_chunk(wd, i, width, chname, chbufs, queue, nring,
                           acol=0):
                p.op(queue, lambda h, i=i: h.dma_start(
                    out=chbufs[i % nring][:, 0:width],
                    in_=wd[i * P:(i + 1) * P, :]),
                    writes=[f"{chname}{i % nring}"], n=16,
                    sem=(f"WC{i % 2}" if queue == "scalar"
                         else f"WD{i % 2}"))
                p.op("vector", lambda h, i=i, acol=acol: h.reduce_sum(
                    acc1[:, acol + i:acol + i + 1],
                    chbufs[i % nring][:, 0:width], AX,
                    apply_absolute_value=True),
                    reads=[f"{chname}{i % nring}"], writes=[f"{chname}ac{i}"])

            def mean_final(nchunks, chname, sw_col, c_col, swn, cn, acol=0):
                p.op("vector", lambda h, n=nchunks, acol=acol: h.reduce_sum(
                    accw[:], acc1[:, acol:acol + n], AX),
                    reads=[f"{chname}ac{i}" for i in range(nchunks)],
                    writes=["accw"])
                p.op("tensor", lambda h: h.matmul(
                    pmean[0:1, 0:1], ones_colf[:], accw[:],
                    start=True, stop=True),
                    reads=["accw", "onesf"], writes=["pmean"])
                p.op("vector", lambda h: h.tensor_scalar(
                    rowt[0:1, 0:1], pmean[0:1, 0:1], 1.0 / (HID * D), 1e-5,
                    op0=ALU.mult, op1=ALU.max),
                    reads=["pmean"], writes=["rowt0"])
                p.op("vector", lambda h: h.reciprocal(
                    rowt[0:1, 1:2], rowt[0:1, 0:1]),
                    reads=["rowt0"], writes=["rowt1"])
                p.op("tensor", lambda h: h.matmul(
                    pbcs[:, 0:1], ones_row[:], rowt[0:1, 1:2],
                    start=True, stop=True),
                    reads=["rowt1", "onesr"], writes=["pbcs0"])
                p.op("tensor", lambda h: h.matmul(
                    pbcs[:, 1:2], ones_row[:], rowt[0:1, 0:1],
                    start=True, stop=True),
                    reads=["rowt0", "onesr"], writes=["pbcs1"])
                p.op("vector", lambda h: h.tensor_copy(sw_col[:], pbcs[:, 0:1]),
                     reads=["pbcs0"], writes=[swn])
                p.op("vector", lambda h: h.tensor_scalar(
                    c_col[:], pbcs[:, 1:2], 1.0 / 127.0, None, op0=ALU.mult),
                    reads=["pbcs1"], writes=[cn])

            def k_half(sl, j):
                kcn = sl * 2 + j
                bp = kcn % 4
                par = sl % 2
                p.op("sync", lambda h, kcn=kcn, bp=bp: h.dma_start(
                    out=kch[bp][:], in_=kk[kcn * P:(kcn + 1) * P, :]),
                    writes=[f"kch{bp}"], n=16, sem=f"KC{kcn % 2}")
                p.op("vector", lambda h, bp=bp: h.tensor_copy(
                    kbf[bp % 2][:], kch[bp][:]),
                    reads=[f"kch{bp}"], writes=[f"kbf{bp % 2}"])
                ev = None
                for dc in range(8):
                    fn = (lambda h, bp=bp, dc=dc:
                          h.transpose(ptk[:, dc * P:(dc + 1) * P],
                                      kbf[bp % 2][:, dc * P:(dc + 1) * P],
                                      identb[:]))
                    if dc == 0:
                        ev = p.op("tensor", fn,
                                  reads=[f"kbf{bp % 2}", "identb"],
                                  writes=["ptk"], track=False)
                    else:
                        ev = p.op("tensor", fn, track=False)
                p.note_read([f"kbf{bp % 2}", "identb"], ev)
                p.note_write(["ptk"], ev)
                for dc in range(8):
                    p.op("vector",
                         lambda h, par=par, dc=dc, j=j: h.tensor_copy(
                             kstage[par][:, dc * 256 + j * P:
                                         dc * 256 + (j + 1) * P],
                             ptk[:, dc * P:(dc + 1) * P]),
                         reads=["ptk"], writes=[f"kst{par}_{j}_{dc}"])
                if j == 1:
                    p.op("sync", lambda h, sl=sl, par=par: h.dma_start(
                        out=kt_s[sl], in_=kstage[par][:]),
                        reads=[f"kst{par}_{jj}_{dc}" for jj in range(2)
                               for dc in range(8)],
                        writes=[f"kt_s{sl}"], n=16, sem=f"KT{sl // 16}")

            def w1_half(sl, j):
                jc = sl * 2 + j
                bp = jc % 4
                par = sl % 2
                p.op("scalar", lambda h, jc=jc, bp=bp: h.dma_start(
                    out=w1ch[bp][:], in_=w1d[jc * P:(jc + 1) * P, :]),
                    writes=[f"w1ch{bp}"], n=16, sem=f"WC{jc % 2}")
                p.op("vector", lambda h, bp=bp: h.tensor_scalar(
                    w1q[:], w1ch[bp][:], sw1col[:], None, op0=ALU.mult),
                    reads=[f"w1ch{bp}", "sw1"], writes=["w1q"])
                p.op("vector", lambda h: h.tensor_scalar(
                    w1q[:], w1q[:], 1.0, -1.0, op0=ALU.min, op1=ALU.max),
                    reads=["w1q"], writes=["w1q"])
                p.op("vector", lambda h, bp=bp: h.tensor_scalar(
                    w1b[bp % 2][:], w1q[:], MAGIC, MAGIC,
                    op0=ALU.add, op1=ALU.subtract),
                    reads=["w1q"], writes=[f"w1b{bp % 2}"])
                ev = None
                for dc in range(8):
                    fn = (lambda h, bp=bp, dc=dc:
                          h.transpose(ptw1[:, dc * P:(dc + 1) * P],
                                      w1b[bp % 2][:, dc * P:(dc + 1) * P],
                                      identb[:]))
                    if dc == 0:
                        ev = p.op("tensor", fn,
                                  reads=[f"w1b{bp % 2}", "identb"],
                                  writes=["ptw1"], track=False)
                    else:
                        ev = p.op("tensor", fn, track=False)
                p.note_read([f"w1b{bp % 2}", "identb"], ev)
                p.note_write(["ptw1"], ev)
                for dc in range(8):
                    p.op("vector",
                         lambda h, par=par, j=j, dc=dc: h.tensor_copy(
                             w1stage[par][:, j * 1024 + dc * P:
                                          j * 1024 + (dc + 1) * P],
                             ptw1[:, dc * P:(dc + 1) * P]),
                         reads=["ptw1"], writes=[f"w1st{par}_{j}_{dc}"])
                if j == 1:
                    p.op("scalar", lambda h, sl=sl, par=par: h.dma_start(
                        out=w1t_s[sl], in_=w1stage[par][:]),
                        reads=[f"w1st{par}_{jj}_{dc}" for jj in range(2)
                               for dc in range(8)],
                        writes=[f"w1t_s{sl}"], n=16, sem="W1S")

            def w2_quant(i):
                bp = i % 2
                p.op("gpsimd", lambda h, i=i, bp=bp: h.dma_start(
                    out=w2ch[bp][:], in_=w2d[i * P:(i + 1) * P, :]),
                    writes=[f"w2ch{bp}"], n=16, sem=f"WD{bp}")
                p.op("vector", lambda h, bp=bp: h.tensor_scalar(
                    w2q[:], w2ch[bp][:], sw2col[:], None, op0=ALU.mult),
                    reads=[f"w2ch{bp}", "sw2"], writes=["w2q"])
                p.op("vector", lambda h: h.tensor_scalar(
                    w2q[:], w2q[:], 1.0, -1.0, op0=ALU.min, op1=ALU.max),
                    reads=["w2q"], writes=["w2q"])
                p.op("vector", lambda h, i=i: h.tensor_scalar(
                    w2full[:, i * HID:(i + 1) * HID], w2q[:], MAGIC, MAGIC,
                    op0=ALU.add, op1=ALU.subtract),
                    reads=["w2q"], writes=[f"w2f{i}"])

            def w2_tr(sl):
                par = sl % 2
                for j in range(2):
                    jc = sl * 2 + j
                    ev = None
                    for ocx in range(8):
                        fn = (lambda h, ocx=ocx, jc=jc:
                              h.transpose(ptw2[:, ocx * P:(ocx + 1) * P],
                                          w2full[:, ocx * HID + jc * P:
                                                 ocx * HID + (jc + 1) * P],
                                          identb[:]))
                        if ocx == 0:
                            ev = p.op("tensor", fn,
                                      reads=[f"w2f{x}" for x in range(8)]
                                      + ["identb"],
                                      writes=["ptw2"], track=False)
                        else:
                            ev = p.op("tensor", fn, track=False)
                    p.note_read([f"w2f{x}" for x in range(8)] + ["identb"], ev)
                    p.note_write(["ptw2"], ev)
                    for ocx in range(8):
                        p.op("vector",
                             lambda h, par=par, j=j, ocx=ocx: h.tensor_copy(
                                 w2stage[par][:, j * 1024 + ocx * P:
                                              j * 1024 + (ocx + 1) * P],
                                 ptw2[:, ocx * P:(ocx + 1) * P]),
                             reads=["ptw2"], writes=[f"w2st{par}_{j}_{ocx}"])
                p.op("gpsimd", lambda h, sl=sl, par=par: h.dma_start(
                    out=w2t_s[sl], in_=w2stage[par][:]),
                    reads=[f"w2st{par}_{j}_{ocx}" for j in range(2)
                           for ocx in range(8)],
                    writes=[f"w2t_s{sl}"], n=16, sem="W2S")

            for i in range(32):
                if i < 16:
                    mean_chunk(w1d, 2 * i, D, "w1ch", w1ch, "scalar", 4)
                    mean_chunk(w1d, 2 * i + 1, D, "w1ch", w1ch, "scalar", 4)
                if i < 8:
                    mean_chunk(w2d, i, HID, "w2ch", w2ch, "gpsimd", 2,
                               acol=32)
                k_half(i, 0)
                k_half(i, 1)
                if i == 15:
                    mean_final(32, "w1ch", sw1col, c1col, "sw1", "c1")
                if i == 7:
                    mean_final(8, "w2ch", sw2col, c2col, "sw2", "c2",
                               acol=32)
                if i >= 16:
                    w1_half(i - 16, 0)
                    w1_half(i - 16, 1)
                    if i % 2 == 0:
                        w2_quant((i - 16) // 2)
            for sl in range(16):
                w2_tr(sl)

            for i in range(8):
                p.op("gpsimd", lambda h, i=i: h.dma_start(
                    out=v_s[i * 1024:(i + 1) * 1024, :],
                    in_=vv[i * 1024:(i + 1) * 1024, :]),
                    writes=[f"v_s{i}"], n=16, sem=f"VC{i // 4}")

            p.barrier()
            for sl in range(32):
                p.W[f"kt_s{sl}"] = (f"KT{sl // 16}", 16 * 16)
            for sl in range(16):
                p.W[f"w1t_s{sl}"] = ("W1S", 16 * 16)
                p.W[f"w2t_s{sl}"] = ("W2S", 16 * 16)
            for i in range(8):
                p.W[f"v_s{i}"] = (f"VC{i // 4}", 16 * 4)

        def ps(name, shape, dtype=F32):
            return S.enter_context(nc.psum_tensor(name, shape, dtype))

        pb = [ps(f"pb{i}", [P, 512]) for i in range(8)]
        PBN = [f"pb{i}" for i in range(8)]

        scT = sb("scT", [P, 64 * 512], BF16)
        y_full = sb("y_full", [P, 32 * 512], BF16)
        wsy = [sb(f"wsy{i}", [P, 2048], BF16) for i in range(6)]
        wgp = [sb(f"wgp{i}", [P, 2048], BF16) for i in range(5)]
        xnT = sb("xnT", [P, 8 * 512], BF16)
        xqT = sb("xqT", [P, 8 * 514], BF16)
        xln = [sb("xln0", [P, D]), sb("xln1", [P, D])]
        xadd = [sb("xadd0", [P, D]), sb("xadd1", [P, D])]
        qtc = [sb("qtc0", [P, D]), sb("qtc1", [P, D])]
        qtmp = sb("qtmp", [P, 512])
        sqz = [sb("sqz0", [P, 512], BF16), sb("sqz1", [P, 512], BF16),
               sb("sqz2", [P, 512], BF16)]
        y_scr = [sb("y_scr0", [P, 256]), sb("y_scr1", [P, 256])]
        mxq = sb("mxq", [P, 512])
        csbc = sb("csbc", [P, 512])
        csbc2 = sb("csbc2", [P, 512])
        invbc = sb("invbc", [P, 512])
        rows = sb("rows", [1, 1024])
        stat = sb("stat", [P, 32])
        x_halo = sb("x_halo", [2, D])
        xq_halo = sb("xq_halo", [2, D], BF16)
        stat_h = sb("stat_h", [2, 8])

        def bslot(tc, which):
            c0 = tc * 2048 + which * 1024
            names = [f"yf{tc * 4 + 2 * which + q}_{hf}"
                     for q in (0, 1) for hf in (0, 1)]
            return y_full[:, c0:c0 + 1024], names

        def mm_group(fns, bank, reads):
            ev = None
            for i, fn in enumerate(fns):
                if i == 0:
                    ev = p.op("tensor", fn, reads=reads, writes=[bank],
                              track=False)
                else:
                    ev = p.op("tensor", fn, track=False)
            p.note_read(reads, ev)
            p.note_write([bank], ev)
            return ev

        def sy_slot(t, i):
            return (t * 64 + i) % 6

        def gp_slot(t, i):
            return (t * 32 + i) % 5

        def emit_B_ln(t, xq="gpsimd"):
            g0 = t * T
            p.op(xq, lambda h, g0=g0: h.dma_start(
                out=x_halo[:],
                in_=bass.AP(xs.tensor, g0 * D, [[513 * D, 2], [1, D]])),
                writes=["x_halo"], n=16, sem="XH")
            for tc in range(4):
                sl = tc % 2
                xnb, xnn = bslot(tc, 0)
                xqb, xqn = bslot(tc, 1)
                p.op(xq, lambda h, tc=tc, sl=sl, g0=g0: h.dma_start(
                    out=xln[sl][:],
                    in_=xs[1 + g0 + tc * P: 1 + g0 + (tc + 1) * P, :]),
                    writes=[f"xln{sl}"], n=16, sem=f"XL{sl}")
                st = stat[:, tc * 8:tc * 8 + 8]
                sn = [f"st{tc}_{i}" for i in range(8)]
                p.op("vector", lambda h, sl=sl, st=st: h.reduce_sum(
                    st[:, 0:1], xln[sl][:], AX),
                    reads=[f"xln{sl}"], writes=[sn[0]])
                p.op("vector", lambda h, st=st: h.tensor_scalar(
                    st[:, 0:1], st[:, 0:1], 1.0 / D, None, op0=ALU.mult),
                    reads=[sn[0]], writes=[sn[0]])
                p.op("vector", lambda h, sl=sl, st=st: h.tensor_scalar(
                    xln[sl][:], xln[sl][:], st[:, 0:1], None,
                    op0=ALU.subtract),
                    reads=[sn[0]], writes=[f"xln{sl}"])
                p.op("scalar", lambda h, sl=sl, xqb=xqb, st=st: h.activation(
                    xqb[:], xln[sl][:], AF.Square, accum_out=st[:, 1:2]),
                    reads=[f"xln{sl}"], writes=xqn + [sn[1]])
                p.op("vector", lambda h, st=st: h.tensor_scalar(
                    st[:, 1:2], st[:, 1:2], 1.0 / D, EPS,
                    op0=ALU.mult, op1=ALU.add),
                    reads=[sn[1]], writes=[sn[1]])
                p.op("scalar", lambda h, st=st: h.activation(
                    st[:, 1:2], st[:, 1:2], AF.Sqrt),
                    reads=[sn[1]], writes=[sn[1]])
                p.op("vector", lambda h, st=st: h.reciprocal(
                    st[:, 2:3], st[:, 1:2]), reads=[sn[1]], writes=[sn[2]])
                p.op("vector", lambda h, sl=sl, st=st: h.reduce_max(
                    st[:, 3:4], xln[sl][:], AX, apply_absolute_value=True),
                    reads=[f"xln{sl}"], writes=[sn[3]])
                p.op("vector", lambda h, st=st: h.tensor_tensor(
                    st[:, 3:4], st[:, 3:4], st[:, 2:3], op=ALU.mult),
                    reads=[sn[3], sn[2]], writes=[sn[3]])
                p.op("vector", lambda h, st=st: h.tensor_scalar(
                    st[:, 3:4], st[:, 3:4], 1e-5, None, op0=ALU.max),
                    reads=[sn[3]], writes=[sn[3]])
                p.op("vector", lambda h, st=st: h.reciprocal(
                    st[:, 4:5], st[:, 3:4]), reads=[sn[3]], writes=[sn[4]])
                p.op("vector", lambda h, st=st: h.tensor_scalar(
                    st[:, 4:5], st[:, 4:5], 127.0, None, op0=ALU.mult),
                    reads=[sn[4]], writes=[sn[4]])
                p.op("vector", lambda h, st=st: h.tensor_tensor(
                    st[:, 5:6], st[:, 2:3], st[:, 4:5], op=ALU.mult),
                    reads=[sn[2], sn[4]], writes=[sn[5]])
                p.op("vector", lambda h, st=st: h.tensor_tensor(
                    st[:, 6:7], st[:, 3:4], c1col[:], op=ALU.mult),
                    reads=[sn[3], "c1"], writes=[sn[6]])
                p.op("vector", lambda h, sl=sl, xnb=xnb, st=st: h.tensor_scalar(
                    xnb[:], xln[sl][:], st[:, 2:3], None, op0=ALU.mult),
                    reads=[f"xln{sl}", sn[2]], writes=xnn)
                for hh in (0, 1):
                    p.op("vector", lambda h, sl=sl, st=st, hh=hh: h.tensor_scalar(
                        qtmp[:], xln[sl][:, hh * 512:(hh + 1) * 512],
                        st[:, 5:6], MAGIC, op0=ALU.mult, op1=ALU.add),
                        reads=[f"xln{sl}", sn[5]], writes=["qtmp"])
                    p.op("vector", lambda h: h.tensor_scalar(
                        qtmp[:], qtmp[:], MAGIC, 127.0,
                        op0=ALU.subtract, op1=ALU.min),
                        reads=["qtmp"], writes=["qtmp"])
                    p.op("vector", lambda h, xqb=xqb, st=st, hh=hh: h.tensor_scalar(
                        xqb[:, hh * 512:(hh + 1) * 512], qtmp[:], -128.0,
                        st[:, 6:7], op0=ALU.max, op1=ALU.mult),
                        reads=["qtmp", sn[6]], writes=xqn)

            p.op("vector", lambda h: h.reduce_sum(
                stat_h[:, 0:1], x_halo[:], AX), reads=["x_halo"], writes=["sh0"])
            p.op("vector", lambda h: h.tensor_scalar(
                stat_h[:, 0:1], stat_h[:, 0:1], 1.0 / D, None, op0=ALU.mult),
                reads=["sh0"], writes=["sh0"])
            p.op("vector", lambda h: h.tensor_scalar(
                x_halo[:], x_halo[:], stat_h[:, 0:1], None, op0=ALU.subtract),
                reads=["sh0"], writes=["x_halo"])
            p.op("scalar", lambda h: h.activation(
                xq_halo[:], x_halo[:], AF.Square, accum_out=stat_h[:, 1:2]),
                reads=["x_halo"], writes=["xq_halo", "sh1"])
            p.op("vector", lambda h: h.tensor_scalar(
                stat_h[:, 1:2], stat_h[:, 1:2], 1.0 / D, EPS,
                op0=ALU.mult, op1=ALU.add), reads=["sh1"], writes=["sh1"])
            p.op("scalar", lambda h: h.activation(
                stat_h[:, 1:2], stat_h[:, 1:2], AF.Sqrt),
                reads=["sh1"], writes=["sh1"])
            p.op("vector", lambda h: h.reciprocal(
                stat_h[:, 2:3], stat_h[:, 1:2]), reads=["sh1"], writes=["sh2"])
            p.op("vector", lambda h: h.reduce_max(
                stat_h[:, 3:4], x_halo[:], AX, apply_absolute_value=True),
                reads=["x_halo"], writes=["sh3"])
            p.op("vector", lambda h: h.tensor_tensor(
                stat_h[:, 3:4], stat_h[:, 3:4], stat_h[:, 2:3], op=ALU.mult),
                reads=["sh3", "sh2"], writes=["sh3"])
            p.op("vector", lambda h: h.tensor_scalar(
                stat_h[:, 3:4], stat_h[:, 3:4], 1e-5, None, op0=ALU.max),
                reads=["sh3"], writes=["sh3"])
            p.op("vector", lambda h: h.reciprocal(
                stat_h[:, 4:5], stat_h[:, 3:4]), reads=["sh3"], writes=["sh4"])
            p.op("vector", lambda h: h.tensor_scalar(
                stat_h[:, 4:5], stat_h[:, 4:5], 127.0, None, op0=ALU.mult),
                reads=["sh4"], writes=["sh4"])
            p.op("vector", lambda h: h.tensor_tensor(
                stat_h[:, 5:6], stat_h[:, 2:3], stat_h[:, 4:5], op=ALU.mult),
                reads=["sh2", "sh4"], writes=["sh5"])
            p.op("vector", lambda h: h.tensor_tensor(
                stat_h[:, 6:7], stat_h[:, 3:4], c1col[0:2, :], op=ALU.mult),
                reads=["sh3", "c1"], writes=["sh6"])
            p.op("vector", lambda h: h.tensor_scalar(
                x_halo[:], x_halo[:], stat_h[:, 5:6], MAGIC,
                op0=ALU.mult, op1=ALU.add),
                reads=["sh5"], writes=["x_halo"])
            p.op("vector", lambda h: h.tensor_scalar(
                x_halo[:], x_halo[:], MAGIC, 127.0,
                op0=ALU.subtract, op1=ALU.min),
                reads=["x_halo"], writes=["x_halo"])
            p.op("vector", lambda h: h.tensor_scalar(
                xq_halo[:], x_halo[:], -128.0, stat_h[:, 6:7],
                op0=ALU.max, op1=ALU.mult),
                reads=["x_halo", "sh6"], writes=["xq_halo"])

        def emit_B_tr(t, bc0):
            bcnt = bc0
            for tc in range(4):
                for which in (0, 1):
                    src, srcn = bslot(tc, which)
                    dst = xnT if which == 0 else xqT
                    dname = "xnT" if which == 0 else "xqT"
                    off = 0 if which == 0 else 1
                    w = 512 if which == 0 else 514
                    for hg in (0, 1):
                        bk = pb[2 + (bcnt % 2)]
                        bkn = PBN[2 + (bcnt % 2)]
                        bcnt += 1
                        ev = None
                        for dq in range(4):
                            dc = hg * 4 + dq
                            fn = (lambda h, src=src, dc=dc, dq=dq, bk=bk:
                                  h.matmul(bk[:, dq * P:(dq + 1) * P],
                                           src[:, dc * P:(dc + 1) * P],
                                           identb[:], start=True, stop=True))
                            if dq == 0:
                                ev = p.op("tensor", fn,
                                          reads=srcn + ["identb"],
                                          writes=[bkn], track=False)
                            else:
                                ev = p.op("tensor", fn, track=False)
                        p.note_read(srcn + ["identb"], ev)
                        p.note_write([bkn], ev)
                        c0 = hg * 4 * w + off + tc * P
                        rl = 8 * w
                        p.op("vector",
                             lambda h, dst=dst, c0=c0, w=w, bk=bk, rl=rl:
                             h.tensor_copy(
                                 bass.AP(dst, c0, [[rl, P], [w, 4], [1, P]]),
                                 bass.AP(bk, 0, [[512, P], [P, 4], [1, P]])),
                             reads=[bkn], writes=[dname])
            ev = None
            for dc in range(8):
                fn = (lambda h, dc=dc: h.matmul(
                    pb[2][:, dc * 2:dc * 2 + 2],
                    xq_halo[:, dc * P:(dc + 1) * P], identb[0:2, 0:2],
                    start=True, stop=True))
                if dc == 0:
                    ev = p.op("tensor", fn, reads=["xq_halo", "identb"],
                              writes=[PBN[2]], track=False)
                else:
                    ev = p.op("tensor", fn, track=False)
            p.note_read(["xq_halo", "identb"], ev)
            p.note_write([PBN[2]], ev)
            for dc in range(8):
                p.op("vector", lambda h, dc=dc: h.tensor_copy(
                    xqT[:, dc * 514:dc * 514 + 1], pb[2][:, dc * 2:dc * 2 + 1]),
                    reads=[PBN[2]], writes=["xqT"])
                p.op("vector", lambda h, dc=dc: h.tensor_copy(
                    xqT[:, dc * 514 + 513:dc * 514 + 514],
                    pb[2][:, dc * 2 + 1:dc * 2 + 2]),
                    reads=[PBN[2]], writes=["xqT"])

        def emit_main(t):
            g0 = t * T

            for ncx in range(64):
                sl = ncx // 2
                if ncx % 2 == 0:
                    ss = sy_slot(t, sl)
                    p.op("sync", lambda h, sl=sl, ss=ss: h.dma_start(
                        out=wsy[ss][:], in_=kt_s[sl]),
                        reads=[f"kt_s{sl}"], writes=[f"wsy{ss}"], n=16,
                        sem=f"SY{ss}")
                ss = sy_slot(t, ncx // 2)
                bank = PBN[ncx % 4]
                fns = []
                for dc in range(8):
                    fns.append(lambda h, ss=ss, ncx=ncx, dc=dc:
                               h.matmul(pb[ncx % 4][:],
                                        wsy[ss][
                                            :, dc * 256 + (ncx % 2) * P:
                                            dc * 256 + (ncx % 2 + 1) * P],
                                        xnT[:, dc * 512:(dc + 1) * 512],
                                        start=(dc == 0), stop=(dc == 7)))
                mm_group(fns, bank, [f"wsy{ss}", "xnT"])
                p.op("vector", lambda h, ncx=ncx: h.tensor_copy(
                    scT[:, ncx * 512:(ncx + 1) * 512], pb[ncx % 4][:]),
                    reads=[bank], writes=[f"sc{ncx}"])
                p.op("scalar", lambda h, ncx=ncx: h.activation(
                    sqz[ncx % 3][:], scT[:, ncx * 512:(ncx + 1) * 512],
                    AF.Square),
                    reads=[f"sc{ncx}"], writes=[f"sqz{ncx % 3}"])
                if ncx >= 2:
                    nl = ncx - 2
                    p.op("tensor", lambda h, nl=nl: h.matmul(
                        pb[4][:], ones_sq[:], sqz[nl % 3][:],
                        start=(nl == 0), stop=False),
                        reads=[f"sqz{nl % 3}", "onessq"], writes=[PBN[4]])
            for nl in (62, 63):
                p.op("tensor", lambda h, nl=nl: h.matmul(
                    pb[4][:], ones_sq[:], sqz[nl % 3][:],
                    start=False, stop=(nl == 63)),
                    reads=[f"sqz{nl % 3}", "onessq"], writes=[PBN[4]])

            p.op("vector", lambda h: h.tensor_copy(
                rows[0:1, 0:512], pb[4][0:1, :]),
                reads=[PBN[4]], writes=["rows"])
            p.op("scalar", lambda h: h.activation(
                rows[0:1, 0:512], rows[0:1, 0:512], AF.Sqrt),
                reads=["rows"], writes=["rows"])
            p.op("vector", lambda h: h.reciprocal(
                rows[0:1, 0:512], rows[0:1, 0:512]),
                reads=["rows"], writes=["rows"])
            p.op("vector", lambda h: h.tensor_scalar(
                rows[0:1, 0:512], rows[0:1, 0:512], SQRT_N, None,
                op0=ALU.mult), reads=["rows"], writes=["rows"])

            def minmax(jq):
                if jq == 0:
                    p.op("vector", lambda h: h.tensor_copy(
                        mxq[:], y_full[:, 0:512]),
                        reads=["yf0_0", "yf0_1"], writes=["mxq0", "mxq1"])
                    p.op("vector", lambda h: h.tensor_copy(
                        pb[7][:], y_full[:, 0:512]),
                        reads=["yf0_0", "yf0_1"], writes=[PBN[7]])
                else:
                    p.op("vector", lambda h, jq=jq: h.tensor_tensor(
                        mxq[:], mxq[:],
                        y_full[:, jq * 512:(jq + 1) * 512], op=ALU.max),
                        reads=[f"yf{jq}_0", f"yf{jq}_1", "mxq0", "mxq1"],
                        writes=["mxq0", "mxq1"])
                    p.op("vector", lambda h, jq=jq: h.tensor_tensor(
                        pb[7][:], pb[7][:],
                        y_full[:, jq * 512:(jq + 1) * 512], op=ALU.min),
                        reads=[f"yf{jq}_0", f"yf{jq}_1", PBN[7]],
                        writes=[PBN[7]])

            for jc in range(32):
                sl = jc // 2
                if jc == 1:
                    p.op("tensor", lambda h: h.matmul(
                        pb[5][:], ones_row[:], rows[0:1, 0:512],
                        start=True, stop=True),
                        reads=["rows", "onesr"], writes=[PBN[5]])
                    p.op("vector", lambda h: h.tensor_copy(csbc[:], pb[5][:]),
                         reads=[PBN[5]], writes=["csbc"])
                if jc % 2 == 0:
                    ws = sy_slot(t, 32 + sl)
                    p.op("sync", lambda h, sl=sl, ws=ws: h.dma_start(
                        out=wsy[ws][:], in_=w1t_s[sl]),
                        reads=[f"w1t_s{sl}"], writes=[f"wsy{ws}"], n=16,
                        sem=f"SY{ws}")
                ws = sy_slot(t, 32 + jc // 2)
                for hf in range(2):
                    ph = (2 * jc + hf) % 6
                    bank = PBN[ph]
                    fns = []
                    for dc in range(8):
                        fns.append(lambda h, ws=ws, jc=jc, dc=dc, hf=hf, ph=ph:
                                   h.matmul(pb[ph][:, 0:258],
                                            wsy[ws][
                                                :, (jc % 2) * 1024 + dc * P:
                                                (jc % 2) * 1024 + (dc + 1) * P],
                                            xqT[:, dc * 514 + hf * 256:
                                                dc * 514 + hf * 256 + 258],
                                            start=(dc == 0), stop=(dc == 7)))
                    mm_group(fns, bank, [f"wsy{ws}", "xqT"])
                    p.op("vector", lambda h, ph=ph, jc=jc: h.tensor_scalar(
                        y_scr[ph % 2][:], pb[ph][:, 1:257],
                        taps[:, jc * 3 + 1:jc * 3 + 2], None, op0=ALU.mult),
                        reads=[bank, "taps"], writes=[f"ysc{ph % 2}"])
                    p.op("vector", lambda h, ph=ph, jc=jc: h.scalar_tensor_tensor(
                        y_scr[ph % 2][:], pb[ph][:, 0:256],
                        taps[:, jc * 3:jc * 3 + 1], y_scr[ph % 2][:],
                        op0=ALU.mult, op1=ALU.add),
                        reads=[bank, "taps"], writes=[f"ysc{ph % 2}"])
                    p.op("vector", lambda h, ph=ph, jc=jc: h.scalar_tensor_tensor(
                        y_scr[ph % 2][:], pb[ph][:, 2:258],
                        taps[:, jc * 3 + 2:jc * 3 + 3], y_scr[ph % 2][:],
                        op0=ALU.mult, op1=ALU.add),
                        reads=[bank, "taps"], writes=[f"ysc{ph % 2}"])
                    p.op("scalar", lambda h, ph=ph, jc=jc, hf=hf: h.activation(
                        y_full[:, jc * 512 + hf * 256:jc * 512 + hf * 256 + 256],
                        y_scr[ph % 2][:], AF.Silu, bias=bias_c[:, jc:jc + 1]),
                        reads=[f"ysc{ph % 2}", "bias"], writes=[f"yf{jc}_{hf}"])
                if jc >= 1:
                    minmax(jc - 1)
                if 27 <= jc <= 31:
                    sv = jc - 27
                    gs = gp_slot(t, sv)
                    p.op("gpsimd", lambda h, sv=sv, gs=gs: h.dma_start(
                        out=bass.AP(wgp[gs], 0,
                                    [[2048, P], [1024, 2], [1, 1024]]),
                        in_=bass.AP(v_s.tensor, sv * 256 * D,
                                    [[D, P], [P * D, 2], [1, D]])),
                        reads=[f"v_s{sv // 4}"], writes=[f"wgp{gs}"], n=16,
                        sem=f"GP{gs}")
            minmax(31)
            p.op("vector", lambda h: h.tensor_scalar(
                pb[7][:], pb[7][:], -1.0, None, op0=ALU.mult),
                reads=[PBN[7]], writes=[PBN[7]])
            p.op("vector", lambda h: h.tensor_tensor(
                mxq[:], mxq[:], pb[7][:], op=ALU.max),
                reads=[PBN[7], "mxq0", "mxq1"],
                writes=["mxq0", "mxq1"])

            def hq_mxqT():
                ev = None
                for tc in range(4):
                    fn = (lambda h, tc=tc: h.transpose(
                        pb[6][:, tc * P:(tc + 1) * P],
                        mxq[:, tc * P:(tc + 1) * P], ident[:]))
                    if tc == 0:
                        ev = p.op("tensor", fn,
                                  reads=["mxq0", "mxq1", "ident"],
                                  writes=[PBN[6]], track=False)
                    else:
                        ev = p.op("tensor", fn, track=False)
                p.note_read(["mxq0", "mxq1", "ident"], ev)
                p.note_write([PBN[6]], ev)
                for tc in range(4):
                    st = stat[:, tc * 8:tc * 8 + 8]
                    p.op("vector", lambda h, tc=tc, st=st: h.reduce_max(
                        st[:, 7:8], pb[6][:, tc * P:(tc + 1) * P], AX),
                        reads=[PBN[6]], writes=[f"st{tc}_7"])

            def hq_statT():
                ev = None
                for tc in range(4):
                    ev = p.op("tensor", lambda h, tc=tc: h.transpose(
                        pb[7][0:1, tc * P:(tc + 1) * P],
                        stat[:, tc * 8 + 7:tc * 8 + 8], ident[:]),
                        reads=[f"st{tc}_7", "ident"],
                        writes=[PBN[7]] if tc == 0 else (), track=(tc == 0))
                p.note_write([PBN[7]], ev)
                p.note_read([f"st{tc}_7" for tc in range(4)], ev)
                p.op("vector", lambda h: h.tensor_scalar(
                    rows[0:1, 512:1024], pb[7][0:1, 0:512], 1e-5, None,
                    op0=ALU.max),
                    reads=[PBN[7]], writes=["rowsi"])
                p.op("vector", lambda h: h.reciprocal(
                    rows[0:1, 0:512], rows[0:1, 512:1024]),
                    reads=["rowsi"], writes=["rows"])
                p.op("vector", lambda h: h.tensor_scalar(
                    rows[0:1, 0:512], rows[0:1, 0:512], 127.0, None,
                    op0=ALU.mult),
                    reads=["rows"], writes=["rows"])
                p.op("vector", lambda h: h.tensor_scalar(
                    rows[0:1, 512:1024], rows[0:1, 512:1024], c2col[0:1, 0:1],
                    None, op0=ALU.mult),
                    reads=["rowsi", "c2"], writes=["rowsi"])

            def hq_bc1():
                p.op("tensor", lambda h: h.matmul(
                    pb[6][:], ones_row[:], rows[0:1, 0:512],
                    start=True, stop=True), reads=["rows", "onesr"],
                    writes=[PBN[6]])
                p.op("vector", lambda h: h.tensor_copy(csbc2[:], pb[6][:]),
                     reads=[PBN[6]], writes=["csbc2"])

            def hq_bc2():
                p.op("tensor", lambda h: h.matmul(
                    pb[7][:], ones_row[:], rows[0:1, 512:1024],
                    start=True, stop=True), reads=["rowsi", "onesr"],
                    writes=[PBN[7]])
                p.op("vector", lambda h: h.tensor_copy(
                    invbc[:, 0:512], pb[7][:]), reads=[PBN[7]],
                    writes=["invbc"])

            def emit_F(f):
                p.op("vector", lambda h, f=f: h.tensor_tensor(
                    sqz[f % 3][:], scT[:, f * 512:(f + 1) * 512],
                    csbc[:], op=ALU.mult),
                    reads=[f"sc{f}", "csbc"], writes=[f"sqz{f % 3}"])
                p.op("scalar", lambda h, f=f: h.activation(
                    scT[:, f * 512:(f + 1) * 512], sqz[f % 3][:],
                    AF.Gelu),
                    reads=[f"sqz{f % 3}"], writes=[f"sc{f}"])

            emit_F(0)
            emit_F(1)
            hq_steps = {1: hq_mxqT, 3: hq_statT, 5: hq_bc1, 7: hq_bc2}
            for ncx in range(8):
                for f in (2 * ncx + 2, 2 * ncx + 3):
                    if f < 64:
                        emit_F(f)
                sl = ncx // 2
                if ncx % 2 == 0 and sl >= 5:
                    gs = gp_slot(t, sl)
                    p.op("gpsimd", lambda h, sl=sl, gs=gs: h.dma_start(
                        out=bass.AP(wgp[gs], 0,
                                    [[2048, P], [1024, 2], [1, 1024]]),
                        in_=bass.AP(v_s.tensor, sl * 256 * D,
                                    [[D, P], [P * D, 2], [1, D]])),
                        reads=[f"v_s{sl // 4}"], writes=[f"wgp{gs}"], n=16,
                        sem=f"GP{gs}")
                gs = gp_slot(t, ncx // 2)
                ev = None
                for ol in range(6):
                    fn = (lambda h, gs=gs, ncx=ncx, ol=ol:
                          h.matmul(pb[ol][:],
                                   wgp[gs][
                                       :, (ncx % 2) * 1024 + ol * P:
                                       (ncx % 2) * 1024 + (ol + 1) * P],
                                   scT[:, ncx * 512:(ncx + 1) * 512],
                                   start=(ncx == 0), stop=False))
                    rd = [f"wgp{gs}", f"sc{ncx}"] if ol == 0 else ()
                    wr = [PBN[ol]] if ncx == 0 else ()
                    ev = p.op("tensor", fn, reads=rd, writes=wr,
                              track=False)
                p.note_read([f"wgp{gs}", f"sc{ncx}"], ev)
                if ncx in hq_steps:
                    hq_steps[ncx]()

            for ncx in range(8):
                gs = gp_slot(t, ncx // 2)
                ev = None
                for ol in (6, 7):
                    fn = (lambda h, gs=gs, ncx=ncx, ol=ol:
                          h.matmul(pb[ol][:],
                                   wgp[gs][
                                       :, (ncx % 2) * 1024 + ol * P:
                                       (ncx % 2) * 1024 + (ol + 1) * P],
                                   scT[:, ncx * 512:(ncx + 1) * 512],
                                   start=(ncx == 0), stop=False))
                    rd = [f"wgp{gs}", f"sc{ncx}"] if ol == 6 else ()
                    wr = [PBN[ol]] if ncx == 0 else ()
                    ev = p.op("tensor", fn, reads=rd, writes=wr,
                              track=False)
                p.note_read([f"wgp{gs}", f"sc{ncx}"], ev)

            last_ev = [None] * 8
            def emit_quant(jq):
                p.op("vector", lambda h, jq=jq: h.tensor_tensor(
                    mxq[:], y_full[:, jq * 512:(jq + 1) * 512], csbc2[:],
                    op=ALU.mult),
                    reads=[f"yf{jq}_0", f"yf{jq}_1", "csbc2"],
                    writes=["mxq0", "mxq1"])
                p.op("vector", lambda h: h.tensor_scalar(
                    mxq[:], mxq[:], 127.0, -128.0, op0=ALU.min,
                    op1=ALU.max),
                    reads=["mxq0", "mxq1"], writes=["mxq0", "mxq1"])
                p.op("vector", lambda h: h.tensor_scalar(
                    mxq[:], mxq[:], MAGIC, MAGIC, op0=ALU.add,
                    op1=ALU.subtract),
                    reads=["mxq0", "mxq1"], writes=["mxq0", "mxq1"])
                p.op("vector", lambda h, jq=jq: h.tensor_tensor(
                    y_full[:, jq * 512:(jq + 1) * 512], mxq[:],
                    invbc[:, 0:512], op=ALU.mult),
                    reads=["mxq0", "mxq1", "invbc"],
                    writes=[f"yf{jq}_0", f"yf{jq}_1"])

            for ncx in range(8, 64):
                for f in (2 * ncx + 2, 2 * ncx + 3):
                    if f < 64:
                        emit_F(f)
                if 8 <= ncx < 40:
                    emit_quant(ncx - 8)
                sl = ncx // 2
                if ncx % 2 == 0 and sl >= 5:
                    gs = gp_slot(t, sl)
                    p.op("gpsimd", lambda h, sl=sl, gs=gs: h.dma_start(
                        out=bass.AP(wgp[gs], 0,
                                    [[2048, P], [1024, 2], [1, 1024]]),
                        in_=bass.AP(v_s.tensor, sl * 256 * D,
                                    [[D, P], [P * D, 2], [1, D]])),
                        reads=[f"v_s{sl // 4}"], writes=[f"wgp{gs}"], n=16,
                        sem=f"GP{gs}")
                gs = gp_slot(t, ncx // 2)
                ev = None
                for ol in range(8):
                    fn = (lambda h, gs=gs, ncx=ncx, ol=ol:
                          h.matmul(pb[ol][:],
                                   wgp[gs][
                                       :, (ncx % 2) * 1024 + ol * P:
                                       (ncx % 2) * 1024 + (ol + 1) * P],
                                   scT[:, ncx * 512:(ncx + 1) * 512],
                                   start=False, stop=False))
                    rd = [f"wgp{gs}", f"sc{ncx}"] if ol == 0 else ()
                    ev = p.op("tensor", fn, reads=rd, track=False)
                p.note_read([f"wgp{gs}", f"sc{ncx}"], ev)
            for jc in range(32):
                sl2 = jc // 2
                if jc % 2 == 0:
                    ss = sy_slot(t, 48 + sl2)
                    p.op("sync", lambda h, sl2=sl2, ss=ss: h.dma_start(
                        out=wsy[ss][:], in_=w2t_s[sl2]),
                        reads=[f"w2t_s{sl2}"], writes=[f"wsy{ss}"], n=16,
                        sem=f"SY{ss}")
                ss = sy_slot(t, 48 + jc // 2)
                ev = None
                for ol in range(8):
                    fn = (lambda h, ss=ss, jc=jc, ol=ol:
                          h.matmul(pb[ol][:],
                                   wsy[ss][
                                       :, (jc % 2) * 1024 + ol * P:
                                       (jc % 2) * 1024 + (ol + 1) * P],
                                   y_full[:, jc * 512:(jc + 1) * 512],
                                   start=False, stop=(jc == 31)))
                    ev = p.op("tensor", fn,
                              reads=[f"wsy{ss}", f"yf{jc}_0", f"yf{jc}_1"]
                              if ol == 0 else (), track=False)
                    if jc == 31:
                        last_ev[ol] = ev
                p.note_read([f"wsy{ss}", f"yf{jc}_0", f"yf{jc}_1"], ev)
            for ol in range(8):
                p.note_write([PBN[ol]], last_ev[ol])

        def emit_evac(t):
            g0 = t * T
            xa = [xadd[0], xadd[1], xadd[0], xadd[1]]
            xan = ["xadd0", "xadd1", "xadd0", "xadd1"]
            for tc in range(2):
                p.op("gpsimd", lambda h, tc=tc, g0=g0: h.dma_start(
                    out=xa[tc][:],
                    in_=xs[1 + g0 + tc * P: 1 + g0 + (tc + 1) * P, :]),
                    writes=[xan[tc]], n=16, sem=f"XA{tc % 2}")
            for ol in range(8):
                stg = mxq if ol % 2 == 0 else csbc
                stgn = ["mxq0", "mxq1"] if ol % 2 == 0 else ["csbc"]
                p.op("vector", lambda h, ol=ol, stg=stg: h.tensor_copy(
                    stg[:], pb[ol][:]),
                    reads=[PBN[ol]], writes=stgn)
                ev = None
                for tc in range(4):
                    fn = (lambda h, ol=ol, tc=tc, stg=stg: h.transpose(
                        pb[ol][:, tc * P:(tc + 1) * P],
                        stg[:, tc * P:(tc + 1) * P], ident[:]))
                    if tc == 0:
                        ev = p.op("tensor", fn, reads=stgn + ["ident"],
                                  writes=[PBN[ol]], track=False)
                    else:
                        ev = p.op("tensor", fn, track=False)
                p.note_read(stgn + ["ident"], ev)
                p.note_write([PBN[ol]], ev)
            for tc in range(4):
                for ol in range(8):
                    p.op("vector",
                         lambda h, tc=tc, ol=ol: h.scalar_tensor_tensor(
                             qtc[tc % 2][:, ol * P:(ol + 1) * P],
                             pb[ol][:, tc * P:(tc + 1) * P], 1.0,
                             xa[tc][:, ol * P:(ol + 1) * P],
                             op0=ALU.mult, op1=ALU.add),
                         reads=[PBN[ol], xan[tc]],
                         writes=[f"qtc{tc % 2}"])
                p.op("scalar", lambda h, tc=tc, g0=g0: h.dma_start(
                    out=outd[g0 + tc * P:g0 + (tc + 1) * P, :],
                    in_=qtc[tc % 2][:]),
                    reads=[f"qtc{tc % 2}"], writes=[f"o{t}_{tc}"],
                    n=16, sem=f"VO{tc % 2}")
                if tc < 2:
                    p.op("gpsimd", lambda h, tc=tc, g0=g0: h.dma_start(
                        out=xa[tc + 2][:],
                        in_=xs[1 + g0 + (tc + 2) * P: 1 + g0 + (tc + 3) * P, :]),
                        writes=[xan[tc + 2]], n=16, sem=f"XA{tc % 2}")

        emit_B_ln(0, xq="sync")
        emit_B_tr(0, 0)
        for t in range(n_tiles):
            emit_main(t)
            if t + 1 < n_tiles:
                emit_B_ln(t + 1)
            emit_evac(t)
            if t + 1 < n_tiles:
                emit_B_tr(t + 1, (t + 1) % 2)

        sync_sems = [f"SY{i}" for i in range(6)] + ["KC0", "KC1",
                     "CS0", "CS1", "CS2", "KT0", "KT1"]
        gp_sems = ([f"GP{i}" for i in range(5)]
                   + ["XL0", "XL1", "XH", "XA0", "XA1",
                      "WD0", "WD1", "W2S", "VC0", "VC1"])
        sc_sems = ["WC0", "WC1", "W1S", "VO0", "VO1"]
        p.wait("sync", [(s, p.cnt.get(s, 0)) for s in sync_sems])
        p.wait("gpsimd", [(s, p.cnt.get(s, 0)) for s in gp_sems])
        p.wait("scalar", [(s, p.cnt.get(s, 0)) for s in sc_sems])

        block = S.enter_context(nc.Block())
        p.emit(block, semh)

    return nc


def kernel(L0_attn_q, L0_attn_k, L0_attn_v, L0_ffn_w1, L0_ffn_conv_w,
           L0_ffn_conv_b, L0_ffn_w2):
    x = np.ascontiguousarray(np.asarray(L0_attn_q, dtype=np.float32))
    k = np.ascontiguousarray(np.asarray(L0_attn_k, dtype=np.float32))
    v = np.ascontiguousarray(np.asarray(L0_attn_v, dtype=np.float32))
    w1 = np.ascontiguousarray(np.asarray(L0_ffn_w1, dtype=np.float32))
    cw = np.ascontiguousarray(
        np.asarray(L0_ffn_conv_w, dtype=np.float32).reshape(HID, 3))
    cb = np.ascontiguousarray(np.asarray(L0_ffn_conv_b, dtype=np.float32))
    w2 = np.ascontiguousarray(np.asarray(L0_ffn_w2, dtype=np.float32))
    ident = np.eye(P, dtype=np.float32)

    zrow = np.zeros((1, D), np.float32)
    in_maps = []
    for c in range(8):
        b, hh = c // 2, c % 2
        tok0 = hh * TOK
        left = x[b, tok0 - 1:tok0] if hh == 1 else zrow
        right = x[b, tok0 + TOK:tok0 + TOK + 1] if hh == 0 else zrow
        xsl = np.concatenate([left, x[b, tok0:tok0 + TOK], right], axis=0)
        in_maps.append(dict(xs=np.ascontiguousarray(xsl), kk=k, vv=v, w1=w1,
                            w2=w2, cw=cw, cb=cb, ident=ident))

    nc = build_nc()
    res = run_bass_kernel_spmd(nc, in_maps, core_ids=list(range(8)))
    outp = np.empty((4, 8192, D), np.float32)
    for c in range(8):
        b, hh = c // 2, c % 2
        outp[b, hh * TOK:(hh + 1) * TOK] = np.asarray(res.results[c]["out"])
    return outp


# version 15
# speedup vs baseline: 1.1897x; 1.0030x over previous
import numpy as np
import concourse.bass as bass
import concourse.mybir as mybir
from concourse.bass_utils import run_bass_kernel_spmd
from contextlib import ExitStack

F32 = mybir.dt.float32
BF16 = mybir.dt.bfloat16
AF = mybir.ActivationFunctionType
ALU = mybir.AluOpType
AX = mybir.AxisListType.X

MAGIC = float(np.float32(3 * 2.0 ** 22))
D = 1024
NB = 8192
HID = 4096
TOK = 4096
T = 512
NT = TOK // T
SQRT_N = float(np.float32(np.sqrt(np.float32(NB))))
EPS = 1e-5
P = 128

ENGS = ("sync", "vector", "scalar", "tensor", "gpsimd")
ESEM = {"sync": "SYD", "vector": "VS", "scalar": "SS", "tensor": "TS",
        "gpsimd": "GSD"}


class Prog:

    def __init__(self):
        self.q = {e: [] for e in ENGS}
        self.cnt = {}
        self.wc = {e: {} for e in ENGS}
        self.W = {}
        self.R = {}

    def op(self, eng, fn, reads=(), writes=(), extra=(), sem=None, n=1,
           track=True):
        sem = sem if sem is not None else ESEM[eng]
        waits = []
        for b in reads:
            waits.append(self.W.get(b))
        for b in writes:
            waits.append(self.W.get(b))
            waits.extend(self.R.get(b, ()))
        waits.extend(extra)
        wl = []
        for w in waits:
            if w is None:
                continue
            s, v = w
            if v <= 0:
                continue
            assert v <= self.cnt.get(s, 0), f"wait on future {s}={v}"
            if self.wc[eng].get(s, 0) >= v:
                continue
            self.wc[eng][s] = v
            wl.append((s, v))
        self.cnt[sem] = self.cnt.get(sem, 0) + n
        ev = (sem, self.cnt[sem])
        self.q[eng].append((fn, tuple(wl), sem, n))
        if track:
            for b in reads:
                self.R.setdefault(b, []).append(ev)
            for b in writes:
                self.W[b] = ev
                self.R[b] = []
        return ev

    def wait(self, eng, evs):
        wl = []
        for w in evs:
            if w is None:
                continue
            s, v = w
            assert v <= self.cnt.get(s, 0)
            if self.wc[eng].get(s, 0) >= v:
                continue
            self.wc[eng][s] = v
            wl.append((s, v))
        if wl:
            self.q[eng].append((None, tuple(wl), None, 0))

    def note_read(self, bufs, ev):
        for b in bufs:
            self.R.setdefault(b, []).append(ev)

    def note_write(self, bufs, ev):
        for b in bufs:
            self.W[b] = ev
            self.R[b] = []

    def barrier(self):
        tgt = self.cnt.get("BAR", 0) + len(ENGS)
        for e in ENGS:
            self.op(e, lambda h: h.nop(), sem="BAR", track=False)
        for e in ENGS:
            self.wait(e, [("BAR", tgt)])

    def emit(self, block, semh):
        decos = {"sync": block.sync, "vector": block.vector,
                 "scalar": block.scalar, "tensor": block.tensor,
                 "gpsimd": block.gpsimd}
        for e in ENGS:
            q = self.q[e]

            def body(h, q=q):
                for fn, waits, sem, n in q:
                    for (s, v) in waits:
                        h.wait_ge(semh[s], v)
                    if fn is not None:
                        ins = fn(h)
                        if sem is not None:
                            ins.then_inc(semh[sem], n)
            decos[e](body)


def build_nc(n_tiles=NT):
    nc = bass.Bass("TRN2", target_bir_lowering=False)

    xs = nc.dram_tensor("xs", [TOK + 2, D], F32, kind="ExternalInput").ap()
    kk = nc.dram_tensor("kk", [NB, D], F32, kind="ExternalInput").ap()
    vv = nc.dram_tensor("vv", [NB, D], F32, kind="ExternalInput").ap()
    w1d = nc.dram_tensor("w1", [HID, D], F32, kind="ExternalInput").ap()
    w2d = nc.dram_tensor("w2", [D, HID], F32, kind="ExternalInput").ap()
    cwd = nc.dram_tensor("cw", [HID, 3], F32, kind="ExternalInput").ap()
    cbd = nc.dram_tensor("cb", [HID], F32, kind="ExternalInput").ap()
    id_in = nc.dram_tensor("ident", [P, P], F32, kind="ExternalInput").ap()
    outd = nc.dram_tensor("out", [TOK, D], F32, kind="ExternalOutput").ap()

    kt_s = nc.dram_tensor("kt_s", [32, P, 2048], BF16, kind="Internal").ap()
    v_s = nc.dram_tensor("v_s", [NB, D], BF16, kind="Internal").ap()
    w1t_s = nc.dram_tensor("w1t_s", [16, P, 2048], BF16, kind="Internal").ap()
    w2t_s = nc.dram_tensor("w2t_s", [16, P, 2048], BF16, kind="Internal").ap()

    p = Prog()

    with ExitStack() as S:
        def sb(name, shape, dtype=F32):
            return S.enter_context(nc.sbuf_tensor(name, shape, dtype))

        semh = {}
        sem_names = (["VS", "SS", "TS", "BAR"]
                     + [f"SY{i}" for i in range(6)]
                     + [f"GP{i}" for i in range(5)]
                     + ["XL0", "XL1", "XH", "XA0", "XA1", "VO0", "VO1",
                        "KC0", "KC1", "WC0", "WC1", "WD0", "WD1",
                        "CS0", "CS1", "CS2", "KT0", "KT1", "W1S", "W2S",
                        "VC0", "VC1"])
        for s_name in sem_names:
            semh[s_name] = S.enter_context(nc.semaphore(s_name))

        ident = sb("identc", [P, P])
        identb = sb("identbc", [P, P], BF16)
        ones_colf = sb("ones_colf", [P, 1])
        ones_sq = sb("ones_sq", [P, P], BF16)
        ones_row = sb("ones_row", [1, P])
        taps = sb("tapsc", [P, 96])
        bias_c = sb("bias_c", [P, 32])
        c1col = sb("c1col", [P, 1])
        c2col = sb("c2col", [P, 1])
        sw1col = sb("sw1col", [P, 1])
        sw2col = sb("sw2col", [P, 1])

        p.op("sync", lambda h: h.dma_start(out=ident[:], in_=id_in),
             writes=["ident"], n=16, sem="CS0")
        p.op("sync", lambda h: h.dma_start(
            out=taps[:], in_=bass.AP(cwd.tensor, 0, [[3, P], [384, 32], [1, 3]])),
            writes=["taps"], n=16, sem="CS1")
        p.op("sync", lambda h: h.dma_start(
            out=bias_c[:], in_=bass.AP(cbd.tensor, 0, [[1, P], [P, 32]]),
            allow_slow_non_contiguous=True),
            writes=["bias"], n=16, sem="CS2")
        p.op("vector", lambda h: h.tensor_copy(identb[:], ident[:]),
             reads=["ident"], writes=["identb"])
        p.op("vector", lambda h: h.memset(ones_colf[:], 1.0), writes=["onesf"])
        p.op("vector", lambda h: h.memset(ones_sq[:], 1.0), writes=["onessq"])
        p.op("vector", lambda h: h.memset(ones_row[:], 1.0), writes=["onesr"])

        with ExitStack() as SU:
            def sbu(name, shape, dtype=F32):
                return SU.enter_context(nc.sbuf_tensor(name, shape, dtype))

            def psu(name, shape, dtype=F32):
                return SU.enter_context(nc.psum_tensor(name, shape, dtype))

            kch = [sbu(f"kch{i}", [P, D]) for i in range(4)]
            kbf = [sbu("kbf0", [P, D], BF16), sbu("kbf1", [P, D], BF16)]
            kstage = [sbu("kstage0", [P, 2048], BF16),
                      sbu("kstage1", [P, 2048], BF16)]
            ptk = psu("ptk", [P, 1024], BF16)
            w1ch = [sbu(f"w1ch{i}", [P, D]) for i in range(4)]
            w1q = sbu("w1q", [P, D])
            w1b = [sbu("w1b0", [P, D], BF16), sbu("w1b1", [P, D], BF16)]
            w1stage = [sbu("w1stage0", [P, 2048], BF16),
                       sbu("w1stage1", [P, 2048], BF16)]
            ptw1 = psu("ptw1", [P, 1024], BF16)
            w2ch = [sbu("w2ch0", [P, HID]), sbu("w2ch1", [P, HID])]
            w2q = sbu("w2q", [P, HID])
            w2full = sbu("w2full", [P, 8 * HID], BF16)
            w2stage = [sbu("w2stage0", [P, 2048], BF16),
                       sbu("w2stage1", [P, 2048], BF16)]
            ptw2 = psu("ptw2", [P, 1024], BF16)
            acc1 = sbu("acc1s", [P, 40])
            accw = sbu("accws", [P, 1])
            rowt = sbu("rowts", [1, 2])
            pmean = psu("pmean", [1, 2])
            pbcs = psu("pbcs", [P, 2])

            def mean_chunk(wd, i, width, chname, chbufs, queue, nring,
                           acol=0):
                p.op(queue, lambda h, i=i: h.dma_start(
                    out=chbufs[i % nring][:, 0:width],
                    in_=wd[i * P:(i + 1) * P, :]),
                    writes=[f"{chname}{i % nring}"], n=16,
                    sem=(f"WC{i % 2}" if queue == "scalar"
                         else f"WD{i % 2}"))
                p.op("vector", lambda h, i=i, acol=acol: h.reduce_sum(
                    acc1[:, acol + i:acol + i + 1],
                    chbufs[i % nring][:, 0:width], AX,
                    apply_absolute_value=True),
                    reads=[f"{chname}{i % nring}"], writes=[f"{chname}ac{i}"])

            def mean_final(nchunks, chname, sw_col, c_col, swn, cn, acol=0):
                p.op("vector", lambda h, n=nchunks, acol=acol: h.reduce_sum(
                    accw[:], acc1[:, acol:acol + n], AX),
                    reads=[f"{chname}ac{i}" for i in range(nchunks)],
                    writes=["accw"])
                p.op("tensor", lambda h: h.matmul(
                    pmean[0:1, 0:1], ones_colf[:], accw[:],
                    start=True, stop=True),
                    reads=["accw", "onesf"], writes=["pmean"])
                p.op("vector", lambda h: h.tensor_scalar(
                    rowt[0:1, 0:1], pmean[0:1, 0:1], 1.0 / (HID * D), 1e-5,
                    op0=ALU.mult, op1=ALU.max),
                    reads=["pmean"], writes=["rowt0"])
                p.op("vector", lambda h: h.reciprocal(
                    rowt[0:1, 1:2], rowt[0:1, 0:1]),
                    reads=["rowt0"], writes=["rowt1"])
                p.op("tensor", lambda h: h.matmul(
                    pbcs[:, 0:1], ones_row[:], rowt[0:1, 1:2],
                    start=True, stop=True),
                    reads=["rowt1", "onesr"], writes=["pbcs0"])
                p.op("tensor", lambda h: h.matmul(
                    pbcs[:, 1:2], ones_row[:], rowt[0:1, 0:1],
                    start=True, stop=True),
                    reads=["rowt0", "onesr"], writes=["pbcs1"])
                p.op("vector", lambda h: h.tensor_copy(sw_col[:], pbcs[:, 0:1]),
                     reads=["pbcs0"], writes=[swn])
                p.op("vector", lambda h: h.tensor_scalar(
                    c_col[:], pbcs[:, 1:2], 1.0 / 127.0, None, op0=ALU.mult),
                    reads=["pbcs1"], writes=[cn])

            def k_half(sl, j):
                kcn = sl * 2 + j
                bp = kcn % 4
                par = sl % 2
                p.op("sync", lambda h, kcn=kcn, bp=bp: h.dma_start(
                    out=kch[bp][:], in_=kk[kcn * P:(kcn + 1) * P, :]),
                    writes=[f"kch{bp}"], n=16, sem=f"KC{kcn % 2}")
                p.op("vector", lambda h, bp=bp: h.tensor_copy(
                    kbf[bp % 2][:], kch[bp][:]),
                    reads=[f"kch{bp}"], writes=[f"kbf{bp % 2}"])
                ev = None
                for dc in range(8):
                    fn = (lambda h, bp=bp, dc=dc:
                          h.transpose(ptk[:, dc * P:(dc + 1) * P],
                                      kbf[bp % 2][:, dc * P:(dc + 1) * P],
                                      identb[:]))
                    if dc == 0:
                        ev = p.op("tensor", fn,
                                  reads=[f"kbf{bp % 2}", "identb"],
                                  writes=["ptk"], track=False)
                    else:
                        ev = p.op("tensor", fn, track=False)
                p.note_read([f"kbf{bp % 2}", "identb"], ev)
                p.note_write(["ptk"], ev)
                for dc in range(8):
                    p.op("vector",
                         lambda h, par=par, dc=dc, j=j: h.tensor_copy(
                             kstage[par][:, dc * 256 + j * P:
                                         dc * 256 + (j + 1) * P],
                             ptk[:, dc * P:(dc + 1) * P]),
                         reads=["ptk"], writes=[f"kst{par}_{j}_{dc}"])
                if j == 1:
                    p.op("sync", lambda h, sl=sl, par=par: h.dma_start(
                        out=kt_s[sl], in_=kstage[par][:]),
                        reads=[f"kst{par}_{jj}_{dc}" for jj in range(2)
                               for dc in range(8)],
                        writes=[f"kt_s{sl}"], n=16, sem=f"KT{sl // 16}")

            def w1_half(sl, j):
                jc = sl * 2 + j
                bp = jc % 4
                par = sl % 2
                p.op("scalar", lambda h, jc=jc, bp=bp: h.dma_start(
                    out=w1ch[bp][:], in_=w1d[jc * P:(jc + 1) * P, :]),
                    writes=[f"w1ch{bp}"], n=16, sem=f"WC{jc % 2}")
                p.op("vector", lambda h, bp=bp: h.tensor_scalar(
                    w1q[:], w1ch[bp][:], sw1col[:], None, op0=ALU.mult),
                    reads=[f"w1ch{bp}", "sw1"], writes=["w1q"])
                p.op("vector", lambda h: h.tensor_scalar(
                    w1q[:], w1q[:], 1.0, -1.0, op0=ALU.min, op1=ALU.max),
                    reads=["w1q"], writes=["w1q"])
                p.op("vector", lambda h, bp=bp: h.tensor_scalar(
                    w1b[bp % 2][:], w1q[:], MAGIC, MAGIC,
                    op0=ALU.add, op1=ALU.subtract),
                    reads=["w1q"], writes=[f"w1b{bp % 2}"])
                ev = None
                for dc in range(8):
                    fn = (lambda h, bp=bp, dc=dc:
                          h.transpose(ptw1[:, dc * P:(dc + 1) * P],
                                      w1b[bp % 2][:, dc * P:(dc + 1) * P],
                                      identb[:]))
                    if dc == 0:
                        ev = p.op("tensor", fn,
                                  reads=[f"w1b{bp % 2}", "identb"],
                                  writes=["ptw1"], track=False)
                    else:
                        ev = p.op("tensor", fn, track=False)
                p.note_read([f"w1b{bp % 2}", "identb"], ev)
                p.note_write(["ptw1"], ev)
                for dc in range(8):
                    p.op("vector",
                         lambda h, par=par, j=j, dc=dc: h.tensor_copy(
                             w1stage[par][:, j * 1024 + dc * P:
                                          j * 1024 + (dc + 1) * P],
                             ptw1[:, dc * P:(dc + 1) * P]),
                         reads=["ptw1"], writes=[f"w1st{par}_{j}_{dc}"])
                if j == 1:
                    p.op("scalar", lambda h, sl=sl, par=par: h.dma_start(
                        out=w1t_s[sl], in_=w1stage[par][:]),
                        reads=[f"w1st{par}_{jj}_{dc}" for jj in range(2)
                               for dc in range(8)],
                        writes=[f"w1t_s{sl}"], n=16, sem="W1S")

            def w2_quant(i):
                bp = i % 2
                p.op("gpsimd", lambda h, i=i, bp=bp: h.dma_start(
                    out=w2ch[bp][:], in_=w2d[i * P:(i + 1) * P, :]),
                    writes=[f"w2ch{bp}"], n=16, sem=f"WD{bp}")
                p.op("vector", lambda h, bp=bp: h.tensor_scalar(
                    w2q[:], w2ch[bp][:], sw2col[:], None, op0=ALU.mult),
                    reads=[f"w2ch{bp}", "sw2"], writes=["w2q"])
                p.op("vector", lambda h: h.tensor_scalar(
                    w2q[:], w2q[:], 1.0, -1.0, op0=ALU.min, op1=ALU.max),
                    reads=["w2q"], writes=["w2q"])
                p.op("vector", lambda h, i=i: h.tensor_scalar(
                    w2full[:, i * HID:(i + 1) * HID], w2q[:], MAGIC, MAGIC,
                    op0=ALU.add, op1=ALU.subtract),
                    reads=["w2q"], writes=[f"w2f{i}"])

            def w2_tr(sl):
                par = sl % 2
                for j in range(2):
                    jc = sl * 2 + j
                    ev = None
                    for ocx in range(8):
                        fn = (lambda h, ocx=ocx, jc=jc:
                              h.transpose(ptw2[:, ocx * P:(ocx + 1) * P],
                                          w2full[:, ocx * HID + jc * P:
                                                 ocx * HID + (jc + 1) * P],
                                          identb[:]))
                        if ocx == 0:
                            ev = p.op("tensor", fn,
                                      reads=[f"w2f{x}" for x in range(8)]
                                      + ["identb"],
                                      writes=["ptw2"], track=False)
                        else:
                            ev = p.op("tensor", fn, track=False)
                    p.note_read([f"w2f{x}" for x in range(8)] + ["identb"], ev)
                    p.note_write(["ptw2"], ev)
                    for ocx in range(8):
                        p.op("vector",
                             lambda h, par=par, j=j, ocx=ocx: h.tensor_copy(
                                 w2stage[par][:, j * 1024 + ocx * P:
                                              j * 1024 + (ocx + 1) * P],
                                 ptw2[:, ocx * P:(ocx + 1) * P]),
                             reads=["ptw2"], writes=[f"w2st{par}_{j}_{ocx}"])
                p.op("gpsimd", lambda h, sl=sl, par=par: h.dma_start(
                    out=w2t_s[sl], in_=w2stage[par][:]),
                    reads=[f"w2st{par}_{j}_{ocx}" for j in range(2)
                           for ocx in range(8)],
                    writes=[f"w2t_s{sl}"], n=16, sem="W2S")

            for i in range(32):
                if i < 16:
                    mean_chunk(w1d, 2 * i, D, "w1ch", w1ch, "scalar", 4)
                    mean_chunk(w1d, 2 * i + 1, D, "w1ch", w1ch, "scalar", 4)
                if i < 8:
                    mean_chunk(w2d, i, HID, "w2ch", w2ch, "gpsimd", 2,
                               acol=32)
                k_half(i, 0)
                k_half(i, 1)
                if i == 15:
                    mean_final(32, "w1ch", sw1col, c1col, "sw1", "c1")
                if i == 7:
                    mean_final(8, "w2ch", sw2col, c2col, "sw2", "c2",
                               acol=32)
                if i >= 16:
                    w1_half(i - 16, 0)
                    w1_half(i - 16, 1)
                    if i % 2 == 0:
                        w2_quant((i - 16) // 2)
            for sl in range(16):
                w2_tr(sl)

            for i in range(8):
                p.op("gpsimd", lambda h, i=i: h.dma_start(
                    out=v_s[i * 1024:(i + 1) * 1024, :],
                    in_=vv[i * 1024:(i + 1) * 1024, :]),
                    writes=[f"v_s{i}"], n=16, sem=f"VC{i // 4}")

            p.barrier()
            for sl in range(32):
                p.W[f"kt_s{sl}"] = (f"KT{sl // 16}", 16 * 16)
            for sl in range(16):
                p.W[f"w1t_s{sl}"] = ("W1S", 16 * 16)
                p.W[f"w2t_s{sl}"] = ("W2S", 16 * 16)
            for i in range(8):
                p.W[f"v_s{i}"] = (f"VC{i // 4}", 16 * 4)

        def ps(name, shape, dtype=F32):
            return S.enter_context(nc.psum_tensor(name, shape, dtype))

        pb = [ps(f"pb{i}", [P, 512]) for i in range(8)]
        PBN = [f"pb{i}" for i in range(8)]

        scT = sb("scT", [P, 64 * 512], BF16)
        y_full = sb("y_full", [P, 32 * 512], BF16)
        wsy = [sb(f"wsy{i}", [P, 2048], BF16) for i in range(6)]
        wgp = [sb(f"wgp{i}", [P, 2048], BF16) for i in range(5)]
        xnT = sb("xnT", [P, 8 * 512], BF16)
        xqT = sb("xqT", [P, 8 * 514], BF16)
        xln = [sb("xln0", [P, D]), sb("xln1", [P, D])]
        xadd = [sb("xadd0", [P, D]), sb("xadd1", [P, D])]
        qtc = [sb("qtc0", [P, D]), sb("qtc1", [P, D])]
        qtmp = sb("qtmp", [P, 512])
        sqz = [sb("sqz0", [P, 512], BF16), sb("sqz1", [P, 512], BF16),
               sb("sqz2", [P, 512], BF16)]
        y_scr = [sb("y_scr0", [P, 256]), sb("y_scr1", [P, 256])]
        mxq = sb("mxq", [P, 512])
        csbc = sb("csbc", [P, 512])
        csbc2 = sb("csbc2", [P, 512])
        invbc = sb("invbc", [P, 512])
        rows = sb("rows", [1, 1024])
        stat = sb("stat", [P, 32])
        x_halo = sb("x_halo", [2, D])
        xq_halo = sb("xq_halo", [2, D], BF16)
        stat_h = sb("stat_h", [2, 8])

        def bslot(tc, which):
            c0 = tc * 2048 + which * 1024
            names = [f"yf{tc * 4 + 2 * which + q}_{hf}"
                     for q in (0, 1) for hf in (0, 1)]
            return y_full[:, c0:c0 + 1024], names

        def mm_group(fns, bank, reads):
            ev = None
            for i, fn in enumerate(fns):
                if i == 0:
                    ev = p.op("tensor", fn, reads=reads, writes=[bank],
                              track=False)
                else:
                    ev = p.op("tensor", fn, track=False)
            p.note_read(reads, ev)
            p.note_write([bank], ev)
            return ev

        def sy_slot(t, i):
            return (t * 64 + i) % 6

        def gp_slot(t, i):
            return (t * 32 + i) % 5

        def emit_B_ln(t, xq="gpsimd"):
            g0 = t * T
            p.op(xq, lambda h, g0=g0: h.dma_start(
                out=x_halo[:],
                in_=bass.AP(xs.tensor, g0 * D, [[513 * D, 2], [1, D]])),
                writes=["x_halo"], n=16, sem="XH")
            for tc in range(4):
                sl = tc % 2
                xnb, xnn = bslot(tc, 0)
                xqb, xqn = bslot(tc, 1)
                p.op(xq, lambda h, tc=tc, sl=sl, g0=g0: h.dma_start(
                    out=xln[sl][:],
                    in_=xs[1 + g0 + tc * P: 1 + g0 + (tc + 1) * P, :]),
                    writes=[f"xln{sl}"], n=16, sem=f"XL{sl}")
                st = stat[:, tc * 8:tc * 8 + 8]
                sn = [f"st{tc}_{i}" for i in range(8)]
                p.op("vector", lambda h, sl=sl, st=st: h.reduce_sum(
                    st[:, 0:1], xln[sl][:], AX),
                    reads=[f"xln{sl}"], writes=[sn[0]])
                p.op("vector", lambda h, st=st: h.tensor_scalar(
                    st[:, 0:1], st[:, 0:1], 1.0 / D, None, op0=ALU.mult),
                    reads=[sn[0]], writes=[sn[0]])
                p.op("vector", lambda h, sl=sl, st=st: h.tensor_scalar(
                    xln[sl][:], xln[sl][:], st[:, 0:1], None,
                    op0=ALU.subtract),
                    reads=[sn[0]], writes=[f"xln{sl}"])
                p.op("scalar", lambda h, sl=sl, xqb=xqb, st=st: h.activation(
                    xqb[:], xln[sl][:], AF.Square, accum_out=st[:, 1:2]),
                    reads=[f"xln{sl}"], writes=xqn + [sn[1]])
                p.op("vector", lambda h, st=st: h.tensor_scalar(
                    st[:, 1:2], st[:, 1:2], 1.0 / D, EPS,
                    op0=ALU.mult, op1=ALU.add),
                    reads=[sn[1]], writes=[sn[1]])
                p.op("scalar", lambda h, st=st: h.activation(
                    st[:, 1:2], st[:, 1:2], AF.Sqrt),
                    reads=[sn[1]], writes=[sn[1]])
                p.op("vector", lambda h, st=st: h.reciprocal(
                    st[:, 2:3], st[:, 1:2]), reads=[sn[1]], writes=[sn[2]])
                p.op("vector", lambda h, sl=sl, st=st: h.reduce_max(
                    st[:, 3:4], xln[sl][:], AX, apply_absolute_value=True),
                    reads=[f"xln{sl}"], writes=[sn[3]])
                p.op("vector", lambda h, st=st: h.tensor_tensor(
                    st[:, 3:4], st[:, 3:4], st[:, 2:3], op=ALU.mult),
                    reads=[sn[3], sn[2]], writes=[sn[3]])
                p.op("vector", lambda h, st=st: h.tensor_scalar(
                    st[:, 3:4], st[:, 3:4], 1e-5, None, op0=ALU.max),
                    reads=[sn[3]], writes=[sn[3]])
                p.op("vector", lambda h, st=st: h.reciprocal(
                    st[:, 4:5], st[:, 3:4]), reads=[sn[3]], writes=[sn[4]])
                p.op("vector", lambda h, st=st: h.tensor_scalar(
                    st[:, 4:5], st[:, 4:5], 127.0, None, op0=ALU.mult),
                    reads=[sn[4]], writes=[sn[4]])
                p.op("vector", lambda h, st=st: h.tensor_tensor(
                    st[:, 5:6], st[:, 2:3], st[:, 4:5], op=ALU.mult),
                    reads=[sn[2], sn[4]], writes=[sn[5]])
                p.op("vector", lambda h, st=st: h.tensor_tensor(
                    st[:, 6:7], st[:, 3:4], c1col[:], op=ALU.mult),
                    reads=[sn[3], "c1"], writes=[sn[6]])
                p.op("vector", lambda h, sl=sl, xnb=xnb, st=st: h.tensor_scalar(
                    xnb[:], xln[sl][:], st[:, 2:3], None, op0=ALU.mult),
                    reads=[f"xln{sl}", sn[2]], writes=xnn)
                for hh in (0, 1):
                    p.op("vector", lambda h, sl=sl, st=st, hh=hh: h.tensor_scalar(
                        qtmp[:], xln[sl][:, hh * 512:(hh + 1) * 512],
                        st[:, 5:6], MAGIC, op0=ALU.mult, op1=ALU.add),
                        reads=[f"xln{sl}", sn[5]], writes=["qtmp"])
                    p.op("vector", lambda h: h.tensor_scalar(
                        qtmp[:], qtmp[:], MAGIC, 127.0,
                        op0=ALU.subtract, op1=ALU.min),
                        reads=["qtmp"], writes=["qtmp"])
                    p.op("vector", lambda h, xqb=xqb, st=st, hh=hh: h.tensor_scalar(
                        xqb[:, hh * 512:(hh + 1) * 512], qtmp[:], -128.0,
                        st[:, 6:7], op0=ALU.max, op1=ALU.mult),
                        reads=["qtmp", sn[6]], writes=xqn)

            p.op("vector", lambda h: h.reduce_sum(
                stat_h[:, 0:1], x_halo[:], AX), reads=["x_halo"], writes=["sh0"])
            p.op("vector", lambda h: h.tensor_scalar(
                stat_h[:, 0:1], stat_h[:, 0:1], 1.0 / D, None, op0=ALU.mult),
                reads=["sh0"], writes=["sh0"])
            p.op("vector", lambda h: h.tensor_scalar(
                x_halo[:], x_halo[:], stat_h[:, 0:1], None, op0=ALU.subtract),
                reads=["sh0"], writes=["x_halo"])
            p.op("scalar", lambda h: h.activation(
                xq_halo[:], x_halo[:], AF.Square, accum_out=stat_h[:, 1:2]),
                reads=["x_halo"], writes=["xq_halo", "sh1"])
            p.op("vector", lambda h: h.tensor_scalar(
                stat_h[:, 1:2], stat_h[:, 1:2], 1.0 / D, EPS,
                op0=ALU.mult, op1=ALU.add), reads=["sh1"], writes=["sh1"])
            p.op("scalar", lambda h: h.activation(
                stat_h[:, 1:2], stat_h[:, 1:2], AF.Sqrt),
                reads=["sh1"], writes=["sh1"])
            p.op("vector", lambda h: h.reciprocal(
                stat_h[:, 2:3], stat_h[:, 1:2]), reads=["sh1"], writes=["sh2"])
            p.op("vector", lambda h: h.reduce_max(
                stat_h[:, 3:4], x_halo[:], AX, apply_absolute_value=True),
                reads=["x_halo"], writes=["sh3"])
            p.op("vector", lambda h: h.tensor_tensor(
                stat_h[:, 3:4], stat_h[:, 3:4], stat_h[:, 2:3], op=ALU.mult),
                reads=["sh3", "sh2"], writes=["sh3"])
            p.op("vector", lambda h: h.tensor_scalar(
                stat_h[:, 3:4], stat_h[:, 3:4], 1e-5, None, op0=ALU.max),
                reads=["sh3"], writes=["sh3"])
            p.op("vector", lambda h: h.reciprocal(
                stat_h[:, 4:5], stat_h[:, 3:4]), reads=["sh3"], writes=["sh4"])
            p.op("vector", lambda h: h.tensor_scalar(
                stat_h[:, 4:5], stat_h[:, 4:5], 127.0, None, op0=ALU.mult),
                reads=["sh4"], writes=["sh4"])
            p.op("vector", lambda h: h.tensor_tensor(
                stat_h[:, 5:6], stat_h[:, 2:3], stat_h[:, 4:5], op=ALU.mult),
                reads=["sh2", "sh4"], writes=["sh5"])
            p.op("vector", lambda h: h.tensor_tensor(
                stat_h[:, 6:7], stat_h[:, 3:4], c1col[0:2, :], op=ALU.mult),
                reads=["sh3", "c1"], writes=["sh6"])
            p.op("vector", lambda h: h.tensor_scalar(
                x_halo[:], x_halo[:], stat_h[:, 5:6], MAGIC,
                op0=ALU.mult, op1=ALU.add),
                reads=["sh5"], writes=["x_halo"])
            p.op("vector", lambda h: h.tensor_scalar(
                x_halo[:], x_halo[:], MAGIC, 127.0,
                op0=ALU.subtract, op1=ALU.min),
                reads=["x_halo"], writes=["x_halo"])
            p.op("vector", lambda h: h.tensor_scalar(
                xq_halo[:], x_halo[:], -128.0, stat_h[:, 6:7],
                op0=ALU.max, op1=ALU.mult),
                reads=["x_halo", "sh6"], writes=["xq_halo"])

        def emit_B_tr(t, bc0):
            bcnt = bc0
            for tc in range(4):
                for which in (0, 1):
                    src, srcn = bslot(tc, which)
                    dst = xnT if which == 0 else xqT
                    dname = "xnT" if which == 0 else "xqT"
                    off = 0 if which == 0 else 1
                    w = 512 if which == 0 else 514
                    for hg in (0, 1):
                        bk = pb[bcnt % 4]
                        bkn = PBN[bcnt % 4]
                        bcnt += 1
                        ev = None
                        for dq in range(4):
                            dc = hg * 4 + dq
                            fn = (lambda h, src=src, dc=dc, dq=dq, bk=bk:
                                  h.matmul(bk[:, dq * P:(dq + 1) * P],
                                           src[:, dc * P:(dc + 1) * P],
                                           identb[:], start=True, stop=True))
                            if dq == 0:
                                ev = p.op("tensor", fn,
                                          reads=srcn + ["identb"],
                                          writes=[bkn], track=False)
                            else:
                                ev = p.op("tensor", fn, track=False)
                        p.note_read(srcn + ["identb"], ev)
                        p.note_write([bkn], ev)
                        c0 = hg * 4 * w + off + tc * P
                        rl = 8 * w
                        p.op("vector",
                             lambda h, dst=dst, c0=c0, w=w, bk=bk, rl=rl:
                             h.tensor_copy(
                                 bass.AP(dst, c0, [[rl, P], [w, 4], [1, P]]),
                                 bass.AP(bk, 0, [[512, P], [P, 4], [1, P]])),
                             reads=[bkn], writes=[dname])
            ev = None
            for dc in range(8):
                fn = (lambda h, dc=dc: h.matmul(
                    pb[2][:, dc * 2:dc * 2 + 2],
                    xq_halo[:, dc * P:(dc + 1) * P], identb[0:2, 0:2],
                    start=True, stop=True))
                if dc == 0:
                    ev = p.op("tensor", fn, reads=["xq_halo", "identb"],
                              writes=[PBN[2]], track=False)
                else:
                    ev = p.op("tensor", fn, track=False)
            p.note_read(["xq_halo", "identb"], ev)
            p.note_write([PBN[2]], ev)
            for dc in range(8):
                p.op("vector", lambda h, dc=dc: h.tensor_copy(
                    xqT[:, dc * 514:dc * 514 + 1], pb[2][:, dc * 2:dc * 2 + 1]),
                    reads=[PBN[2]], writes=["xqT"])
                p.op("vector", lambda h, dc=dc: h.tensor_copy(
                    xqT[:, dc * 514 + 513:dc * 514 + 514],
                    pb[2][:, dc * 2 + 1:dc * 2 + 2]),
                    reads=[PBN[2]], writes=["xqT"])

        def emit_main(t):
            g0 = t * T

            for ncx in range(64):
                sl = ncx // 2
                if ncx % 2 == 0:
                    ss = sy_slot(t, sl)
                    p.op("sync", lambda h, sl=sl, ss=ss: h.dma_start(
                        out=wsy[ss][:], in_=kt_s[sl]),
                        reads=[f"kt_s{sl}"], writes=[f"wsy{ss}"], n=16,
                        sem=f"SY{ss}")
                ss = sy_slot(t, ncx // 2)
                bank = PBN[ncx % 4]
                fns = []
                for dc in range(8):
                    fns.append(lambda h, ss=ss, ncx=ncx, dc=dc:
                               h.matmul(pb[ncx % 4][:],
                                        wsy[ss][
                                            :, dc * 256 + (ncx % 2) * P:
                                            dc * 256 + (ncx % 2 + 1) * P],
                                        xnT[:, dc * 512:(dc + 1) * 512],
                                        start=(dc == 0), stop=(dc == 7)))
                mm_group(fns, bank, [f"wsy{ss}", "xnT"])
                p.op("vector", lambda h, ncx=ncx: h.tensor_copy(
                    scT[:, ncx * 512:(ncx + 1) * 512], pb[ncx % 4][:]),
                    reads=[bank], writes=[f"sc{ncx}"])
                p.op("scalar", lambda h, ncx=ncx: h.activation(
                    sqz[ncx % 3][:], scT[:, ncx * 512:(ncx + 1) * 512],
                    AF.Square),
                    reads=[f"sc{ncx}"], writes=[f"sqz{ncx % 3}"])
                if ncx >= 2:
                    nl = ncx - 2
                    p.op("tensor", lambda h, nl=nl: h.matmul(
                        pb[4][:], ones_sq[:], sqz[nl % 3][:],
                        start=(nl == 0), stop=False),
                        reads=[f"sqz{nl % 3}", "onessq"], writes=[PBN[4]])
            for nl in (62, 63):
                p.op("tensor", lambda h, nl=nl: h.matmul(
                    pb[4][:], ones_sq[:], sqz[nl % 3][:],
                    start=False, stop=(nl == 63)),
                    reads=[f"sqz{nl % 3}", "onessq"], writes=[PBN[4]])

            p.op("vector", lambda h: h.tensor_copy(
                rows[0:1, 0:512], pb[4][0:1, :]),
                reads=[PBN[4]], writes=["rows"])
            p.op("scalar", lambda h: h.activation(
                rows[0:1, 0:512], rows[0:1, 0:512], AF.Sqrt),
                reads=["rows"], writes=["rows"])
            p.op("vector", lambda h: h.reciprocal(
                rows[0:1, 0:512], rows[0:1, 0:512]),
                reads=["rows"], writes=["rows"])
            p.op("vector", lambda h: h.tensor_scalar(
                rows[0:1, 0:512], rows[0:1, 0:512], SQRT_N, None,
                op0=ALU.mult), reads=["rows"], writes=["rows"])

            def minmax(jq):
                if jq == 0:
                    p.op("vector", lambda h: h.tensor_copy(
                        mxq[:], y_full[:, 0:512]),
                        reads=["yf0_0", "yf0_1"], writes=["mxq0", "mxq1"])
                    p.op("vector", lambda h: h.tensor_copy(
                        pb[7][:], y_full[:, 0:512]),
                        reads=["yf0_0", "yf0_1"], writes=[PBN[7]])
                else:
                    p.op("vector", lambda h, jq=jq: h.tensor_tensor(
                        mxq[:], mxq[:],
                        y_full[:, jq * 512:(jq + 1) * 512], op=ALU.max),
                        reads=[f"yf{jq}_0", f"yf{jq}_1", "mxq0", "mxq1"],
                        writes=["mxq0", "mxq1"])
                    p.op("vector", lambda h, jq=jq: h.tensor_tensor(
                        pb[7][:], pb[7][:],
                        y_full[:, jq * 512:(jq + 1) * 512], op=ALU.min),
                        reads=[f"yf{jq}_0", f"yf{jq}_1", PBN[7]],
                        writes=[PBN[7]])

            for jc in range(32):
                sl = jc // 2
                if jc == 1:
                    p.op("tensor", lambda h: h.matmul(
                        pb[5][:], ones_row[:], rows[0:1, 0:512],
                        start=True, stop=True),
                        reads=["rows", "onesr"], writes=[PBN[5]])
                    p.op("vector", lambda h: h.tensor_copy(csbc[:], pb[5][:]),
                         reads=[PBN[5]], writes=["csbc"])
                if jc % 2 == 0:
                    ws = sy_slot(t, 32 + sl)
                    p.op("sync", lambda h, sl=sl, ws=ws: h.dma_start(
                        out=wsy[ws][:], in_=w1t_s[sl]),
                        reads=[f"w1t_s{sl}"], writes=[f"wsy{ws}"], n=16,
                        sem=f"SY{ws}")
                ws = sy_slot(t, 32 + jc // 2)
                for hf in range(2):
                    ph = (2 * jc + hf) % 6
                    bank = PBN[ph]
                    fns = []
                    for dc in range(8):
                        fns.append(lambda h, ws=ws, jc=jc, dc=dc, hf=hf, ph=ph:
                                   h.matmul(pb[ph][:, 0:258],
                                            wsy[ws][
                                                :, (jc % 2) * 1024 + dc * P:
                                                (jc % 2) * 1024 + (dc + 1) * P],
                                            xqT[:, dc * 514 + hf * 256:
                                                dc * 514 + hf * 256 + 258],
                                            start=(dc == 0), stop=(dc == 7)))
                    mm_group(fns, bank, [f"wsy{ws}", "xqT"])
                    p.op("vector", lambda h, ph=ph, jc=jc: h.tensor_scalar(
                        y_scr[ph % 2][:], pb[ph][:, 1:257],
                        taps[:, jc * 3 + 1:jc * 3 + 2], None, op0=ALU.mult),
                        reads=[bank, "taps"], writes=[f"ysc{ph % 2}"])
                    p.op("vector", lambda h, ph=ph, jc=jc: h.scalar_tensor_tensor(
                        y_scr[ph % 2][:], pb[ph][:, 0:256],
                        taps[:, jc * 3:jc * 3 + 1], y_scr[ph % 2][:],
                        op0=ALU.mult, op1=ALU.add),
                        reads=[bank, "taps"], writes=[f"ysc{ph % 2}"])
                    p.op("vector", lambda h, ph=ph, jc=jc: h.scalar_tensor_tensor(
                        y_scr[ph % 2][:], pb[ph][:, 2:258],
                        taps[:, jc * 3 + 2:jc * 3 + 3], y_scr[ph % 2][:],
                        op0=ALU.mult, op1=ALU.add),
                        reads=[bank, "taps"], writes=[f"ysc{ph % 2}"])
                    p.op("scalar", lambda h, ph=ph, jc=jc, hf=hf: h.activation(
                        y_full[:, jc * 512 + hf * 256:jc * 512 + hf * 256 + 256],
                        y_scr[ph % 2][:], AF.Silu, bias=bias_c[:, jc:jc + 1]),
                        reads=[f"ysc{ph % 2}", "bias"], writes=[f"yf{jc}_{hf}"])
                if jc >= 1:
                    minmax(jc - 1)
                if 27 <= jc <= 31:
                    sv = jc - 27
                    gs = gp_slot(t, sv)
                    p.op("gpsimd", lambda h, sv=sv, gs=gs: h.dma_start(
                        out=bass.AP(wgp[gs], 0,
                                    [[2048, P], [1024, 2], [1, 1024]]),
                        in_=bass.AP(v_s.tensor, sv * 256 * D,
                                    [[D, P], [P * D, 2], [1, D]])),
                        reads=[f"v_s{sv // 4}"], writes=[f"wgp{gs}"], n=16,
                        sem=f"GP{gs}")
            minmax(31)
            p.op("vector", lambda h: h.tensor_scalar(
                pb[7][:], pb[7][:], -1.0, None, op0=ALU.mult),
                reads=[PBN[7]], writes=[PBN[7]])
            p.op("vector", lambda h: h.tensor_tensor(
                mxq[:], mxq[:], pb[7][:], op=ALU.max),
                reads=[PBN[7], "mxq0", "mxq1"],
                writes=["mxq0", "mxq1"])

            def hq_mxqT():
                ev = None
                for tc in range(4):
                    fn = (lambda h, tc=tc: h.transpose(
                        pb[6][:, tc * P:(tc + 1) * P],
                        mxq[:, tc * P:(tc + 1) * P], ident[:]))
                    if tc == 0:
                        ev = p.op("tensor", fn,
                                  reads=["mxq0", "mxq1", "ident"],
                                  writes=[PBN[6]], track=False)
                    else:
                        ev = p.op("tensor", fn, track=False)
                p.note_read(["mxq0", "mxq1", "ident"], ev)
                p.note_write([PBN[6]], ev)
                for tc in range(4):
                    st = stat[:, tc * 8:tc * 8 + 8]
                    p.op("vector", lambda h, tc=tc, st=st: h.reduce_max(
                        st[:, 7:8], pb[6][:, tc * P:(tc + 1) * P], AX),
                        reads=[PBN[6]], writes=[f"st{tc}_7"])

            def hq_statT():
                ev = None
                for tc in range(4):
                    ev = p.op("tensor", lambda h, tc=tc: h.transpose(
                        pb[7][0:1, tc * P:(tc + 1) * P],
                        stat[:, tc * 8 + 7:tc * 8 + 8], ident[:]),
                        reads=[f"st{tc}_7", "ident"],
                        writes=[PBN[7]] if tc == 0 else (), track=(tc == 0))
                p.note_write([PBN[7]], ev)
                p.note_read([f"st{tc}_7" for tc in range(4)], ev)
                p.op("vector", lambda h: h.tensor_scalar(
                    rows[0:1, 512:1024], pb[7][0:1, 0:512], 1e-5, None,
                    op0=ALU.max),
                    reads=[PBN[7]], writes=["rowsi"])
                p.op("vector", lambda h: h.reciprocal(
                    rows[0:1, 0:512], rows[0:1, 512:1024]),
                    reads=["rowsi"], writes=["rows"])
                p.op("vector", lambda h: h.tensor_scalar(
                    rows[0:1, 0:512], rows[0:1, 0:512], 127.0, None,
                    op0=ALU.mult),
                    reads=["rows"], writes=["rows"])
                p.op("vector", lambda h: h.tensor_scalar(
                    rows[0:1, 512:1024], rows[0:1, 512:1024], c2col[0:1, 0:1],
                    None, op0=ALU.mult),
                    reads=["rowsi", "c2"], writes=["rowsi"])

            def hq_bc1():
                p.op("tensor", lambda h: h.matmul(
                    pb[6][:], ones_row[:], rows[0:1, 0:512],
                    start=True, stop=True), reads=["rows", "onesr"],
                    writes=[PBN[6]])
                p.op("vector", lambda h: h.tensor_copy(csbc2[:], pb[6][:]),
                     reads=[PBN[6]], writes=["csbc2"])

            def hq_bc2():
                p.op("tensor", lambda h: h.matmul(
                    pb[7][:], ones_row[:], rows[0:1, 512:1024],
                    start=True, stop=True), reads=["rowsi", "onesr"],
                    writes=[PBN[7]])
                p.op("vector", lambda h: h.tensor_copy(
                    invbc[:, 0:512], pb[7][:]), reads=[PBN[7]],
                    writes=["invbc"])

            def emit_F(f):
                p.op("vector", lambda h, f=f: h.tensor_tensor(
                    sqz[f % 3][:], scT[:, f * 512:(f + 1) * 512],
                    csbc[:], op=ALU.mult),
                    reads=[f"sc{f}", "csbc"], writes=[f"sqz{f % 3}"])
                p.op("scalar", lambda h, f=f: h.activation(
                    scT[:, f * 512:(f + 1) * 512], sqz[f % 3][:],
                    AF.Gelu),
                    reads=[f"sqz{f % 3}"], writes=[f"sc{f}"])

            emit_F(0)
            emit_F(1)
            hq_steps = {1: hq_mxqT, 3: hq_statT, 5: hq_bc1, 7: hq_bc2}
            for ncx in range(8):
                for f in (2 * ncx + 2, 2 * ncx + 3):
                    if f < 64:
                        emit_F(f)
                sl = ncx // 2
                if ncx % 2 == 0 and sl >= 5:
                    gs = gp_slot(t, sl)
                    p.op("gpsimd", lambda h, sl=sl, gs=gs: h.dma_start(
                        out=bass.AP(wgp[gs], 0,
                                    [[2048, P], [1024, 2], [1, 1024]]),
                        in_=bass.AP(v_s.tensor, sl * 256 * D,
                                    [[D, P], [P * D, 2], [1, D]])),
                        reads=[f"v_s{sl // 4}"], writes=[f"wgp{gs}"], n=16,
                        sem=f"GP{gs}")
                gs = gp_slot(t, ncx // 2)
                ev = None
                for ol in range(6):
                    fn = (lambda h, gs=gs, ncx=ncx, ol=ol:
                          h.matmul(pb[ol][:],
                                   wgp[gs][
                                       :, (ncx % 2) * 1024 + ol * P:
                                       (ncx % 2) * 1024 + (ol + 1) * P],
                                   scT[:, ncx * 512:(ncx + 1) * 512],
                                   start=(ncx == 0), stop=False))
                    rd = [f"wgp{gs}", f"sc{ncx}"] if ol == 0 else ()
                    wr = [PBN[ol]] if ncx == 0 else ()
                    ev = p.op("tensor", fn, reads=rd, writes=wr,
                              track=False)
                p.note_read([f"wgp{gs}", f"sc{ncx}"], ev)
                if ncx in hq_steps:
                    hq_steps[ncx]()

            for ncx in range(8):
                gs = gp_slot(t, ncx // 2)
                ev = None
                for ol in (6, 7):
                    fn = (lambda h, gs=gs, ncx=ncx, ol=ol:
                          h.matmul(pb[ol][:],
                                   wgp[gs][
                                       :, (ncx % 2) * 1024 + ol * P:
                                       (ncx % 2) * 1024 + (ol + 1) * P],
                                   scT[:, ncx * 512:(ncx + 1) * 512],
                                   start=(ncx == 0), stop=False))
                    rd = [f"wgp{gs}", f"sc{ncx}"] if ol == 6 else ()
                    wr = [PBN[ol]] if ncx == 0 else ()
                    ev = p.op("tensor", fn, reads=rd, writes=wr,
                              track=False)
                p.note_read([f"wgp{gs}", f"sc{ncx}"], ev)

            last_ev = [None] * 8
            def emit_quant(jq):
                p.op("vector", lambda h, jq=jq: h.tensor_tensor(
                    mxq[:], y_full[:, jq * 512:(jq + 1) * 512], csbc2[:],
                    op=ALU.mult),
                    reads=[f"yf{jq}_0", f"yf{jq}_1", "csbc2"],
                    writes=["mxq0", "mxq1"])
                p.op("vector", lambda h: h.tensor_scalar(
                    mxq[:], mxq[:], 127.0, -128.0, op0=ALU.min,
                    op1=ALU.max),
                    reads=["mxq0", "mxq1"], writes=["mxq0", "mxq1"])
                p.op("vector", lambda h: h.tensor_scalar(
                    mxq[:], mxq[:], MAGIC, MAGIC, op0=ALU.add,
                    op1=ALU.subtract),
                    reads=["mxq0", "mxq1"], writes=["mxq0", "mxq1"])
                p.op("vector", lambda h, jq=jq: h.tensor_tensor(
                    y_full[:, jq * 512:(jq + 1) * 512], mxq[:],
                    invbc[:, 0:512], op=ALU.mult),
                    reads=["mxq0", "mxq1", "invbc"],
                    writes=[f"yf{jq}_0", f"yf{jq}_1"])

            for ncx in range(8, 64):
                for f in (2 * ncx + 2, 2 * ncx + 3):
                    if f < 64:
                        emit_F(f)
                if 8 <= ncx < 40:
                    emit_quant(ncx - 8)
                sl = ncx // 2
                if ncx % 2 == 0 and sl >= 5:
                    gs = gp_slot(t, sl)
                    p.op("gpsimd", lambda h, sl=sl, gs=gs: h.dma_start(
                        out=bass.AP(wgp[gs], 0,
                                    [[2048, P], [1024, 2], [1, 1024]]),
                        in_=bass.AP(v_s.tensor, sl * 256 * D,
                                    [[D, P], [P * D, 2], [1, D]])),
                        reads=[f"v_s{sl // 4}"], writes=[f"wgp{gs}"], n=16,
                        sem=f"GP{gs}")
                gs = gp_slot(t, ncx // 2)
                ev = None
                for ol in range(8):
                    fn = (lambda h, gs=gs, ncx=ncx, ol=ol:
                          h.matmul(pb[ol][:],
                                   wgp[gs][
                                       :, (ncx % 2) * 1024 + ol * P:
                                       (ncx % 2) * 1024 + (ol + 1) * P],
                                   scT[:, ncx * 512:(ncx + 1) * 512],
                                   start=False, stop=False))
                    rd = [f"wgp{gs}", f"sc{ncx}"] if ol == 0 else ()
                    ev = p.op("tensor", fn, reads=rd, track=False)
                p.note_read([f"wgp{gs}", f"sc{ncx}"], ev)
            for jc in range(32):
                sl2 = jc // 2
                if jc % 2 == 0:
                    ss = sy_slot(t, 48 + sl2)
                    p.op("sync", lambda h, sl2=sl2, ss=ss: h.dma_start(
                        out=wsy[ss][:], in_=w2t_s[sl2]),
                        reads=[f"w2t_s{sl2}"], writes=[f"wsy{ss}"], n=16,
                        sem=f"SY{ss}")
                ss = sy_slot(t, 48 + jc // 2)
                ev = None
                for ol in range(8):
                    fn = (lambda h, ss=ss, jc=jc, ol=ol:
                          h.matmul(pb[ol][:],
                                   wsy[ss][
                                       :, (jc % 2) * 1024 + ol * P:
                                       (jc % 2) * 1024 + (ol + 1) * P],
                                   y_full[:, jc * 512:(jc + 1) * 512],
                                   start=False, stop=(jc == 31)))
                    ev = p.op("tensor", fn,
                              reads=[f"wsy{ss}", f"yf{jc}_0", f"yf{jc}_1"]
                              if ol == 0 else (), track=False)
                    if jc == 31:
                        last_ev[ol] = ev
                p.note_read([f"wsy{ss}", f"yf{jc}_0", f"yf{jc}_1"], ev)
            for ol in range(8):
                p.note_write([PBN[ol]], last_ev[ol])

        def emit_evac(t):
            g0 = t * T
            xa = [xadd[0], xadd[1], xadd[0], xadd[1]]
            xan = ["xadd0", "xadd1", "xadd0", "xadd1"]
            for tc in range(2):
                p.op("gpsimd", lambda h, tc=tc, g0=g0: h.dma_start(
                    out=xa[tc][:],
                    in_=xs[1 + g0 + tc * P: 1 + g0 + (tc + 1) * P, :]),
                    writes=[xan[tc]], n=16, sem=f"XA{tc % 2}")
            for ol in range(8):
                stg = mxq if ol % 2 == 0 else csbc
                stgn = ["mxq0", "mxq1"] if ol % 2 == 0 else ["csbc"]
                p.op("vector", lambda h, ol=ol, stg=stg: h.tensor_copy(
                    stg[:], pb[ol][:]),
                    reads=[PBN[ol]], writes=stgn)
                ev = None
                for tc in range(4):
                    fn = (lambda h, ol=ol, tc=tc, stg=stg: h.transpose(
                        pb[ol][:, tc * P:(tc + 1) * P],
                        stg[:, tc * P:(tc + 1) * P], ident[:]))
                    if tc == 0:
                        ev = p.op("tensor", fn, reads=stgn + ["ident"],
                                  writes=[PBN[ol]], track=False)
                    else:
                        ev = p.op("tensor", fn, track=False)
                p.note_read(stgn + ["ident"], ev)
                p.note_write([PBN[ol]], ev)
            for tc in range(4):
                for ol in range(8):
                    p.op("vector",
                         lambda h, tc=tc, ol=ol: h.scalar_tensor_tensor(
                             qtc[tc % 2][:, ol * P:(ol + 1) * P],
                             pb[ol][:, tc * P:(tc + 1) * P], 1.0,
                             xa[tc][:, ol * P:(ol + 1) * P],
                             op0=ALU.mult, op1=ALU.add),
                         reads=[PBN[ol], xan[tc]],
                         writes=[f"qtc{tc % 2}"])
                p.op("scalar", lambda h, tc=tc, g0=g0: h.dma_start(
                    out=outd[g0 + tc * P:g0 + (tc + 1) * P, :],
                    in_=qtc[tc % 2][:]),
                    reads=[f"qtc{tc % 2}"], writes=[f"o{t}_{tc}"],
                    n=16, sem=f"VO{tc % 2}")
                if tc < 2:
                    p.op("gpsimd", lambda h, tc=tc, g0=g0: h.dma_start(
                        out=xa[tc + 2][:],
                        in_=xs[1 + g0 + (tc + 2) * P: 1 + g0 + (tc + 3) * P, :]),
                        writes=[xan[tc + 2]], n=16, sem=f"XA{tc % 2}")

        emit_B_ln(0, xq="sync")
        emit_B_tr(0, 0)
        for t in range(n_tiles):
            emit_main(t)
            if t + 1 < n_tiles:
                emit_B_ln(t + 1)
            emit_evac(t)
            if t + 1 < n_tiles:
                emit_B_tr(t + 1, (t + 1) % 2)

        sync_sems = [f"SY{i}" for i in range(6)] + ["KC0", "KC1",
                     "CS0", "CS1", "CS2", "KT0", "KT1"]
        gp_sems = ([f"GP{i}" for i in range(5)]
                   + ["XL0", "XL1", "XH", "XA0", "XA1",
                      "WD0", "WD1", "W2S", "VC0", "VC1"])
        sc_sems = ["WC0", "WC1", "W1S", "VO0", "VO1"]
        p.wait("sync", [(s, p.cnt.get(s, 0)) for s in sync_sems])
        p.wait("gpsimd", [(s, p.cnt.get(s, 0)) for s in gp_sems])
        p.wait("scalar", [(s, p.cnt.get(s, 0)) for s in sc_sems])

        block = S.enter_context(nc.Block())
        p.emit(block, semh)

    return nc


def kernel(L0_attn_q, L0_attn_k, L0_attn_v, L0_ffn_w1, L0_ffn_conv_w,
           L0_ffn_conv_b, L0_ffn_w2):
    x = np.ascontiguousarray(np.asarray(L0_attn_q, dtype=np.float32))
    k = np.ascontiguousarray(np.asarray(L0_attn_k, dtype=np.float32))
    v = np.ascontiguousarray(np.asarray(L0_attn_v, dtype=np.float32))
    w1 = np.ascontiguousarray(np.asarray(L0_ffn_w1, dtype=np.float32))
    cw = np.ascontiguousarray(
        np.asarray(L0_ffn_conv_w, dtype=np.float32).reshape(HID, 3))
    cb = np.ascontiguousarray(np.asarray(L0_ffn_conv_b, dtype=np.float32))
    w2 = np.ascontiguousarray(np.asarray(L0_ffn_w2, dtype=np.float32))
    ident = np.eye(P, dtype=np.float32)

    zrow = np.zeros((1, D), np.float32)
    in_maps = []
    for c in range(8):
        b, hh = c // 2, c % 2
        tok0 = hh * TOK
        left = x[b, tok0 - 1:tok0] if hh == 1 else zrow
        right = x[b, tok0 + TOK:tok0 + TOK + 1] if hh == 0 else zrow
        xsl = np.concatenate([left, x[b, tok0:tok0 + TOK], right], axis=0)
        in_maps.append(dict(xs=np.ascontiguousarray(xsl), kk=k, vv=v, w1=w1,
                            w2=w2, cw=cw, cb=cb, ident=ident))

    nc = build_nc()
    res = run_bass_kernel_spmd(nc, in_maps, core_ids=list(range(8)))
    outp = np.empty((4, 8192, D), np.float32)
    for c in range(8):
        b, hh = c // 2, c % 2
        outp[b, hh * TOK:(hh + 1) * TOK] = np.asarray(res.results[c]["out"])
    return outp


# version 16 (final)
# speedup vs baseline: 1.1936x; 1.0033x over previous
import numpy as np
import concourse.bass as bass
import concourse.mybir as mybir
from concourse.bass_utils import run_bass_kernel_spmd
from contextlib import ExitStack

F32 = mybir.dt.float32
BF16 = mybir.dt.bfloat16
AF = mybir.ActivationFunctionType
ALU = mybir.AluOpType
AX = mybir.AxisListType.X

MAGIC = float(np.float32(3 * 2.0 ** 22))
D = 1024
NB = 8192
HID = 4096
TOK = 4096
T = 512
NT = TOK // T
SQRT_N = float(np.float32(np.sqrt(np.float32(NB))))
EPS = 1e-5
P = 128

ENGS = ("sync", "vector", "scalar", "tensor", "gpsimd")
ESEM = {"sync": "SYD", "vector": "VS", "scalar": "SS", "tensor": "TS",
        "gpsimd": "GSD"}


class Prog:

    def __init__(self):
        self.q = {e: [] for e in ENGS}
        self.cnt = {}
        self.wc = {e: {} for e in ENGS}
        self.W = {}
        self.R = {}

    def op(self, eng, fn, reads=(), writes=(), extra=(), sem=None, n=1,
           track=True):
        sem = sem if sem is not None else ESEM[eng]
        waits = []
        for b in reads:
            waits.append(self.W.get(b))
        for b in writes:
            waits.append(self.W.get(b))
            waits.extend(self.R.get(b, ()))
        waits.extend(extra)
        wl = []
        for w in waits:
            if w is None:
                continue
            s, v = w
            if v <= 0:
                continue
            assert v <= self.cnt.get(s, 0), f"wait on future {s}={v}"
            if self.wc[eng].get(s, 0) >= v:
                continue
            self.wc[eng][s] = v
            wl.append((s, v))
        self.cnt[sem] = self.cnt.get(sem, 0) + n
        ev = (sem, self.cnt[sem])
        self.q[eng].append((fn, tuple(wl), sem, n))
        if track:
            for b in reads:
                self.R.setdefault(b, []).append(ev)
            for b in writes:
                self.W[b] = ev
                self.R[b] = []
        return ev

    def wait(self, eng, evs):
        wl = []
        for w in evs:
            if w is None:
                continue
            s, v = w
            assert v <= self.cnt.get(s, 0)
            if self.wc[eng].get(s, 0) >= v:
                continue
            self.wc[eng][s] = v
            wl.append((s, v))
        if wl:
            self.q[eng].append((None, tuple(wl), None, 0))

    def note_read(self, bufs, ev):
        for b in bufs:
            self.R.setdefault(b, []).append(ev)

    def note_write(self, bufs, ev):
        for b in bufs:
            self.W[b] = ev
            self.R[b] = []

    def barrier(self):
        tgt = self.cnt.get("BAR", 0) + len(ENGS)
        for e in ENGS:
            self.op(e, lambda h: h.nop(), sem="BAR", track=False)
        for e in ENGS:
            self.wait(e, [("BAR", tgt)])

    def emit(self, block, semh):
        decos = {"sync": block.sync, "vector": block.vector,
                 "scalar": block.scalar, "tensor": block.tensor,
                 "gpsimd": block.gpsimd}
        for e in ENGS:
            q = self.q[e]

            def body(h, q=q):
                for fn, waits, sem, n in q:
                    for (s, v) in waits:
                        h.wait_ge(semh[s], v)
                    if fn is not None:
                        ins = fn(h)
                        if sem is not None:
                            ins.then_inc(semh[sem], n)
            decos[e](body)


def build_nc(n_tiles=NT):
    nc = bass.Bass("TRN2", target_bir_lowering=False)

    xs = nc.dram_tensor("xs", [TOK + 2, D], F32, kind="ExternalInput").ap()
    kk = nc.dram_tensor("kk", [NB, D], F32, kind="ExternalInput").ap()
    vv = nc.dram_tensor("vv", [NB, D], F32, kind="ExternalInput").ap()
    w1d = nc.dram_tensor("w1", [HID, D], F32, kind="ExternalInput").ap()
    w2d = nc.dram_tensor("w2", [D, HID], F32, kind="ExternalInput").ap()
    cwd = nc.dram_tensor("cw", [HID, 3], F32, kind="ExternalInput").ap()
    cbd = nc.dram_tensor("cb", [HID], F32, kind="ExternalInput").ap()
    id_in = nc.dram_tensor("ident", [P, P], F32, kind="ExternalInput").ap()
    outd = nc.dram_tensor("out", [TOK, D], F32, kind="ExternalOutput").ap()

    kt_s = nc.dram_tensor("kt_s", [32, P, 2048], BF16, kind="Internal").ap()
    v_s = nc.dram_tensor("v_s", [NB, D], BF16, kind="Internal").ap()
    w1t_s = nc.dram_tensor("w1t_s", [16, P, 2048], BF16, kind="Internal").ap()
    w2t_s = nc.dram_tensor("w2t_s", [16, P, 2048], BF16, kind="Internal").ap()

    p = Prog()

    with ExitStack() as S:
        def sb(name, shape, dtype=F32):
            return S.enter_context(nc.sbuf_tensor(name, shape, dtype))

        semh = {}
        sem_names = (["VS", "SS", "TS", "BAR"]
                     + [f"SY{i}" for i in range(6)]
                     + [f"GP{i}" for i in range(5)]
                     + ["XL0", "XL1", "XH", "XA0", "XA1", "VO0", "VO1",
                        "KC0", "KC1", "WC0", "WC1", "WD0", "WD1",
                        "CS0", "CS1", "CS2", "KT0", "KT1", "W1S", "W2S",
                        "VC0", "VC1"])
        for s_name in sem_names:
            semh[s_name] = S.enter_context(nc.semaphore(s_name))

        ident = sb("identc", [P, P])
        identb = sb("identbc", [P, P], BF16)
        ones_colf = sb("ones_colf", [P, 1])
        ones_sq = sb("ones_sq", [P, P], BF16)
        ones_row = sb("ones_row", [1, P])
        taps = sb("tapsc", [P, 96])
        bias_c = sb("bias_c", [P, 32])
        c1col = sb("c1col", [P, 1])
        c2col = sb("c2col", [P, 1])
        sw1col = sb("sw1col", [P, 1])
        sw2col = sb("sw2col", [P, 1])

        p.op("sync", lambda h: h.dma_start(out=ident[:], in_=id_in),
             writes=["ident"], n=16, sem="CS0")
        p.op("sync", lambda h: h.dma_start(
            out=taps[:], in_=bass.AP(cwd.tensor, 0, [[3, P], [384, 32], [1, 3]])),
            writes=["taps"], n=16, sem="CS1")
        p.op("sync", lambda h: h.dma_start(
            out=bias_c[:], in_=bass.AP(cbd.tensor, 0, [[1, P], [P, 32]]),
            allow_slow_non_contiguous=True),
            writes=["bias"], n=16, sem="CS2")
        p.op("vector", lambda h: h.tensor_copy(identb[:], ident[:]),
             reads=["ident"], writes=["identb"])
        p.op("vector", lambda h: h.memset(ones_colf[:], 1.0), writes=["onesf"])
        p.op("vector", lambda h: h.memset(ones_sq[:], 1.0), writes=["onessq"])
        p.op("vector", lambda h: h.memset(ones_row[:], 1.0), writes=["onesr"])

        with ExitStack() as SU:
            def sbu(name, shape, dtype=F32):
                return SU.enter_context(nc.sbuf_tensor(name, shape, dtype))

            def psu(name, shape, dtype=F32):
                return SU.enter_context(nc.psum_tensor(name, shape, dtype))

            kch = [sbu(f"kch{i}", [P, D]) for i in range(4)]
            kbf = [sbu("kbf0", [P, D], BF16), sbu("kbf1", [P, D], BF16)]
            kstage = [sbu("kstage0", [P, 2048], BF16),
                      sbu("kstage1", [P, 2048], BF16)]
            ptk = psu("ptk", [P, 1024], BF16)
            w1ch = [sbu(f"w1ch{i}", [P, D]) for i in range(4)]
            w1q = sbu("w1q", [P, D])
            w1b = [sbu("w1b0", [P, D], BF16), sbu("w1b1", [P, D], BF16)]
            w1stage = [sbu("w1stage0", [P, 2048], BF16),
                       sbu("w1stage1", [P, 2048], BF16)]
            ptw1 = psu("ptw1", [P, 1024], BF16)
            w2ch = [sbu("w2ch0", [P, HID]), sbu("w2ch1", [P, HID])]
            w2q = sbu("w2q", [P, HID])
            w2full = sbu("w2full", [P, 8 * HID], BF16)
            w2stage = [sbu("w2stage0", [P, 2048], BF16),
                       sbu("w2stage1", [P, 2048], BF16)]
            ptw2 = psu("ptw2", [P, 1024], BF16)
            acc1 = sbu("acc1s", [P, 40])
            accw = sbu("accws", [P, 1])
            rowt = sbu("rowts", [1, 2])
            pmean = psu("pmean", [1, 2])
            pbcs = psu("pbcs", [P, 2])

            def mean_chunk(wd, i, width, chname, chbufs, queue, nring,
                           acol=0):
                p.op(queue, lambda h, i=i: h.dma_start(
                    out=chbufs[i % nring][:, 0:width],
                    in_=wd[i * P:(i + 1) * P, :]),
                    writes=[f"{chname}{i % nring}"], n=16,
                    sem=(f"WC{i % 2}" if queue == "scalar"
                         else f"WD{i % 2}"))
                p.op("vector", lambda h, i=i, acol=acol: h.reduce_sum(
                    acc1[:, acol + i:acol + i + 1],
                    chbufs[i % nring][:, 0:width], AX,
                    apply_absolute_value=True),
                    reads=[f"{chname}{i % nring}"], writes=[f"{chname}ac{i}"])

            def mean_final(nchunks, chname, sw_col, c_col, swn, cn, acol=0):
                p.op("vector", lambda h, n=nchunks, acol=acol: h.reduce_sum(
                    accw[:], acc1[:, acol:acol + n], AX),
                    reads=[f"{chname}ac{i}" for i in range(nchunks)],
                    writes=["accw"])
                p.op("tensor", lambda h: h.matmul(
                    pmean[0:1, 0:1], ones_colf[:], accw[:],
                    start=True, stop=True),
                    reads=["accw", "onesf"], writes=["pmean"])
                p.op("vector", lambda h: h.tensor_scalar(
                    rowt[0:1, 0:1], pmean[0:1, 0:1], 1.0 / (HID * D), 1e-5,
                    op0=ALU.mult, op1=ALU.max),
                    reads=["pmean"], writes=["rowt0"])
                p.op("vector", lambda h: h.reciprocal(
                    rowt[0:1, 1:2], rowt[0:1, 0:1]),
                    reads=["rowt0"], writes=["rowt1"])
                p.op("tensor", lambda h: h.matmul(
                    pbcs[:, 0:1], ones_row[:], rowt[0:1, 1:2],
                    start=True, stop=True),
                    reads=["rowt1", "onesr"], writes=["pbcs0"])
                p.op("tensor", lambda h: h.matmul(
                    pbcs[:, 1:2], ones_row[:], rowt[0:1, 0:1],
                    start=True, stop=True),
                    reads=["rowt0", "onesr"], writes=["pbcs1"])
                p.op("vector", lambda h: h.tensor_copy(sw_col[:], pbcs[:, 0:1]),
                     reads=["pbcs0"], writes=[swn])
                p.op("vector", lambda h: h.tensor_scalar(
                    c_col[:], pbcs[:, 1:2], 1.0 / 127.0, None, op0=ALU.mult),
                    reads=["pbcs1"], writes=[cn])

            def k_half(sl, j):
                kcn = sl * 2 + j
                bp = kcn % 4
                par = sl % 2
                p.op("sync", lambda h, kcn=kcn, bp=bp: h.dma_start(
                    out=kch[bp][:], in_=kk[kcn * P:(kcn + 1) * P, :]),
                    writes=[f"kch{bp}"], n=16, sem=f"KC{kcn % 2}")
                p.op("vector", lambda h, bp=bp: h.tensor_copy(
                    kbf[bp % 2][:], kch[bp][:]),
                    reads=[f"kch{bp}"], writes=[f"kbf{bp % 2}"])
                ev = None
                for dc in range(8):
                    fn = (lambda h, bp=bp, dc=dc:
                          h.transpose(ptk[:, dc * P:(dc + 1) * P],
                                      kbf[bp % 2][:, dc * P:(dc + 1) * P],
                                      identb[:]))
                    if dc == 0:
                        ev = p.op("tensor", fn,
                                  reads=[f"kbf{bp % 2}", "identb"],
                                  writes=["ptk"], track=False)
                    else:
                        ev = p.op("tensor", fn, track=False)
                p.note_read([f"kbf{bp % 2}", "identb"], ev)
                p.note_write(["ptk"], ev)
                for dc in range(8):
                    p.op("vector",
                         lambda h, par=par, dc=dc, j=j: h.tensor_copy(
                             kstage[par][:, dc * 256 + j * P:
                                         dc * 256 + (j + 1) * P],
                             ptk[:, dc * P:(dc + 1) * P]),
                         reads=["ptk"], writes=[f"kst{par}_{j}_{dc}"])
                if j == 1:
                    p.op("sync", lambda h, sl=sl, par=par: h.dma_start(
                        out=kt_s[sl], in_=kstage[par][:]),
                        reads=[f"kst{par}_{jj}_{dc}" for jj in range(2)
                               for dc in range(8)],
                        writes=[f"kt_s{sl}"], n=16, sem=f"KT{sl // 16}")

            def w1_half(sl, j):
                jc = sl * 2 + j
                bp = jc % 4
                par = sl % 2
                p.op("scalar", lambda h, jc=jc, bp=bp: h.dma_start(
                    out=w1ch[bp][:], in_=w1d[jc * P:(jc + 1) * P, :]),
                    writes=[f"w1ch{bp}"], n=16, sem=f"WC{jc % 2}")
                p.op("vector", lambda h, bp=bp: h.tensor_scalar(
                    w1q[:], w1ch[bp][:], sw1col[:], None, op0=ALU.mult),
                    reads=[f"w1ch{bp}", "sw1"], writes=["w1q"])
                p.op("vector", lambda h: h.tensor_scalar(
                    w1q[:], w1q[:], 1.0, -1.0, op0=ALU.min, op1=ALU.max),
                    reads=["w1q"], writes=["w1q"])
                p.op("vector", lambda h, bp=bp: h.tensor_scalar(
                    w1b[bp % 2][:], w1q[:], MAGIC, MAGIC,
                    op0=ALU.add, op1=ALU.subtract),
                    reads=["w1q"], writes=[f"w1b{bp % 2}"])
                ev = None
                for dc in range(8):
                    fn = (lambda h, bp=bp, dc=dc:
                          h.transpose(ptw1[:, dc * P:(dc + 1) * P],
                                      w1b[bp % 2][:, dc * P:(dc + 1) * P],
                                      identb[:]))
                    if dc == 0:
                        ev = p.op("tensor", fn,
                                  reads=[f"w1b{bp % 2}", "identb"],
                                  writes=["ptw1"], track=False)
                    else:
                        ev = p.op("tensor", fn, track=False)
                p.note_read([f"w1b{bp % 2}", "identb"], ev)
                p.note_write(["ptw1"], ev)
                for dc in range(8):
                    p.op("vector",
                         lambda h, par=par, j=j, dc=dc: h.tensor_copy(
                             w1stage[par][:, j * 1024 + dc * P:
                                          j * 1024 + (dc + 1) * P],
                             ptw1[:, dc * P:(dc + 1) * P]),
                         reads=["ptw1"], writes=[f"w1st{par}_{j}_{dc}"])
                if j == 1:
                    p.op("scalar", lambda h, sl=sl, par=par: h.dma_start(
                        out=w1t_s[sl], in_=w1stage[par][:]),
                        reads=[f"w1st{par}_{jj}_{dc}" for jj in range(2)
                               for dc in range(8)],
                        writes=[f"w1t_s{sl}"], n=16, sem="W1S")

            def w2_quant(i):
                bp = i % 2
                p.op("gpsimd", lambda h, i=i, bp=bp: h.dma_start(
                    out=w2ch[bp][:], in_=w2d[i * P:(i + 1) * P, :]),
                    writes=[f"w2ch{bp}"], n=16, sem=f"WD{bp}")
                p.op("vector", lambda h, bp=bp: h.tensor_scalar(
                    w2q[:], w2ch[bp][:], sw2col[:], None, op0=ALU.mult),
                    reads=[f"w2ch{bp}", "sw2"], writes=["w2q"])
                p.op("vector", lambda h: h.tensor_scalar(
                    w2q[:], w2q[:], 1.0, -1.0, op0=ALU.min, op1=ALU.max),
                    reads=["w2q"], writes=["w2q"])
                p.op("vector", lambda h, i=i: h.tensor_scalar(
                    w2full[:, i * HID:(i + 1) * HID], w2q[:], MAGIC, MAGIC,
                    op0=ALU.add, op1=ALU.subtract),
                    reads=["w2q"], writes=[f"w2f{i}"])

            def w2_tr(sl):
                par = sl % 2
                for j in range(2):
                    jc = sl * 2 + j
                    ev = None
                    for ocx in range(8):
                        fn = (lambda h, ocx=ocx, jc=jc:
                              h.transpose(ptw2[:, ocx * P:(ocx + 1) * P],
                                          w2full[:, ocx * HID + jc * P:
                                                 ocx * HID + (jc + 1) * P],
                                          identb[:]))
                        if ocx == 0:
                            ev = p.op("tensor", fn,
                                      reads=[f"w2f{x}" for x in range(8)]
                                      + ["identb"],
                                      writes=["ptw2"], track=False)
                        else:
                            ev = p.op("tensor", fn, track=False)
                    p.note_read([f"w2f{x}" for x in range(8)] + ["identb"], ev)
                    p.note_write(["ptw2"], ev)
                    for ocx in range(8):
                        p.op("vector",
                             lambda h, par=par, j=j, ocx=ocx: h.tensor_copy(
                                 w2stage[par][:, j * 1024 + ocx * P:
                                              j * 1024 + (ocx + 1) * P],
                                 ptw2[:, ocx * P:(ocx + 1) * P]),
                             reads=["ptw2"], writes=[f"w2st{par}_{j}_{ocx}"])
                p.op("gpsimd", lambda h, sl=sl, par=par: h.dma_start(
                    out=w2t_s[sl], in_=w2stage[par][:]),
                    reads=[f"w2st{par}_{j}_{ocx}" for j in range(2)
                           for ocx in range(8)],
                    writes=[f"w2t_s{sl}"], n=16, sem="W2S")

            for i in range(32):
                if i < 16:
                    mean_chunk(w1d, 2 * i, D, "w1ch", w1ch, "scalar", 4)
                    mean_chunk(w1d, 2 * i + 1, D, "w1ch", w1ch, "scalar", 4)
                if i < 8:
                    mean_chunk(w2d, i, HID, "w2ch", w2ch, "gpsimd", 2,
                               acol=32)
                k_half(i, 0)
                k_half(i, 1)
                if i == 15:
                    mean_final(32, "w1ch", sw1col, c1col, "sw1", "c1")
                if i == 7:
                    mean_final(8, "w2ch", sw2col, c2col, "sw2", "c2",
                               acol=32)
                if i >= 16:
                    w1_half(i - 16, 0)
                    w1_half(i - 16, 1)
                    if i % 2 == 0:
                        w2_quant((i - 16) // 2)
            for sl in range(16):
                w2_tr(sl)

            for i in range(8):
                p.op("gpsimd", lambda h, i=i: h.dma_start(
                    out=v_s[i * 1024:(i + 1) * 1024, :],
                    in_=vv[i * 1024:(i + 1) * 1024, :]),
                    writes=[f"v_s{i}"], n=16, sem=f"VC{i // 4}")

            p.barrier()
            for sl in range(32):
                p.W[f"kt_s{sl}"] = (f"KT{sl // 16}", 16 * 16)
            for sl in range(16):
                p.W[f"w1t_s{sl}"] = ("W1S", 16 * 16)
                p.W[f"w2t_s{sl}"] = ("W2S", 16 * 16)
            for i in range(8):
                p.W[f"v_s{i}"] = (f"VC{i // 4}", 16 * 4)

        def ps(name, shape, dtype=F32):
            return S.enter_context(nc.psum_tensor(name, shape, dtype))

        pb = [ps(f"pb{i}", [P, 512]) for i in range(8)]
        PBN = [f"pb{i}" for i in range(8)]

        scT = sb("scT", [P, 64 * 512], BF16)
        y_full = sb("y_full", [P, 32 * 512], BF16)
        wsy = [sb(f"wsy{i}", [P, 2048], BF16) for i in range(6)]
        wgp = [sb(f"wgp{i}", [P, 2048], BF16) for i in range(5)]
        xnT = sb("xnT", [P, 8 * 512], BF16)
        xqT = sb("xqT", [P, 8 * 514], BF16)
        xln = [sb("xln0", [P, D]), sb("xln1", [P, D])]
        xadd = [sb("xadd0", [P, D]), sb("xadd1", [P, D])]
        qtc = [sb("qtc0", [P, D]), sb("qtc1", [P, D])]
        qtmp = sb("qtmp", [P, 512])
        sqz = [sb("sqz0", [P, 512], BF16), sb("sqz1", [P, 512], BF16),
               sb("sqz2", [P, 512], BF16)]
        y_scr = [sb("y_scr0", [P, 256]), sb("y_scr1", [P, 256])]
        mxq = sb("mxq", [P, 512])
        csbc = sb("csbc", [P, 512])
        csbc2 = sb("csbc2", [P, 512])
        invbc = sb("invbc", [P, 512])
        rows = sb("rows", [1, 1024])
        stat = sb("stat", [P, 32])
        x_halo = sb("x_halo", [2, D])
        xq_halo = sb("xq_halo", [2, D], BF16)
        stat_h = sb("stat_h", [2, 8])

        def bslot(tc, which):
            c0 = tc * 2048 + which * 1024
            names = [f"yf{tc * 4 + 2 * which + q}_{hf}"
                     for q in (0, 1) for hf in (0, 1)]
            return y_full[:, c0:c0 + 1024], names

        def mm_group(fns, bank, reads):
            ev = None
            for i, fn in enumerate(fns):
                if i == 0:
                    ev = p.op("tensor", fn, reads=reads, writes=[bank],
                              track=False)
                else:
                    ev = p.op("tensor", fn, track=False)
            p.note_read(reads, ev)
            p.note_write([bank], ev)
            return ev

        def sy_slot(t, i):
            return (t * 64 + i) % 6

        def gp_slot(t, i):
            return (t * 32 + i) % 5

        def emit_B_ln(t, xq="gpsimd"):
            g0 = t * T
            p.op(xq, lambda h, g0=g0: h.dma_start(
                out=x_halo[:],
                in_=bass.AP(xs.tensor, g0 * D, [[513 * D, 2], [1, D]])),
                writes=["x_halo"], n=16, sem="XH")
            for tc in range(4):
                sl = tc % 2
                xnb, xnn = bslot(tc, 0)
                xqb, xqn = bslot(tc, 1)
                p.op(xq, lambda h, tc=tc, sl=sl, g0=g0: h.dma_start(
                    out=xln[sl][:],
                    in_=xs[1 + g0 + tc * P: 1 + g0 + (tc + 1) * P, :]),
                    writes=[f"xln{sl}"], n=16, sem=f"XL{sl}")
                st = stat[:, tc * 8:tc * 8 + 8]
                sn = [f"st{tc}_{i}" for i in range(8)]
                p.op("vector", lambda h, sl=sl, st=st: h.reduce_sum(
                    st[:, 0:1], xln[sl][:], AX),
                    reads=[f"xln{sl}"], writes=[sn[0]])
                p.op("vector", lambda h, st=st: h.tensor_scalar(
                    st[:, 0:1], st[:, 0:1], 1.0 / D, None, op0=ALU.mult),
                    reads=[sn[0]], writes=[sn[0]])
                p.op("vector", lambda h, sl=sl, st=st: h.tensor_scalar(
                    xln[sl][:], xln[sl][:], st[:, 0:1], None,
                    op0=ALU.subtract),
                    reads=[sn[0]], writes=[f"xln{sl}"])
                p.op("scalar", lambda h, sl=sl, xqb=xqb, st=st: h.activation(
                    xqb[:], xln[sl][:], AF.Square, accum_out=st[:, 1:2]),
                    reads=[f"xln{sl}"], writes=xqn + [sn[1]])
                p.op("vector", lambda h, st=st: h.tensor_scalar(
                    st[:, 1:2], st[:, 1:2], 1.0 / D, EPS,
                    op0=ALU.mult, op1=ALU.add),
                    reads=[sn[1]], writes=[sn[1]])
                p.op("scalar", lambda h, st=st: h.activation(
                    st[:, 1:2], st[:, 1:2], AF.Sqrt),
                    reads=[sn[1]], writes=[sn[1]])
                p.op("vector", lambda h, st=st: h.reciprocal(
                    st[:, 2:3], st[:, 1:2]), reads=[sn[1]], writes=[sn[2]])
                p.op("vector", lambda h, sl=sl, st=st: h.reduce_max(
                    st[:, 3:4], xln[sl][:], AX, apply_absolute_value=True),
                    reads=[f"xln{sl}"], writes=[sn[3]])
                p.op("vector", lambda h, st=st: h.tensor_tensor(
                    st[:, 3:4], st[:, 3:4], st[:, 2:3], op=ALU.mult),
                    reads=[sn[3], sn[2]], writes=[sn[3]])
                p.op("vector", lambda h, st=st: h.tensor_scalar(
                    st[:, 3:4], st[:, 3:4], 1e-5, None, op0=ALU.max),
                    reads=[sn[3]], writes=[sn[3]])
                p.op("vector", lambda h, st=st: h.reciprocal(
                    st[:, 4:5], st[:, 3:4]), reads=[sn[3]], writes=[sn[4]])
                p.op("vector", lambda h, st=st: h.tensor_scalar(
                    st[:, 4:5], st[:, 4:5], 127.0, None, op0=ALU.mult),
                    reads=[sn[4]], writes=[sn[4]])
                p.op("vector", lambda h, st=st: h.tensor_tensor(
                    st[:, 5:6], st[:, 2:3], st[:, 4:5], op=ALU.mult),
                    reads=[sn[2], sn[4]], writes=[sn[5]])
                p.op("vector", lambda h, st=st: h.tensor_tensor(
                    st[:, 6:7], st[:, 3:4], c1col[:], op=ALU.mult),
                    reads=[sn[3], "c1"], writes=[sn[6]])
                p.op("vector", lambda h, sl=sl, xnb=xnb, st=st: h.tensor_scalar(
                    xnb[:], xln[sl][:], st[:, 2:3], None, op0=ALU.mult),
                    reads=[f"xln{sl}", sn[2]], writes=xnn)
                for hh in (0, 1):
                    p.op("vector", lambda h, sl=sl, st=st, hh=hh: h.tensor_scalar(
                        qtmp[:], xln[sl][:, hh * 512:(hh + 1) * 512],
                        st[:, 5:6], MAGIC, op0=ALU.mult, op1=ALU.add),
                        reads=[f"xln{sl}", sn[5]], writes=["qtmp"])
                    p.op("vector", lambda h: h.tensor_scalar(
                        qtmp[:], qtmp[:], MAGIC, 127.0,
                        op0=ALU.subtract, op1=ALU.min),
                        reads=["qtmp"], writes=["qtmp"])
                    p.op("vector", lambda h, xqb=xqb, st=st, hh=hh: h.tensor_scalar(
                        xqb[:, hh * 512:(hh + 1) * 512], qtmp[:], -128.0,
                        st[:, 6:7], op0=ALU.max, op1=ALU.mult),
                        reads=["qtmp", sn[6]], writes=xqn)

            p.op("vector", lambda h: h.reduce_sum(
                stat_h[:, 0:1], x_halo[:], AX), reads=["x_halo"], writes=["sh0"])
            p.op("vector", lambda h: h.tensor_scalar(
                stat_h[:, 0:1], stat_h[:, 0:1], 1.0 / D, None, op0=ALU.mult),
                reads=["sh0"], writes=["sh0"])
            p.op("vector", lambda h: h.tensor_scalar(
                x_halo[:], x_halo[:], stat_h[:, 0:1], None, op0=ALU.subtract),
                reads=["sh0"], writes=["x_halo"])
            p.op("scalar", lambda h: h.activation(
                xq_halo[:], x_halo[:], AF.Square, accum_out=stat_h[:, 1:2]),
                reads=["x_halo"], writes=["xq_halo", "sh1"])
            p.op("vector", lambda h: h.tensor_scalar(
                stat_h[:, 1:2], stat_h[:, 1:2], 1.0 / D, EPS,
                op0=ALU.mult, op1=ALU.add), reads=["sh1"], writes=["sh1"])
            p.op("scalar", lambda h: h.activation(
                stat_h[:, 1:2], stat_h[:, 1:2], AF.Sqrt),
                reads=["sh1"], writes=["sh1"])
            p.op("vector", lambda h: h.reciprocal(
                stat_h[:, 2:3], stat_h[:, 1:2]), reads=["sh1"], writes=["sh2"])
            p.op("vector", lambda h: h.reduce_max(
                stat_h[:, 3:4], x_halo[:], AX, apply_absolute_value=True),
                reads=["x_halo"], writes=["sh3"])
            p.op("vector", lambda h: h.tensor_tensor(
                stat_h[:, 3:4], stat_h[:, 3:4], stat_h[:, 2:3], op=ALU.mult),
                reads=["sh3", "sh2"], writes=["sh3"])
            p.op("vector", lambda h: h.tensor_scalar(
                stat_h[:, 3:4], stat_h[:, 3:4], 1e-5, None, op0=ALU.max),
                reads=["sh3"], writes=["sh3"])
            p.op("vector", lambda h: h.reciprocal(
                stat_h[:, 4:5], stat_h[:, 3:4]), reads=["sh3"], writes=["sh4"])
            p.op("vector", lambda h: h.tensor_scalar(
                stat_h[:, 4:5], stat_h[:, 4:5], 127.0, None, op0=ALU.mult),
                reads=["sh4"], writes=["sh4"])
            p.op("vector", lambda h: h.tensor_tensor(
                stat_h[:, 5:6], stat_h[:, 2:3], stat_h[:, 4:5], op=ALU.mult),
                reads=["sh2", "sh4"], writes=["sh5"])
            p.op("vector", lambda h: h.tensor_tensor(
                stat_h[:, 6:7], stat_h[:, 3:4], c1col[0:2, :], op=ALU.mult),
                reads=["sh3", "c1"], writes=["sh6"])
            p.op("vector", lambda h: h.tensor_scalar(
                x_halo[:], x_halo[:], stat_h[:, 5:6], MAGIC,
                op0=ALU.mult, op1=ALU.add),
                reads=["sh5"], writes=["x_halo"])
            p.op("vector", lambda h: h.tensor_scalar(
                x_halo[:], x_halo[:], MAGIC, 127.0,
                op0=ALU.subtract, op1=ALU.min),
                reads=["x_halo"], writes=["x_halo"])
            p.op("vector", lambda h: h.tensor_scalar(
                xq_halo[:], x_halo[:], -128.0, stat_h[:, 6:7],
                op0=ALU.max, op1=ALU.mult),
                reads=["x_halo", "sh6"], writes=["xq_halo"])

        def emit_B_tr(t, bc0):
            bcnt = bc0
            for tc in range(4):
                for which in (0, 1):
                    src, srcn = bslot(tc, which)
                    dst = xnT if which == 0 else xqT
                    dname = "xnT" if which == 0 else "xqT"
                    off = 0 if which == 0 else 1
                    w = 512 if which == 0 else 514
                    for hg in (0, 1):
                        bk = pb[bcnt % 4]
                        bkn = PBN[bcnt % 4]
                        bcnt += 1
                        ev = None
                        for dq in range(4):
                            dc = hg * 4 + dq
                            fn = (lambda h, src=src, dc=dc, dq=dq, bk=bk:
                                  h.matmul(bk[:, dq * P:(dq + 1) * P],
                                           src[:, dc * P:(dc + 1) * P],
                                           identb[:], start=True, stop=True))
                            if dq == 0:
                                ev = p.op("tensor", fn,
                                          reads=srcn + ["identb"],
                                          writes=[bkn], track=False)
                            else:
                                ev = p.op("tensor", fn, track=False)
                        p.note_read(srcn + ["identb"], ev)
                        p.note_write([bkn], ev)
                        c0 = hg * 4 * w + off + tc * P
                        rl = 8 * w
                        p.op("vector",
                             lambda h, dst=dst, c0=c0, w=w, bk=bk, rl=rl:
                             h.tensor_copy(
                                 bass.AP(dst, c0, [[rl, P], [w, 4], [1, P]]),
                                 bass.AP(bk, 0, [[512, P], [P, 4], [1, P]])),
                             reads=[bkn], writes=[dname])
            ev = None
            for dc in range(8):
                fn = (lambda h, dc=dc: h.matmul(
                    pb[2][:, dc * 2:dc * 2 + 2],
                    xq_halo[:, dc * P:(dc + 1) * P], identb[0:2, 0:2],
                    start=True, stop=True))
                if dc == 0:
                    ev = p.op("tensor", fn, reads=["xq_halo", "identb"],
                              writes=[PBN[2]], track=False)
                else:
                    ev = p.op("tensor", fn, track=False)
            p.note_read(["xq_halo", "identb"], ev)
            p.note_write([PBN[2]], ev)
            for dc in range(8):
                p.op("vector", lambda h, dc=dc: h.tensor_copy(
                    xqT[:, dc * 514:dc * 514 + 1], pb[2][:, dc * 2:dc * 2 + 1]),
                    reads=[PBN[2]], writes=["xqT"])
                p.op("vector", lambda h, dc=dc: h.tensor_copy(
                    xqT[:, dc * 514 + 513:dc * 514 + 514],
                    pb[2][:, dc * 2 + 1:dc * 2 + 2]),
                    reads=[PBN[2]], writes=["xqT"])

        def emit_main(t):
            g0 = t * T

            for ncx in range(64):
                sl = ncx // 2
                if ncx % 2 == 0:
                    ss = sy_slot(t, sl)
                    p.op("sync", lambda h, sl=sl, ss=ss: h.dma_start(
                        out=wsy[ss][:], in_=kt_s[sl]),
                        reads=[f"kt_s{sl}"], writes=[f"wsy{ss}"], n=16,
                        sem=f"SY{ss}")
                ss = sy_slot(t, ncx // 2)
                bank = PBN[ncx % 4]
                fns = []
                for dc in range(8):
                    fns.append(lambda h, ss=ss, ncx=ncx, dc=dc:
                               h.matmul(pb[ncx % 4][:],
                                        wsy[ss][
                                            :, dc * 256 + (ncx % 2) * P:
                                            dc * 256 + (ncx % 2 + 1) * P],
                                        xnT[:, dc * 512:(dc + 1) * 512],
                                        start=(dc == 0), stop=(dc == 7)))
                mm_group(fns, bank, [f"wsy{ss}", "xnT"])
                p.op("vector", lambda h, ncx=ncx: h.tensor_copy(
                    scT[:, ncx * 512:(ncx + 1) * 512], pb[ncx % 4][:]),
                    reads=[bank], writes=[f"sc{ncx}"])
                p.op("scalar", lambda h, ncx=ncx: h.activation(
                    sqz[ncx % 3][:], scT[:, ncx * 512:(ncx + 1) * 512],
                    AF.Square),
                    reads=[f"sc{ncx}"], writes=[f"sqz{ncx % 3}"])
                if ncx >= 2:
                    nl = ncx - 2
                    p.op("tensor", lambda h, nl=nl: h.matmul(
                        pb[4][:], ones_sq[:], sqz[nl % 3][:],
                        start=(nl == 0), stop=False),
                        reads=[f"sqz{nl % 3}", "onessq"], writes=[PBN[4]])
            for nl in (62, 63):
                p.op("tensor", lambda h, nl=nl: h.matmul(
                    pb[4][:], ones_sq[:], sqz[nl % 3][:],
                    start=False, stop=(nl == 63)),
                    reads=[f"sqz{nl % 3}", "onessq"], writes=[PBN[4]])

            p.op("vector", lambda h: h.tensor_copy(
                rows[0:1, 0:512], pb[4][0:1, :]),
                reads=[PBN[4]], writes=["rows"])
            p.op("scalar", lambda h: h.activation(
                rows[0:1, 0:512], rows[0:1, 0:512], AF.Sqrt),
                reads=["rows"], writes=["rows"])
            p.op("vector", lambda h: h.reciprocal(
                rows[0:1, 0:512], rows[0:1, 0:512]),
                reads=["rows"], writes=["rows"])
            p.op("vector", lambda h: h.tensor_scalar(
                rows[0:1, 0:512], rows[0:1, 0:512], SQRT_N, None,
                op0=ALU.mult), reads=["rows"], writes=["rows"])

            def minmax(jq):
                if jq == 0:
                    p.op("vector", lambda h: h.tensor_copy(
                        mxq[:], y_full[:, 0:512]),
                        reads=["yf0_0", "yf0_1"], writes=["mxq0", "mxq1"])
                    p.op("vector", lambda h: h.tensor_copy(
                        pb[7][:], y_full[:, 0:512]),
                        reads=["yf0_0", "yf0_1"], writes=[PBN[7]])
                else:
                    p.op("vector", lambda h, jq=jq: h.tensor_tensor(
                        mxq[:], mxq[:],
                        y_full[:, jq * 512:(jq + 1) * 512], op=ALU.max),
                        reads=[f"yf{jq}_0", f"yf{jq}_1", "mxq0", "mxq1"],
                        writes=["mxq0", "mxq1"])
                    p.op("vector", lambda h, jq=jq: h.tensor_tensor(
                        pb[7][:], pb[7][:],
                        y_full[:, jq * 512:(jq + 1) * 512], op=ALU.min),
                        reads=[f"yf{jq}_0", f"yf{jq}_1", PBN[7]],
                        writes=[PBN[7]])

            for jc in range(32):
                sl = jc // 2
                if jc == 1:
                    p.op("tensor", lambda h: h.matmul(
                        pb[5][:], ones_row[:], rows[0:1, 0:512],
                        start=True, stop=True),
                        reads=["rows", "onesr"], writes=[PBN[5]])
                    p.op("vector", lambda h: h.tensor_copy(csbc[:], pb[5][:]),
                         reads=[PBN[5]], writes=["csbc"])
                if jc % 2 == 0:
                    ws = sy_slot(t, 32 + sl)
                    p.op("sync", lambda h, sl=sl, ws=ws: h.dma_start(
                        out=wsy[ws][:], in_=w1t_s[sl]),
                        reads=[f"w1t_s{sl}"], writes=[f"wsy{ws}"], n=16,
                        sem=f"SY{ws}")
                ws = sy_slot(t, 32 + jc // 2)
                for hf in range(2):
                    ph = (2 * jc + hf) % 6
                    bank = PBN[ph]
                    fns = []
                    for dc in range(8):
                        fns.append(lambda h, ws=ws, jc=jc, dc=dc, hf=hf, ph=ph:
                                   h.matmul(pb[ph][:, 0:258],
                                            wsy[ws][
                                                :, (jc % 2) * 1024 + dc * P:
                                                (jc % 2) * 1024 + (dc + 1) * P],
                                            xqT[:, dc * 514 + hf * 256:
                                                dc * 514 + hf * 256 + 258],
                                            start=(dc == 0), stop=(dc == 7)))
                    mm_group(fns, bank, [f"wsy{ws}", "xqT"])
                    p.op("vector", lambda h, ph=ph, jc=jc: h.tensor_scalar(
                        y_scr[ph % 2][:], pb[ph][:, 1:257],
                        taps[:, jc * 3 + 1:jc * 3 + 2], None, op0=ALU.mult),
                        reads=[bank, "taps"], writes=[f"ysc{ph % 2}"])
                    p.op("vector", lambda h, ph=ph, jc=jc: h.scalar_tensor_tensor(
                        y_scr[ph % 2][:], pb[ph][:, 0:256],
                        taps[:, jc * 3:jc * 3 + 1], y_scr[ph % 2][:],
                        op0=ALU.mult, op1=ALU.add),
                        reads=[bank, "taps"], writes=[f"ysc{ph % 2}"])
                    p.op("vector", lambda h, ph=ph, jc=jc: h.scalar_tensor_tensor(
                        y_scr[ph % 2][:], pb[ph][:, 2:258],
                        taps[:, jc * 3 + 2:jc * 3 + 3], y_scr[ph % 2][:],
                        op0=ALU.mult, op1=ALU.add),
                        reads=[bank, "taps"], writes=[f"ysc{ph % 2}"])
                    p.op("scalar", lambda h, ph=ph, jc=jc, hf=hf: h.activation(
                        y_full[:, jc * 512 + hf * 256:jc * 512 + hf * 256 + 256],
                        y_scr[ph % 2][:], AF.Silu, bias=bias_c[:, jc:jc + 1]),
                        reads=[f"ysc{ph % 2}", "bias"], writes=[f"yf{jc}_{hf}"])
                if jc >= 1:
                    minmax(jc - 1)
                if 27 <= jc <= 31:
                    sv = jc - 27
                    gs = gp_slot(t, sv)
                    p.op("gpsimd", lambda h, sv=sv, gs=gs: h.dma_start(
                        out=bass.AP(wgp[gs], 0,
                                    [[2048, P], [1024, 2], [1, 1024]]),
                        in_=bass.AP(v_s.tensor, sv * 256 * D,
                                    [[D, P], [P * D, 2], [1, D]])),
                        reads=[f"v_s{sv // 4}"], writes=[f"wgp{gs}"], n=16,
                        sem=f"GP{gs}")
            minmax(31)
            p.op("vector", lambda h: h.tensor_scalar(
                pb[7][:], pb[7][:], -1.0, None, op0=ALU.mult),
                reads=[PBN[7]], writes=[PBN[7]])
            p.op("vector", lambda h: h.tensor_tensor(
                mxq[:], mxq[:], pb[7][:], op=ALU.max),
                reads=[PBN[7], "mxq0", "mxq1"],
                writes=["mxq0", "mxq1"])

            def hq_mxqT():
                ev = None
                for tc in range(4):
                    fn = (lambda h, tc=tc: h.transpose(
                        pb[6][:, tc * P:(tc + 1) * P],
                        mxq[:, tc * P:(tc + 1) * P], ident[:]))
                    if tc == 0:
                        ev = p.op("tensor", fn,
                                  reads=["mxq0", "mxq1", "ident"],
                                  writes=[PBN[6]], track=False)
                    else:
                        ev = p.op("tensor", fn, track=False)
                p.note_read(["mxq0", "mxq1", "ident"], ev)
                p.note_write([PBN[6]], ev)
                for tc in range(4):
                    st = stat[:, tc * 8:tc * 8 + 8]
                    p.op("vector", lambda h, tc=tc, st=st: h.reduce_max(
                        st[:, 7:8], pb[6][:, tc * P:(tc + 1) * P], AX),
                        reads=[PBN[6]], writes=[f"st{tc}_7"])

            def hq_statT():
                ev = None
                for tc in range(4):
                    ev = p.op("tensor", lambda h, tc=tc: h.transpose(
                        pb[7][0:1, tc * P:(tc + 1) * P],
                        stat[:, tc * 8 + 7:tc * 8 + 8], ident[:]),
                        reads=[f"st{tc}_7", "ident"],
                        writes=[PBN[7]] if tc == 0 else (), track=(tc == 0))
                p.note_write([PBN[7]], ev)
                p.note_read([f"st{tc}_7" for tc in range(4)], ev)
                p.op("vector", lambda h: h.tensor_scalar(
                    rows[0:1, 512:1024], pb[7][0:1, 0:512], 1e-5, None,
                    op0=ALU.max),
                    reads=[PBN[7]], writes=["rowsi"])
                p.op("vector", lambda h: h.reciprocal(
                    rows[0:1, 0:512], rows[0:1, 512:1024]),
                    reads=["rowsi"], writes=["rows"])
                p.op("vector", lambda h: h.tensor_scalar(
                    rows[0:1, 0:512], rows[0:1, 0:512], 127.0, None,
                    op0=ALU.mult),
                    reads=["rows"], writes=["rows"])
                p.op("vector", lambda h: h.tensor_scalar(
                    rows[0:1, 512:1024], rows[0:1, 512:1024], c2col[0:1, 0:1],
                    None, op0=ALU.mult),
                    reads=["rowsi", "c2"], writes=["rowsi"])

            def hq_bc1():
                p.op("tensor", lambda h: h.matmul(
                    pb[6][:], ones_row[:], rows[0:1, 0:512],
                    start=True, stop=True), reads=["rows", "onesr"],
                    writes=[PBN[6]])
                p.op("vector", lambda h: h.tensor_copy(csbc2[:], pb[6][:]),
                     reads=[PBN[6]], writes=["csbc2"])

            def hq_bc2():
                p.op("tensor", lambda h: h.matmul(
                    pb[7][:], ones_row[:], rows[0:1, 512:1024],
                    start=True, stop=True), reads=["rowsi", "onesr"],
                    writes=[PBN[7]])
                p.op("vector", lambda h: h.tensor_copy(
                    invbc[:, 0:512], pb[7][:]), reads=[PBN[7]],
                    writes=["invbc"])

            def emit_F(f):
                p.op("vector", lambda h, f=f: h.tensor_tensor(
                    sqz[f % 3][:], scT[:, f * 512:(f + 1) * 512],
                    csbc[:], op=ALU.mult),
                    reads=[f"sc{f}", "csbc"], writes=[f"sqz{f % 3}"])
                p.op("scalar", lambda h, f=f: h.activation(
                    scT[:, f * 512:(f + 1) * 512], sqz[f % 3][:],
                    AF.Gelu),
                    reads=[f"sqz{f % 3}"], writes=[f"sc{f}"])

            emit_F(0)
            emit_F(1)
            hq_steps = {1: hq_mxqT, 3: hq_statT, 5: hq_bc1, 7: hq_bc2}
            for ncx in range(8):
                for f in (2 * ncx + 2, 2 * ncx + 3):
                    if f < 64:
                        emit_F(f)
                sl = ncx // 2
                if ncx % 2 == 0 and sl >= 5:
                    gs = gp_slot(t, sl)
                    p.op("gpsimd", lambda h, sl=sl, gs=gs: h.dma_start(
                        out=bass.AP(wgp[gs], 0,
                                    [[2048, P], [1024, 2], [1, 1024]]),
                        in_=bass.AP(v_s.tensor, sl * 256 * D,
                                    [[D, P], [P * D, 2], [1, D]])),
                        reads=[f"v_s{sl // 4}"], writes=[f"wgp{gs}"], n=16,
                        sem=f"GP{gs}")
                gs = gp_slot(t, ncx // 2)
                ev = None
                for ol in range(6):
                    fn = (lambda h, gs=gs, ncx=ncx, ol=ol:
                          h.matmul(pb[ol][:],
                                   wgp[gs][
                                       :, (ncx % 2) * 1024 + ol * P:
                                       (ncx % 2) * 1024 + (ol + 1) * P],
                                   scT[:, ncx * 512:(ncx + 1) * 512],
                                   start=(ncx == 0), stop=False))
                    rd = [f"wgp{gs}", f"sc{ncx}"] if ol == 0 else ()
                    wr = [PBN[ol]] if ncx == 0 else ()
                    ev = p.op("tensor", fn, reads=rd, writes=wr,
                              track=False)
                p.note_read([f"wgp{gs}", f"sc{ncx}"], ev)
                if ncx in hq_steps:
                    hq_steps[ncx]()

            for ncx in range(8):
                gs = gp_slot(t, ncx // 2)
                ev = None
                for ol in (6, 7):
                    fn = (lambda h, gs=gs, ncx=ncx, ol=ol:
                          h.matmul(pb[ol][:],
                                   wgp[gs][
                                       :, (ncx % 2) * 1024 + ol * P:
                                       (ncx % 2) * 1024 + (ol + 1) * P],
                                   scT[:, ncx * 512:(ncx + 1) * 512],
                                   start=(ncx == 0), stop=False))
                    rd = [f"wgp{gs}", f"sc{ncx}"] if ol == 6 else ()
                    wr = [PBN[ol]] if ncx == 0 else ()
                    ev = p.op("tensor", fn, reads=rd, writes=wr,
                              track=False)
                p.note_read([f"wgp{gs}", f"sc{ncx}"], ev)

            last_ev = [None] * 8
            def emit_quant(jq):
                p.op("vector", lambda h, jq=jq: h.tensor_tensor(
                    mxq[:], y_full[:, jq * 512:(jq + 1) * 512], csbc2[:],
                    op=ALU.mult),
                    reads=[f"yf{jq}_0", f"yf{jq}_1", "csbc2"],
                    writes=["mxq0", "mxq1"])
                p.op("vector", lambda h: h.tensor_scalar(
                    mxq[:], mxq[:], 127.0, -128.0, op0=ALU.min,
                    op1=ALU.max),
                    reads=["mxq0", "mxq1"], writes=["mxq0", "mxq1"])
                p.op("vector", lambda h: h.tensor_scalar(
                    mxq[:], mxq[:], MAGIC, MAGIC, op0=ALU.add,
                    op1=ALU.subtract),
                    reads=["mxq0", "mxq1"], writes=["mxq0", "mxq1"])
                p.op("vector", lambda h, jq=jq: h.tensor_tensor(
                    y_full[:, jq * 512:(jq + 1) * 512], mxq[:],
                    invbc[:, 0:512], op=ALU.mult),
                    reads=["mxq0", "mxq1", "invbc"],
                    writes=[f"yf{jq}_0", f"yf{jq}_1"])

            for ncx in range(8, 64):
                for f in (2 * ncx + 2, 2 * ncx + 3):
                    if f < 64:
                        emit_F(f)
                if 8 <= ncx < 40:
                    emit_quant(ncx - 8)
                sl = ncx // 2
                if ncx % 2 == 0 and sl >= 5:
                    gs = gp_slot(t, sl)
                    p.op("gpsimd", lambda h, sl=sl, gs=gs: h.dma_start(
                        out=bass.AP(wgp[gs], 0,
                                    [[2048, P], [1024, 2], [1, 1024]]),
                        in_=bass.AP(v_s.tensor, sl * 256 * D,
                                    [[D, P], [P * D, 2], [1, D]])),
                        reads=[f"v_s{sl // 4}"], writes=[f"wgp{gs}"], n=16,
                        sem=f"GP{gs}")
                gs = gp_slot(t, ncx // 2)
                ev = None
                for ol in range(8):
                    fn = (lambda h, gs=gs, ncx=ncx, ol=ol:
                          h.matmul(pb[ol][:],
                                   wgp[gs][
                                       :, (ncx % 2) * 1024 + ol * P:
                                       (ncx % 2) * 1024 + (ol + 1) * P],
                                   scT[:, ncx * 512:(ncx + 1) * 512],
                                   start=False, stop=False))
                    rd = [f"wgp{gs}", f"sc{ncx}"] if ol == 0 else ()
                    ev = p.op("tensor", fn, reads=rd, track=False)
                p.note_read([f"wgp{gs}", f"sc{ncx}"], ev)
            for jc in range(32):
                sl2 = jc // 2
                if jc % 2 == 0:
                    ss = sy_slot(t, 48 + sl2)
                    p.op("sync", lambda h, sl2=sl2, ss=ss: h.dma_start(
                        out=wsy[ss][:], in_=w2t_s[sl2]),
                        reads=[f"w2t_s{sl2}"], writes=[f"wsy{ss}"], n=16,
                        sem=f"SY{ss}")
                ss = sy_slot(t, 48 + jc // 2)
                ev = None
                for ol in range(8):
                    fn = (lambda h, ss=ss, jc=jc, ol=ol:
                          h.matmul(pb[ol][:],
                                   wsy[ss][
                                       :, (jc % 2) * 1024 + ol * P:
                                       (jc % 2) * 1024 + (ol + 1) * P],
                                   y_full[:, jc * 512:(jc + 1) * 512],
                                   start=False, stop=(jc == 31)))
                    ev = p.op("tensor", fn,
                              reads=[f"wsy{ss}", f"yf{jc}_0", f"yf{jc}_1"]
                              if ol == 0 else (), track=False)
                    if jc == 31:
                        last_ev[ol] = ev
                p.note_read([f"wsy{ss}", f"yf{jc}_0", f"yf{jc}_1"], ev)
            for ol in range(8):
                p.note_write([PBN[ol]], last_ev[ol])

        def emit_evac(t):
            g0 = t * T
            xa = [xadd[0], xadd[1], xadd[0], xadd[1]]
            xan = ["xadd0", "xadd1", "xadd0", "xadd1"]
            for tc in range(2):
                p.op("gpsimd", lambda h, tc=tc, g0=g0: h.dma_start(
                    out=xa[tc][:],
                    in_=xs[1 + g0 + tc * P: 1 + g0 + (tc + 1) * P, :]),
                    writes=[xan[tc]], n=16, sem=f"XA{tc % 2}")
            for ol in range(8):
                stg = [mxq, csbc, csbc2, invbc][ol % 4]
                stgn = [["mxq0", "mxq1"], ["csbc"], ["csbc2"],
                        ["invbc"]][ol % 4]
                p.op("vector", lambda h, ol=ol, stg=stg: h.tensor_copy(
                    stg[:], pb[ol][:]),
                    reads=[PBN[ol]], writes=stgn)
                ev = None
                for tc in range(4):
                    fn = (lambda h, ol=ol, tc=tc, stg=stg: h.transpose(
                        pb[ol][:, tc * P:(tc + 1) * P],
                        stg[:, tc * P:(tc + 1) * P], ident[:]))
                    if tc == 0:
                        ev = p.op("tensor", fn, reads=stgn + ["ident"],
                                  writes=[PBN[ol]], track=False)
                    else:
                        ev = p.op("tensor", fn, track=False)
                p.note_read(stgn + ["ident"], ev)
                p.note_write([PBN[ol]], ev)
            for tc in range(4):
                for ol in range(8):
                    p.op("vector",
                         lambda h, tc=tc, ol=ol: h.scalar_tensor_tensor(
                             qtc[tc % 2][:, ol * P:(ol + 1) * P],
                             pb[ol][:, tc * P:(tc + 1) * P], 1.0,
                             xa[tc][:, ol * P:(ol + 1) * P],
                             op0=ALU.mult, op1=ALU.add),
                         reads=[PBN[ol], xan[tc]],
                         writes=[f"qtc{tc % 2}"])
                p.op("scalar", lambda h, tc=tc, g0=g0: h.dma_start(
                    out=outd[g0 + tc * P:g0 + (tc + 1) * P, :],
                    in_=qtc[tc % 2][:]),
                    reads=[f"qtc{tc % 2}"], writes=[f"o{t}_{tc}"],
                    n=16, sem=f"VO{tc % 2}")
                if tc < 2:
                    p.op("gpsimd", lambda h, tc=tc, g0=g0: h.dma_start(
                        out=xa[tc + 2][:],
                        in_=xs[1 + g0 + (tc + 2) * P: 1 + g0 + (tc + 3) * P, :]),
                        writes=[xan[tc + 2]], n=16, sem=f"XA{tc % 2}")

        emit_B_ln(0, xq="sync")
        emit_B_tr(0, 0)
        for t in range(n_tiles):
            emit_main(t)
            if t + 1 < n_tiles:
                emit_B_ln(t + 1)
            emit_evac(t)
            if t + 1 < n_tiles:
                emit_B_tr(t + 1, (t + 1) % 2)

        sync_sems = [f"SY{i}" for i in range(6)] + ["KC0", "KC1",
                     "CS0", "CS1", "CS2", "KT0", "KT1"]
        gp_sems = ([f"GP{i}" for i in range(5)]
                   + ["XL0", "XL1", "XH", "XA0", "XA1",
                      "WD0", "WD1", "W2S", "VC0", "VC1"])
        sc_sems = ["WC0", "WC1", "W1S", "VO0", "VO1"]
        p.wait("sync", [(s, p.cnt.get(s, 0)) for s in sync_sems])
        p.wait("gpsimd", [(s, p.cnt.get(s, 0)) for s in gp_sems])
        p.wait("scalar", [(s, p.cnt.get(s, 0)) for s in sc_sems])

        block = S.enter_context(nc.Block())
        p.emit(block, semh)

    return nc


def kernel(L0_attn_q, L0_attn_k, L0_attn_v, L0_ffn_w1, L0_ffn_conv_w,
           L0_ffn_conv_b, L0_ffn_w2):
    x = np.ascontiguousarray(np.asarray(L0_attn_q, dtype=np.float32))
    k = np.ascontiguousarray(np.asarray(L0_attn_k, dtype=np.float32))
    v = np.ascontiguousarray(np.asarray(L0_attn_v, dtype=np.float32))
    w1 = np.ascontiguousarray(np.asarray(L0_ffn_w1, dtype=np.float32))
    cw = np.ascontiguousarray(
        np.asarray(L0_ffn_conv_w, dtype=np.float32).reshape(HID, 3))
    cb = np.ascontiguousarray(np.asarray(L0_ffn_conv_b, dtype=np.float32))
    w2 = np.ascontiguousarray(np.asarray(L0_ffn_w2, dtype=np.float32))
    ident = np.eye(P, dtype=np.float32)

    zrow = np.zeros((1, D), np.float32)
    in_maps = []
    for c in range(8):
        b, hh = c // 2, c % 2
        tok0 = hh * TOK
        left = x[b, tok0 - 1:tok0] if hh == 1 else zrow
        right = x[b, tok0 + TOK:tok0 + TOK + 1] if hh == 0 else zrow
        xsl = np.concatenate([left, x[b, tok0:tok0 + TOK], right], axis=0)
        in_maps.append(dict(xs=np.ascontiguousarray(xsl), kk=k, vv=v, w1=w1,
                            w2=w2, cw=cw, cb=cb, ident=ident))

    nc = build_nc()
    res = run_bass_kernel_spmd(nc, in_maps, core_ids=list(range(8)))
    outp = np.empty((4, 8192, D), np.float32)
    for c in range(8):
        b, hh = c // 2, c % 2
        outp[b, hh * TOK:(hh + 1) * TOK] = np.asarray(res.results[c]["out"])
    return outp
